# version 22
# baseline (speedup 1.0000x reference)
import numpy as np
import ml_dtypes
from contextlib import ExitStack

import concourse.bass as bass
import concourse.mybir as mybir
from concourse.bass_utils import run_bass_kernel_spmd

F32 = mybir.dt.float32
BF16 = mybir.dt.bfloat16
AF = mybir.ActivationFunctionType
ALU = mybir.AluOpType
BF = ml_dtypes.bfloat16

MASK = -10000.0
EPS_LN = 1e-5
EPS_RMS = 1e-6
ROPE_BASE = 100000.0


def make_cfg(B=4, NSLOT=4, D=2048, NH=16, NKV=4, DH=8192):
    cfg = dict(B=B, NSLOT=NSLOT, D=D, NH=NH, NKV=NKV, DH=DH)
    cfg["T"] = 1024 * NSLOT
    cfg["OWN"] = 512 * NSLOT
    cfg["KEXT"] = cfg["T"] + 512 * NSLOT
    cfg["DK"] = D // 128
    cfg["HC"] = DH // 128
    cfg["GRP"] = NH // NKV
    cfg["ITERS"] = [8 * j + 8 for j in range(NSLOT)]
    cfg["NIT"] = sum(cfg["ITERS"])
    cfg["NGK"] = cfg["KEXT"] // 512
    cfg["NCORE"] = 2 * B
    return cfg


CFG = make_cfg()
SCALE = float(np.float32(1.0) / np.sqrt(np.float32(128.0)))



def core_perms(cfg, ct):
    T, NS = cfg["T"], cfg["NSLOT"]
    groups = [2 * j + ct for j in range(NS)]
    perm_q = np.concatenate([np.arange(512 * g, 512 * g + 512) for g in groups])
    dup = np.concatenate([np.arange(1024 * j + 512 * ct, 1024 * j + 512 * ct + 512)
                          for j in range(NS)])
    perm_kv = np.concatenate([np.arange(T), dup])
    return groups, perm_q, perm_kv


def host_inputs(cfg, x, wq, wk, wv, wo, wg, wu, wd):
    B, T, NS, NIT = cfg["B"], cfg["T"], cfg["NSLOT"], cfg["NIT"]

    shared = {
        "wq": np.ascontiguousarray(wq, np.float32),
        "wk": np.ascontiguousarray(wk, np.float32),
        "wv": np.ascontiguousarray(wv, np.float32),
        "wo": np.ascontiguousarray(wo, np.float32),
        "wgT": np.ascontiguousarray(np.asarray(wg, np.float32).T),
        "wuT": np.ascontiguousarray(np.asarray(wu, np.float32).T),
        "wdT": np.ascontiguousarray(np.asarray(wd, np.float32).T),
    }
    selbuf = np.zeros((128, 32), BF)
    selbuf[:, 16] = 1.0
    onesrow = np.ones((1, 128), BF)
    tri = np.zeros((128, 128), BF)
    for i in range(128):
        tri[i, :i] = MASK
    shared["selbuf"] = selbuf
    shared["onesrow"] = onesrow
    shared["onesrf"] = np.ones((1, 128), np.float32)
    shared["tri"] = tri
    consts = np.array([EPS_LN, EPS_RMS, 1.0 / 128.0, -0.5, SCALE,
                       float(np.log(np.float32(SCALE))), 1.0, -1.0],
                      np.float32)
    shared["consts"] = np.tile(consts[None, :], (128, 1))

    half = 64
    inv = (1.0 / (np.float32(ROPE_BASE) **
                  (np.arange(half, dtype=np.float32) / np.float32(half)))).astype(np.float32)
    pos = np.arange(T, dtype=np.float32)
    freqs = pos[:, None] * inv[None, :]
    cosT = np.cos(freqs).astype(np.float32)
    sinT = np.sin(freqs).astype(np.float32)

    maps = []
    for c in range(2 * B):
        b, ct = c // 2, c % 2
        groups, perm_q, perm_kv = core_perms(cfg, ct)
        m = dict(shared)
        xb = np.asarray(x[b], np.float32)
        m["xqT"] = np.ascontiguousarray(xb[perm_q].T)
        m["xkvT"] = np.ascontiguousarray(xb[perm_kv].T)
        m["cosq"] = np.ascontiguousarray(cosT[perm_q].T)
        m["sinq"] = np.ascontiguousarray(sinT[perm_q].T)
        m["cosk"] = np.ascontiguousarray(cosT[perm_kv].T)
        m["sink"] = np.ascontiguousarray(sinT[perm_kv].T)
        kb = np.zeros((128, NIT), BF)
        u = 0
        for j in range(NS):
            iters = cfg["ITERS"][j]
            plains = iters - 4
            for it in range(iters):
                if ct == 0 and plains - 4 <= it < plains:
                    kb[:, u] = MASK
                u += 1
        m["kbias"] = kb
        maps.append(m)
    return maps



class Ctx:

    def __init__(self, nc):
        self.nc = nc
        self.sems = {}
        self.cnt = {}
        self.cur = None
        self.eng = None

    def mksem(self, es, *names):
        for n in names:
            self.sems[n] = es.enter_context(self.nc.semaphore(n))
            self.cnt[n] = 0

    def c(self, name):
        return self.cnt[name]

    def op(self, engine, fn=None, waits=(), inc=None, n=1, guard=True):
        waits = [(s, v) for (s, v) in waits if v > 0]
        if engine in ("vector", "scalar") and fn is not None:
            gsem = "vm" if engine == "vector" else "sm"
            if guard and self.cnt[gsem] > 0:
                waits.append((gsem, self.cnt[gsem]))
            if inc is None:
                inc = gsem
        if self.cur == engine:
            for s, v in waits:
                self.eng.wait_ge(self.sems[s], v)
            if fn is not None:
                instr = fn(self.eng)
                if inc is not None:
                    instr.then_inc(self.sems[inc], n)
        if fn is not None and inc is not None:
            self.cnt[inc] += n

    def wait(self, engine, waits):
        self.op(engine, None, waits=waits)

    def barrier(self):
        marks = [(s, self.cnt[s]) for s in ("tm", "sm", "vm") if self.cnt[s] > 0]
        for e in ("tensor", "scalar", "vector", "sync", "gp"):
            self.op(e, None, waits=marks)


def run_phase(nc, ctx, fn):
    snap = dict(ctx.cnt)
    with nc.Block() as blk:
        def mk(name):
            def body(eng, name=name):
                ctx.cnt = dict(snap)
                ctx.cur, ctx.eng = name, eng
                fn(ctx)
            return body
        blk.tensor(mk("tensor"))
        blk.scalar(mk("scalar"))
        blk.vector(mk("vector"))
        blk.sync(mk("sync"))
        blk.gpsimd(mk("gp"))
    ctx.cnt = dict(snap)
    ctx.cur, ctx.eng = None, None
    fn(ctx)


def A(t, off, dims):
    return bass.AP(t, off, [list(d) for d in dims])



def build_program(cfg, dump=()):
    B, T, D, NH, NKV, DH = (cfg[k] for k in ("B", "T", "D", "NH", "NKV", "DH"))
    NS, OWN, KEXT, DK, HC = (cfg[k] for k in ("NSLOT", "OWN", "KEXT", "DK", "HC"))
    GRP, ITERS, NIT, NGK = cfg["GRP"], cfg["ITERS"], cfg["NIT"], cfg["NGK"]
    NPAIR = GRP // 2
    QD = NH * 128
    KD = NKV * 128
    KCH = KEXT // 128
    TCH = T // 128
    NGRP = OWN // 512
    DSETW = min(8, DK)
    NDSET = DK // DSETW
    F2ROT = DSETW <= 4
    HGS = min(8, HC)
    NHG = HC // HGS

    nc = bass.Bass("TRN2", target_bir_lowering=False)
    dram = {}
    for name, shape, dt in [
        ("xqT", [D, OWN], F32), ("xkvT", [D, KEXT], F32),
        ("wq", [D, QD], F32), ("wk", [D, KD], F32), ("wv", [D, KD], F32),
        ("wo", [D, D], F32),
        ("wgT", [D, DH], F32), ("wuT", [D, DH], F32), ("wdT", [DH, D], F32),
        ("cosq", [64, OWN], F32), ("sinq", [64, OWN], F32),
        ("cosk", [64, KEXT], F32), ("sink", [64, KEXT], F32),
        ("kbias", [128, NIT], BF16), ("tri", [128, 128], BF16),
        ("selbuf", [128, 32], BF16), ("onesrow", [1, 128], BF16),
        ("onesrf", [1, 128], F32),
        ("consts", [128, 8], F32),
    ]:
        dram[name] = nc.dram_tensor(name, shape, dt, kind="ExternalInput").ap()
    out = nc.dram_tensor("outT", [D, OWN], F32, kind="ExternalOutput").ap()
    scr_out1 = nc.dram_tensor("scr_out1", [D, OWN], F32,
                              kind="ExternalOutput" if "out1" in dump else "Internal").ap()
    scr_ln2 = nc.dram_tensor("scr_ln2", [D, OWN], BF16,
                             kind="ExternalOutput" if "ln2" in dump else "Internal").ap()
    scr_gated = nc.dram_tensor("scr_gated", [DH, OWN], BF16,
                               kind="ExternalOutput" if "gated" in dump else "Internal").ap()
    scr_wd = nc.dram_tensor("scr_wd", [DH, D], BF16, kind="Internal").ap()
    scr_wq2 = nc.dram_tensor("scr_wq2", [QD // 128, 128, DK * 128], BF16, kind="Internal").ap()
    scr_wo2 = nc.dram_tensor("scr_wo2", [D // 128, 128, DK * 128], BF16, kind="Internal").ap()
    dump_t = {}
    if "pad" in dump:
        for pi in range(2):
            nc.dram_tensor(f"padscr{pi}", [1024, 1024], F32, kind="Internal")
    if "kv" in dump:
        dump_t["kT"] = nc.dram_tensor("dump_kT", [128, NKV * KEXT], BF16, kind="ExternalOutput").ap()
        dump_t["V"] = nc.dram_tensor("dump_V", [128, KCH * KD], BF16, kind="ExternalOutput").ap()
    if "qy" in dump:
        dump_t["qT"] = nc.dram_tensor("dump_qT", [128, NS * NH * 512], BF16, kind="ExternalOutput").ap()
        dump_t["yT"] = nc.dram_tensor("dump_yT", [128, NS * NH * 512], BF16, kind="ExternalOutput").ap()

    es = ExitStack()
    ctx = Ctx(nc)
    ctx.mksem(es, "tm", "sm", "vm", "gm",
              "d_const", "d_xk0", "d_xk1", "d_wkv", "d_cs", "d_cs0", "d_cs1", "d_xq0", "d_xq1",
              "d_wq0", "d_wq1", "d_xr0", "d_xr1", "d_sp0", "d_sp1", "d_wp2a", "d_wp2b", "d_ws2a", "d_ws2b",
              "d_l20", "d_l21", "d_fw0", "d_fw1", "d_wp0", "d_wp1",
              "d_ws0", "d_ws1", "d_ln2ld", "d_gs0", "d_gs1",
              "d_gl0", "d_gl1", "d_gl2", "d_wl0", "d_wl1", "d_wl2", "d_wm0", "d_wm1", "d_wm2", "d_o1ld", "d_o10", "d_o11",
              "d_ou0", "d_ou1", "d_dump")

    def sb(name, shape, dt=F32):
        return es.enter_context(nc.sbuf_tensor(name, shape, dt))

    with es:
        selb = sb("selb", [128, 32], BF16)
        onesrf = sb("onesrf_s", [1, 128])
        trib = sb("trib", [128, 128], BF16)
        kbb = sb("kbb", [128, NIT], BF16)
        cst = sb("cst", [128, 8])

        def CC(i, p=128):
            return cst[0:p, i:i + 1]

        onescol = selb[:, 16:17]

        akv_es = ExitStack()

        def asb0(name, shape, dt=F32):
            return akv_es.enter_context(nc.sbuf_tensor(name, shape, dt))

        kT = asb0("kT", [128, NKV * KEXT], BF16)
        V = asb0("V", [128, KCH * KD], BF16)
        row_m = asb0("row_m", [1, 512])
        row_t = asb0("row_t", [1, 512])
        row_r = asb0("row_r", [1, 512])
        row_mr = asb0("row_mr", [1, 512])
        rsc = asb0("rsc", [128, NKV * KCH])

        def ln_rows2(ctx, psA_ap, psB_ap, m_stats, eps_idx):
            class _W:
                def __init__(self, ap):
                    self.ap = ap

                def __getitem__(self, k):
                    return self.ap
            return ln_rows(ctx, _W(psA_ap), _W(psB_ap), m_stats, eps_idx)

        def ln_rows(ctx, psA, psB, m_stats, eps_idx):
            ctx.op("vector", lambda e: e.tensor_scalar_mul(row_m[:, :], psA[0:1, :], 1.0 / D),
                   waits=[("tm", m_stats)])
            ctx.op("vector", lambda e: e.tensor_mul(row_t[:, :], row_m[:, :], row_m[:, :]))
            ctx.op("vector", lambda e: e.scalar_tensor_tensor(
                row_t[:, :], psB[0:1, :], 1.0 / D, row_t[:, :], ALU.mult, ALU.subtract))
            m_var = ctx.c("vm")
            ctx.op("scalar", lambda e: e.activation(row_t[:, :], row_t[:, :], AF.Ln,
                                                    bias=CC(eps_idx, 1)),
                   waits=[("vm", m_var)])
            ctx.op("scalar", lambda e: e.activation(row_r[:, :], row_t[:, :], AF.Exp,
                                                    scale=CC(3, 1)))
            m_rstd = ctx.c("sm")
            ctx.op("vector", lambda e: e.tensor_mul(row_mr[:, :], row_m[:, :], row_r[:, :]),
                   waits=[("sm", m_rstd)])
            return ctx.c("vm")

        def bcastF(ctx, dst, row_ap, waits):
            ctx.op("tensor", lambda e: e.matmul(dst, onesrf[0:1, :], row_ap,
                                                start=True, stop=True), waits=waits, inc="tm")
            return ctx.c("tm")


        def ph_const(ctx):
            ctx.op("sync", lambda e: e.dma_start(out=selb[:, :], in_=dram["selbuf"][:, :]), inc="d_const", n=16)
            ctx.op("sync", lambda e: e.dma_start(out=onesrf[:, :], in_=dram["onesrf"][:, :]), inc="d_const", n=16)
            ctx.op("sync", lambda e: e.dma_start(out=trib[:, :], in_=dram["tri"][:, :]), inc="d_const", n=16)
            ctx.op("sync", lambda e: e.dma_start(out=kbb[:, :], in_=dram["kbias"][:, :]), inc="d_const", n=16)
            ctx.op("sync", lambda e: e.dma_start(out=cst[:, :], in_=dram["consts"][:, :]), inc="d_const", n=16)
            ctx.wait("sync", [("d_const", 80)])
            ctx.wait("tensor", [("d_const", 80)])
            ctx.wait("vector", [("d_const", 80)])
            ctx.wait("scalar", [("d_const", 80)])

        run_phase(nc, ctx, ph_const)

        with ExitStack() as kv_es:
            def ksb(name, shape, dt=F32):
                return kv_es.enter_context(nc.sbuf_tensor(name, shape, dt))

            def kps(name, shape):
                return kv_es.enter_context(nc.psum_tensor(name, shape, F32))

            wkb = ksb("wkb", [128, DK * KD], BF16)
            wvb = ksb("wvb", [128, DK * KD], BF16)
            xb0 = ksb("xb0", [128, DK * 512], BF16)
            xb1 = ksb("xb1", [128, DK * 512], BF16)
            xsq = ksb("xsq", [128, 1024], BF16)
            tmA = ksb("tmA", [64, 1024])
            tmB = ksb("tmB", [64, 1024])
            negm = ksb("negm", [1, 512], BF16)
            rtc = ksb("rtc", [128, 8])
            wpre0 = ksb("wpre0", [128, 2048], BF16)
            wpre1 = ksb("wpre1", [128, 2048], BF16)
            wpre = [wpre0, wpre1]
            kcolsb = ksb("kcolsb", [1, KD], BF16)
            vcolsb = ksb("vcolsb", [1, KD], BF16)
            csk0 = ksb("csk0", [64, 512])
            snk0 = ksb("snk0", [64, 512])
            csk1 = ksb("csk1", [64, 512])
            snk1 = ksb("snk1", [64, 512])
            rosq = ksb("rosq", [128, 1024], BF16)
            ps_a = kps("ps_a", [1, 512])
            ps_b = kps("ps_b", [1, 512])
            ps_rt = kps("ps_rt", [128, 8])
            ps_K = kps("ps_K", [128, 1024])
            ps_V = kps("ps_V", [128, max(KD, 512)])
            ps_rc = kps("ps_rc", [128, NKV * KCH])
            xbb = [xb0, xb1]
            csk = [csk0, csk1]
            snk = [snk0, snk1]
            NPR = NKV // 2 if NKV >= 2 else 1
            PRW = 2 if NKV >= 2 else 1

            def ph_kv(ctx):
                ctx.op("gp", lambda e: e.dma_start(
                    out=A(wkb, 0, [[DK * KD, 128], [KD, DK], [1, KD]]),
                    in_=A(dram["wk"].tensor, 0, [[KD, 128], [128 * KD, DK], [1, KD]])),
                    inc="d_wkv", n=16)
                ctx.op("gp", lambda e: e.dma_start(
                    out=A(wvb, 0, [[DK * KD, 128], [KD, DK], [1, KD]]),
                    in_=A(dram["wv"].tensor, 0, [[KD, 128], [128 * KD, DK], [1, KD]])),
                    inc="d_wkv", n=16)

                mk = {"stats": 0, "kvdone": 0, "ropeM": {},
                      "cs_v": [0, 0], "mxkv": [("d_xk0", 0), ("d_xk1", 0)], "mcs": [0, 0], "lastLn": 0,
                      "rows_t": 0, "negm_v": 0, "rt_s": 0, "rt_s2": 0, "rms_t": 0}

                CW = min(512, KD)
                for j in range(KD // CW):
                    for dk in range(DK):
                        ctx.op("tensor", lambda e, j=j, dk=dk: e.matmul(
                            ps_a[0:1, 0:CW], onescol, wkb[:, dk * KD + j * CW: dk * KD + (j + 1) * CW],
                            start=(dk == 0), stop=(dk == DK - 1)),
                            waits=([("d_wkv", 32)] if dk == 0 and j == 0 else []))
                        ctx.op("tensor", lambda e, j=j, dk=dk: e.matmul(
                            ps_b[0:1, 0:CW], onescol, wvb[:, dk * KD + j * CW: dk * KD + (j + 1) * CW],
                            start=(dk == 0), stop=(dk == DK - 1)),
                            waits=([("d_wkv", 32)] if dk == 0 and j == 0 else []),
                            inc=("tm" if dk == DK - 1 else None))
                    m_cs0 = ctx.c("tm")
                    ctx.op("scalar", lambda e, j=j: e.activation(
                        kcolsb[0:1, j * CW:(j + 1) * CW], ps_a[0:1, 0:CW], AF.Copy),
                        waits=[("tm", m_cs0)])
                    ctx.op("scalar", lambda e, j=j: e.activation(
                        vcolsb[0:1, j * CW:(j + 1) * CW], ps_b[0:1, 0:CW], AF.Copy))
                mk["colsum_s"] = ctx.c("sm")
                mk["lastLn"] = ctx.c("sm")

                def emit_prep_dma_only(g):
                    gb = g % 2
                    ctx.op("gp", lambda e, g=g, gb=gb: e.dma_start(
                        out=A(xbb[gb], 0, [[DK * 512, 128], [512, DK], [1, 512]]),
                        in_=A(dram["xkvT"].tensor, g * 512,
                              [[KEXT, 128], [128 * KEXT, DK], [1, 512]])),
                        waits=[("tm", mk["kvdone"])],
                        inc=f"d_xk{gb}", n=16)
                    mk["mxkv"][gb] = (f"d_xk{gb}", ctx.c(f"d_xk{gb}"))
                    cb = g % 2
                    ctx.op("sync", lambda e, g=g, cb=cb: e.dma_start(
                        out=csk[cb][:, :], in_=A(dram["cosk"].tensor, g * 512, [[KEXT, 64], [1, 512]])),
                        waits=[("vm", mk["cs_v"][cb])], inc=f"d_cs{cb}", n=16)
                    ctx.op("sync", lambda e, g=g, cb=cb: e.dma_start(
                        out=snk[cb][:, :], in_=A(dram["sink"].tensor, g * 512, [[KEXT, 64], [1, 512]])),
                        inc=f"d_cs{cb}", n=16)
                    mk["mcs"][cb] = ctx.c(f"d_cs{cb}")

                def emit_prep_compute(g):
                    gb = g % 2
                    xb = xbb[gb]
                    m_xkv = mk["mxkv"][gb]
                    for dk in range(DK):
                        ctx.op("tensor", lambda e, dk=dk, xb=xb: e.matmul(
                            ps_a[0:1, :], onescol, xb[:, dk * 512:(dk + 1) * 512],
                            start=(dk == 0), stop=(dk == DK - 1)),
                            waits=([m_xkv, ("vm", mk["rows_t"]),
                                    ("sm", mk["lastLn"])] if dk == 0 else []),
                            inc=("tm" if dk == DK - 1 else None))
                    m_suma = ctx.c("tm")
                    msq = {}
                    mmb = {-2: mk["stats"], -1: mk["stats"]}
                    for dk in range(DK):
                        ctx.op("scalar", lambda e, dk=dk, xb=xb: e.activation(
                            xsq[:, (dk % 2) * 512:(dk % 2) * 512 + 512],
                            xb[:, dk * 512:(dk + 1) * 512], AF.Square),
                            waits=([m_xkv] if dk == 0 else []) +
                                  [("tm", mmb[dk - 2])])
                        msq[dk] = ctx.c("sm")
                        ctx.op("tensor", lambda e, dk=dk: e.matmul(
                            ps_b[0:1, :], onescol, xsq[:, (dk % 2) * 512:(dk % 2) * 512 + 512],
                            start=(dk == 0), stop=(dk == DK - 1)),
                            waits=[("sm", msq[dk])], inc="tm")
                        mmb[dk] = ctx.c("tm")
                    mk["stats"] = ctx.c("tm")
                    ctx.op("vector", lambda e: e.tensor_scalar_mul(negm[:, :], ps_a[0:1, :], -1.0 / D),
                           waits=[("tm", m_suma), ("tm", mk["rank1_t"] if "rank1_t" in mk else 0)])
                    mk["negm_v"] = ctx.c("vm")
                    ctx.op("vector", lambda e: e.tensor_scalar_mul(row_m[:, :], ps_a[0:1, :], 1.0 / D))
                    ctx.op("vector", lambda e: e.tensor_mul(row_t[:, :], row_m[:, :], row_m[:, :]))
                    ctx.op("vector", lambda e: e.scalar_tensor_tensor(
                        row_t[:, :], ps_b[0:1, :], 1.0 / D, row_t[:, :], ALU.mult, ALU.subtract),
                        waits=[("tm", mk["stats"])])
                    m_var = ctx.c("vm")
                    ctx.op("scalar", lambda e: e.activation(row_t[:, :], row_t[:, :], AF.Ln,
                                                            bias=CC(0, 1)),
                           waits=[("vm", m_var)])
                    ctx.op("scalar", lambda e: e.activation(row_r[:, :], row_t[:, :], AF.Exp,
                                                            scale=CC(3, 1)))
                    mk["lastLn"] = ctx.c("sm")
                    m_rstd = ctx.c("sm")
                    for ts in range(4):
                        ctx.op("tensor", lambda e, ts=ts: e.matmul(
                            ps_rt[:, ts:ts + 1], row_r[0:1, ts * 128:(ts + 1) * 128],
                            cst[0:1, 6:7], start=True, stop=True),
                            waits=([("sm", m_rstd), ("sm", mk["rt_s"])] if ts == 0 else []),
                            inc=("tm" if ts == 3 else None))
                    m_rtt = ctx.c("tm")
                    ctx.op("scalar", lambda e: e.activation(rtc[:, 0:4], ps_rt[:, 0:4], AF.Copy),
                           waits=[("tm", m_rtt)])
                    mk["rt_s2"] = ctx.c("sm")
                    mk["rows_t"] = ctx.c("vm")
                    mk["rt_t"] = ctx.c("tm")

                def emit_kproj(g, p):
                    gb = g % 2
                    xb = xbb[gb]
                    m_xkv = mk["mxkv"][gb]
                    for pi in range(PRW):
                        kc = p * PRW + pi
                        for dk in range(DK):
                            w = []
                            if dk == 0:
                                w = [("d_wkv", 32), m_xkv,
                                     ("vm", mk["ropeM"].get(p - 1, 0))]
                            ctx.op("tensor", lambda e, kc=kc, pi=pi, dk=dk, xb=xb: e.matmul(
                                ps_K[:, pi * 512:(pi + 1) * 512],
                                wkb[:, dk * KD + kc * 128: dk * KD + (kc + 1) * 128],
                                xb[:, dk * 512:(dk + 1) * 512],
                                start=(dk == 0), stop=False), waits=w)
                        ctx.op("tensor", lambda e, kc=kc, pi=pi: e.matmul(
                            ps_K[:, pi * 512:(pi + 1) * 512],
                            kcolsb[0:1, kc * 128:(kc + 1) * 128], negm[0:1, :],
                            start=False, stop=True),
                            waits=[("vm", mk["negm_v"]), ("sm", mk["colsum_s"])],
                            inc=("tm" if pi == PRW - 1 else None))
                    return ctx.c("tm")

                def emit_vproj(g, ts, m_evprev):
                    gb = g % 2
                    xb = xbb[gb]
                    posv = g * 4 + ts
                    for dk in range(DK):
                        w = []
                        if dk == 0:
                            w = [("sm", m_evprev), ("d_wkv", 32)]
                        ctx.op("tensor", lambda e, ts=ts, dk=dk, xb=xb: e.matmul(
                            ps_V[:, 0:KD],
                            xb[:, dk * 512 + ts * 128: dk * 512 + ts * 128 + 128],
                            wvb[:, dk * KD:(dk + 1) * KD],
                            start=(dk == 0), stop=False), waits=w)
                    ctx.op("tensor", lambda e, ts=ts: e.matmul(
                        ps_V[:, 0:KD], negm[0:1, ts * 128:(ts + 1) * 128],
                        vcolsb[0:1, 0:KD],
                        start=False, stop=True),
                        waits=[("vm", mk["negm_v"]), ("sm", mk["colsum_s"]),
                               ("tm", mk["rt_t"])],
                        inc="tm")
                    m_V = ctx.c("tm")
                    ctx.op("scalar", lambda e, posv=posv, ts=ts: e.activation(
                        V[:, posv * KD:(posv + 1) * KD], ps_V[:, 0:KD], AF.Copy,
                        scale=rtc[:, ts:ts + 1]),
                        waits=[("tm", m_V), ("sm", mk["rt_s2"])])
                    mk["rt_s"] = mk["rt_s2"]
                    return ctx.c("sm")

                def rope_pair(g, p, m_K):
                    cb = g % 2
                    kv0 = p * PRW
                    x1 = A(ps_K, 0, [[1024, 64], [512, PRW], [1, 512]])
                    x2 = A(ps_K, 64 * 1024, [[1024, 64], [512, PRW], [1, 512]])
                    cc = A(csk[cb], 0, [[512, 64], [0, PRW], [1, 512]])
                    ss = A(snk[cb], 0, [[512, 64], [0, PRW], [1, 512]])
                    tA = A(tmA, 0, [[1024, 64], [512, PRW], [1, 512]])
                    tB = A(tmB, 0, [[1024, 64], [512, PRW], [1, 512]])
                    ktlo = A(kT, kv0 * KEXT + g * 512, [[KEXT * NKV, 64], [KEXT, PRW], [1, 512]])
                    kthi = A(kT, 64 * KEXT * NKV + kv0 * KEXT + g * 512,
                             [[KEXT * NKV, 64], [KEXT, PRW], [1, 512]])
                    ctx.op("vector", lambda e: e.tensor_tensor(tA, x1, cc, ALU.mult),
                           waits=[("tm", m_K), (f"d_cs{cb}", mk["mcs"][cb])])
                    ctx.op("vector", lambda e: e.tensor_tensor(tB, x2, ss, ALU.mult))
                    ctx.op("vector", lambda e: e.tensor_tensor(ktlo, tA, tB, ALU.add))
                    ctx.op("vector", lambda e: e.tensor_tensor(tA, x2, cc, ALU.mult))
                    ctx.op("vector", lambda e: e.tensor_tensor(tB, x1, ss, ALU.mult))
                    ctx.op("vector", lambda e: e.tensor_tensor(kthi, tA, tB, ALU.subtract))
                    mk["ropeM"][p] = ctx.c("vm")
                    mk["cs_v"][cb] = ctx.c("vm")
                    m_kt = ctx.c("vm")
                    ctx.op("scalar", lambda e, kv0=kv0, g=g: e.activation(
                        A(rosq, 0, [[1024, 128], [512, PRW], [1, 512]]),
                        A(kT, kv0 * KEXT + g * 512, [[KEXT * NKV, 128], [KEXT, PRW], [1, 512]]),
                        AF.Square),
                        waits=[("vm", m_kt), ("tm", mk["rms_t"])])
                    return ctx.c("sm")

                def rms_cols(g, p, m_rosq):
                    kv0 = p * PRW
                    for pi in range(PRW):
                        kv = kv0 + pi
                        for ts in range(4):
                            col = kv * KCH + g * 4 + ts
                            ctx.op("tensor", lambda e, pi=pi, ts=ts, col=col: e.matmul(
                                ps_rc[:, col:col + 1],
                                rosq[:, pi * 512 + ts * 128: pi * 512 + ts * 128 + 128],
                                onescol, start=True, stop=True),
                                waits=([("sm", m_rosq)] if (pi == 0 and ts == 0) else []),
                                inc=("tm" if (pi == PRW - 1 and ts == 3) else None))
                    mk["rms_t"] = ctx.c("tm")
                    return ctx.c("tm")

                def compute(g):
                    mk["ropeM"] = {}
                    mK = {}
                    rq = {}
                    mK[0] = emit_kproj(g, 0)
                    m_ev = ctx.c("sm")
                    rq[0] = rope_pair(g, 0, mK[0])
                    vts = 0
                    for p in range(NPR):
                        for _ in range(4 // NPR):
                            m_ev = emit_vproj(g, vts, m_ev)
                            vts += 1
                        rms_cols(g, p, rq[p])
                        if p + 1 < NPR:
                            mK[p + 1] = emit_kproj(g, p + 1)
                            rq[p + 1] = rope_pair(g, p + 1, mK[p + 1])
                    while vts < 4:
                        m_ev = emit_vproj(g, vts, m_ev)
                        vts += 1
                    mk["rank1_t"] = ctx.c("tm")
                    mk["kvdone"] = ctx.c("tm")

                pre_jobs = ([("wq", scr_wq2, i) for i in range(DK)] +
                            [("wo", scr_wo2, i) for i in range(DK)])
                mk["pre_i"] = 0

                def emit_precast(n):
                    for _ in range(n):
                        i = mk["pre_i"]
                        if i >= len(pre_jobs):
                            return
                        mk["pre_i"] += 1
                        name, scr, dk = pre_jobs[i]
                        pb = i % 2
                        ps2 = "ab"[pb]
                        W = QD if name == "wq" else D
                        ctx.op("gp", lambda e, name=name, dk=dk, pb=pb, W=W: e.dma_start(
                            out=wpre[pb][:, 0:W],
                            in_=A(dram[name].tensor, dk * 128 * W, [[W, 128], [1, W]])),
                            waits=[(f"d_ws2{ps2}", ctx.c(f"d_ws2{ps2}"))],
                            inc=f"d_wp2{ps2}", n=16)
                        m_l = ctx.c(f"d_wp2{ps2}")
                        ctx.op("sync", lambda e, scr=scr, dk=dk, pb=pb, W=W: e.dma_start(
                            out=A(scr.tensor, dk * 128, [[D, 128], [128 * D, W // 128], [1, 128]]),
                            in_=A(wpre[pb], 0, [[2048, 128], [128, W // 128], [1, 128]])),
                            waits=[(f"d_wp2{ps2}", m_l)], inc=f"d_ws2{ps2}", n=16)

                emit_prep_dma_only(0)
                emit_prep_compute(0)
                for g in range(NGK):
                    if g + 1 < NGK:
                        emit_prep_dma_only(g + 1)
                    emit_precast(3)
                    compute(g)
                    if g + 1 < NGK:
                        emit_prep_compute(g + 1)
                emit_precast(len(pre_jobs))
                ctx.wait("sync", [("d_ws2a", ctx.c("d_ws2a")), ("d_ws2b", ctx.c("d_ws2b"))])

                m_cols = ctx.c("tm")
                ctx.op("scalar", lambda e: e.activation(
                    A(rsc, 0, [[NKV * KCH, 128], [1, NKV * KCH]]),
                    ps_rc[:, :], AF.Ln, bias=CC(1), scale=CC(2)),
                    waits=[("tm", m_cols)])
                ctx.op("scalar", lambda e: e.activation(
                    A(rsc, 0, [[NKV * KCH, 128], [1, NKV * KCH]]),
                    A(rsc, 0, [[NKV * KCH, 128], [1, NKV * KCH]]),
                    AF.Exp, scale=CC(3), bias=CC(5)))
                mk["lastLn"] = ctx.c("sm")

                if "kv" in dump:
                    mv = ctx.c("vm")
                    ctx.op("sync", lambda e: e.dma_start(out=dump_t["kT"][:, :], in_=kT[:, :]),
                           waits=[("vm", mv)], inc="d_dump", n=16)
                    ctx.op("sync", lambda e: e.dma_start(out=dump_t["V"][:, :], in_=V[:, :]),
                           inc="d_dump", n=16)
                    ctx.wait("sync", [("d_dump", ctx.c("d_dump"))])

            run_phase(nc, ctx, ph_kv)

        with ExitStack() as at_es:
            def asb(name, shape, dt=F32):
                return at_es.enter_context(nc.sbuf_tensor(name, shape, dt))

            def aps(name, shape):
                return at_es.enter_context(nc.psum_tensor(name, shape, F32))

            qT = asb("qT", [128, NH * 512], BF16)
            xr4v = qT.bitcast(F32)
            lyT = asb("lyT", [128, max(NH, DK) * 512], BF16)
            ln1q = lyT
            yT = lyT
            xbq0 = asb("xbq0", [128, DK * 512], BF16)
            xbq1 = asb("xbq1", [128, DK * 512], BF16)
            xbqs = [xbq0, xbq1]
            xsq2 = asb("xsq2", [128, 512], BF16)
            probs0 = asb("probs0", [128, 1024], BF16)
            probs1 = asb("probs1", [128, 1024], BF16)
            reca = row_t
            recb = row_r
            wqb0 = asb("wqb0", [128, DK * 128], BF16)
            wqb1 = asb("wqb1", [128, DK * 128], BF16)
            csq = asb("csq", [64, 512])
            snq = asb("snq", [64, 512])
            roq0 = asb("roq0", [128, 512])
            roq1 = asb("roq1", [128, 512])
            tmpq = asb("tmpq", [128, 512])
            rowr1 = asb("rowr1", [1, 512])
            lnt2 = asb("lnt2", [128, 512])
            negmq = asb("negmq", [1, 512], BF16)
            qcolsb = asb("qcolsb", [1, QD], BF16)
            o1f0 = asb("o1f0", [128, 512])
            o1f1 = asb("o1f1", [128, 512])
            roqs = [roq0, roq1]
            rowrs = [row_r, rowr1]
            pS0 = aps("pS0", [128, 1024])
            pS1 = aps("pS1", [128, 1024])
            pY = aps("pY", [128, 1024])
            pR = aps("pR", [128, 1024])

            probs = [probs0, probs1]
            pS = [pS0, pS1]
            wqb = [wqb0, wqb1]
            o1f = [o1f0, o1f1]

            def ph_slots(ctx):
                ctx.barrier()
                mk = {"q_v": 0, "wq_t": [0, 0], "yev": 0, "avp": 0,
                      "xres_v": [0, 0], "o1_t": 0, "pexp": [0, 0], "msc": {},
                      "qcol_s": 0, "qS_s": 0, "qpfree": 0, "pSfree_s": 0, "pSfree_v": 0,
                      "xbq_v": [0, 0], "xbq_t": [0, 0],
                      "m_xq": [0, 0], "pRfree": 0}
                def xq_load(slot):
                    sb2 = slot % 2
                    ctx.op("gp", lambda e, slot=slot, sb2=sb2: e.dma_start(
                        out=A(xbqs[sb2], 0, [[DK * 512, 128], [512, DK], [1, 512]]),
                        in_=A(dram["xqT"].tensor, slot * 512,
                              [[OWN, 128], [128 * OWN, DK], [1, 512]])),
                        waits=[("vm", mk["xbq_v"][sb2]), ("tm", mk["xbq_t"][sb2])],
                        inc=f"d_xq{sb2}", n=16)
                    mk["m_xq"][sb2] = ctx.c(f"d_xq{sb2}")

                xq_load(0)
                for slot in range(NS):
                    uslot = sum(ITERS[:slot])
                    sb2 = slot % 2
                    xbq = xbqs[sb2]
                    m_xq0 = (f"d_xq{sb2}", mk["m_xq"][sb2])
                    m_xq = mk["m_xq"][sb2]
                    ctx.op("sync", lambda e, slot=slot: e.dma_start(
                        out=csq[:, :], in_=A(dram["cosq"].tensor, slot * 512, [[OWN, 64], [1, 512]])),
                        waits=[("vm", mk["q_v"])], inc="d_cs", n=16)
                    ctx.op("sync", lambda e, slot=slot: e.dma_start(
                        out=snq[:, :], in_=A(dram["sinq"].tensor, slot * 512, [[OWN, 64], [1, 512]])),
                        inc="d_cs", n=16)
                    m_cs = ctx.c("d_cs")
                    for dk in range(DK):
                        ctx.op("tensor", lambda e, dk=dk: e.matmul(
                            pS1[0:1, 0:512], onescol, xbq[:, dk * 512:(dk + 1) * 512],
                            start=(dk == 0), stop=(dk == DK - 1)),
                            waits=([m_xq0, ("sm", mk["pSfree_s"]),
                                    ("vm", mk["pSfree_v"])] if dk == 0 else []),
                            inc=("tm" if dk == DK - 1 else None))
                    m_stats = ctx.c("tm")
                    ctx.op("vector", lambda e: e.tensor_scalar_mul(negmq[:, :], pS1[0:1, 0:512], -1.0 / D),
                           waits=[("tm", m_stats)])
                    m_negmq = ctx.c("vm")

                    mq = {"proj": {}, "ropeM": {}, "exp": {}, "hilo": {}, "ktm": {}}

                    def q_P(qc):
                        buf = qc % 2
                        ctx.op("sync" if buf == 0 else "gp", lambda e, qc=qc, buf=buf: e.dma_start(
                            out=A(wqb[buf], 0, [[DK * 128, 128], [1, DK * 128]]),
                            in_=A(scr_wq2.tensor, qc * 128 * D, [[D, 128], [1, D]])),
                            waits=[("tm", mk["wq_t"][buf])], inc=f"d_wq{buf}", n=16)
                        m_wq = ctx.c(f"d_wq{buf}")
                        if slot == 0:
                            for dk in range(DK):
                                ctx.op("tensor", lambda e, dk=dk, buf=buf: e.matmul(
                                    pR[0:1, 128:256], onescol,
                                    wqb[buf][:, dk * 128:(dk + 1) * 128],
                                    start=(dk == 0), stop=(dk == DK - 1)),
                                    waits=([(f"d_wq{buf}", m_wq),
                                            ("sm", mk["qcol_s"])] if dk == 0 else []),
                                    inc=("tm" if dk == DK - 1 else None))
                            m_qc = ctx.c("tm")
                            ctx.op("scalar", lambda e, qc=qc: e.activation(
                                qcolsb[0:1, qc * 128:(qc + 1) * 128], pR[0:1, 128:256], AF.Copy),
                                waits=[("tm", m_qc)])
                            mk["qcol_s"] = ctx.c("sm")
                        qp = pS0[:, buf * 512:buf * 512 + 512]
                        for dk in range(DK):
                            w = []
                            if dk == 0:
                                w = [(f"d_wq{buf}", m_wq), m_xq0,
                                     ("vm", mq["ropeM"].get(qc - 2, 0)),
                                     ("sm", mq["exp"].get(qc - 2, 0)),
                                     ("vm", mk["pSfree_v"] if qc < 2 else 0)]
                            ctx.op("tensor", lambda e, qc=qc, dk=dk, qp=qp, buf=buf: e.matmul(
                                qp, wqb[buf][:, dk * 128:(dk + 1) * 128],
                                xbq[:, dk * 512:(dk + 1) * 512],
                                start=(dk == 0), stop=False), waits=w)
                        ctx.op("tensor", lambda e, qc=qc, qp=qp: e.matmul(
                            qp, qcolsb[0:1, qc * 128:(qc + 1) * 128], negmq[0:1, :],
                            start=False, stop=True),
                            waits=[("vm", m_negmq), ("sm", mk["qcol_s"])], inc="tm")
                        mk["wq_t"][buf] = ctx.c("tm")
                        mq["proj"][qc] = ctx.c("tm")

                    def q_R(qc):
                        qp = pS0[:, (qc % 2) * 512:(qc % 2) * 512 + 512]
                        ro = roqs[qc % 2]
                        x1 = qp[0:64, :]
                        x2 = qp[64:128, :]
                        ctx.op("vector", lambda e, x1=x1, ro=ro: e.tensor_mul(ro[0:64, :], x1, csq[:, :]),
                               waits=[("tm", mq["proj"][qc]), ("d_cs", m_cs)])
                        ctx.op("vector", lambda e, x2=x2: e.tensor_mul(tmpq[0:64, :], x2, snq[:, :]))
                        ctx.op("vector", lambda e, ro=ro: e.tensor_add(ro[0:64, :], ro[0:64, :], tmpq[0:64, :]))
                        ctx.op("vector", lambda e, x2=x2, ro=ro: e.tensor_mul(ro[64:128, :], x2, csq[:, :]))
                        ctx.op("vector", lambda e, x1=x1: e.tensor_mul(tmpq[64:128, :], x1, snq[:, :]))
                        ctx.op("vector", lambda e, ro=ro: e.tensor_sub(ro[64:128, :], ro[64:128, :], tmpq[64:128, :]))
                        mq["ropeM"][qc] = ctx.c("vm")

                    def q_S(qc):
                        ro = roqs[qc % 2]
                        ctx.op("scalar", lambda e, ro=ro: e.activation(xsq2[:, :], ro[:, :], AF.Square),
                               waits=[("vm", mq["ropeM"][qc]), ("tm", ctx.c("tm"))])
                        m_rosq = ctx.c("sm")
                        ctx.op("tensor", lambda e: e.matmul(pS1[0:1, 512:1024], onescol, xsq2[:, :],
                                                            start=True, stop=True),
                               waits=[("sm", m_rosq), ("vm", m_negmq if qc == 0 else 0)], inc="tm")
                        m_rms = ctx.c("tm")
                        ctx.op("scalar", lambda e: e.activation(row_t[:, :], pS1[0:1, 512:1024], AF.Ln,
                                                                bias=CC(1, 1), scale=CC(2, 1)),
                               waits=[("tm", m_rms)])
                        ctx.op("scalar", lambda e, qc=qc: e.activation(rowrs[qc % 2][:, :], row_t[:, :],
                                                                       AF.Exp, scale=CC(3, 1)),
                               waits=[("tm", mq["ktm"].get(qc - 2, 0))])
                        mq["exp"][qc] = ctx.c("sm")

                    def q_F(qc):
                        ro = roqs[qc % 2]
                        m_rqb = bcastF(ctx, pY[:, 0:512], rowrs[qc % 2][0:1, :],
                                       [("sm", mq["exp"][qc]),
                                        ("vm", mk["yev"] if qc == 0 else mq["hilo"].get(qc - 1, 0))])
                        mq["ktm"][qc] = m_rqb
                        ctx.op("vector", lambda e, qc=qc, ro=ro: e.tensor_mul(
                            qT[:, qc * 512:(qc + 1) * 512], ro[:, :], pY[:, 0:512]),
                            waits=[("tm", m_rqb),
                                   ("d_xr0", ctx.c("d_xr0")), ("d_xr1", ctx.c("d_xr1"))])
                        mq["hilo"][qc] = ctx.c("vm")

                    q_P(0)
                    if NH > 1:
                        q_P(1)
                    q_R(0)
                    q_S(0)
                    for qc in range(NH):
                        if qc + 2 < NH:
                            q_P(qc + 2)
                        if qc + 1 < NH:
                            q_R(qc + 1)
                            q_S(qc + 1)
                        q_F(qc)
                    m_qT = ctx.c("vm")
                    mk["qS_s"] = ctx.c("sm")
                    if slot + 1 < NS:
                        xq_load(slot + 1)

                    iters = ITERS[slot]
                    plains = iters - 4
                    for kv in range(NKV):
                        for pair in range(NPAIR):
                            h0 = kv * GRP + pair * 2
                            m_exp = {-2: mk["pexp"][0], -1: mk["pexp"][1]}
                            m_av_h = {-2: mk["avp"], -1: mk["avp"]}

                            def geo(it):
                                diag = it >= plains
                                r = it - plains if diag else 0
                                return (it % 2, diag, 128 * r,
                                        it if not diag else TCH + slot * 4 + r)

                            def emit_scores(it):
                                g2, diag, noff, pos = geo(it)
                                w_sc = [("sm", m_exp.get(it - 2, 0))]
                                if it == 0:
                                    w_sc.append(("vm", m_qT))
                                    w_sc.append(("sm", mk["qS_s"]))
                                for h in range(2):
                                    ctx.op("tensor", lambda e, pos=pos, h=h,
                                           noff=noff, g2=g2: e.matmul(
                                        pS[g2][:, h * 512 + noff: (h + 1) * 512],
                                        kT[:, kv * KEXT + pos * 128: kv * KEXT + (pos + 1) * 128],
                                        qT[:, (h0 + h) * 512 + noff: (h0 + h + 1) * 512],
                                        start=True, stop=True),
                                        waits=(w_sc if h == 0 else []),
                                        inc=("tm" if h == 1 else None))
                                m_sc = ctx.c("tm")
                                m_mask = 0
                                if diag:
                                    ctx.op("vector", lambda e, g2=g2, noff=noff: e.tensor_tensor(
                                        A(pS[g2], noff, [[1024, 128], [512, 2], [1, 128]]),
                                        A(pS[g2], noff, [[1024, 128], [512, 2], [1, 128]]),
                                        A(trib, 0, [[128, 128], [0, 2], [1, 128]]),
                                        ALU.add),
                                        waits=[("tm", m_sc)])
                                    m_mask = ctx.c("vm")
                                mk["msc"][it] = (m_sc, m_mask)

                            def emit_exp(it):
                                g2, diag, noff, pos = geo(it)
                                u = uslot + it
                                m_sc, m_mask = mk["msc"][it]
                                wexp = [("tm", m_sc), ("tm", m_av_h.get(it - 2, 0)),
                                        ("vm", m_mask)]
                                rcol = kv * KCH + pos
                                ctx.op("scalar", lambda e, g2=g2, noff=noff, u=u,
                                       rcol=rcol: e.activation(
                                    A(probs[g2], noff, [[1024, 128], [512, 2], [1, 512 - noff]]),
                                    A(pS[g2], noff, [[1024, 128], [512, 2], [1, 512 - noff]]),
                                    AF.Exp, bias=kbb[:, u:u + 1], scale=rsc[:, rcol:rcol + 1]),
                                    waits=wexp)
                                m_exp[it] = ctx.c("sm")

                            def emit_sumav(it):
                                g2, diag, noff, pos = geo(it)
                                for h in range(2):
                                    w = [("sm", m_exp[it])] if h == 0 else []
                                    if it == 0 and h == 0:
                                        w.append(("vm", max(mk["yev"], mk["pRfree"])))
                                    ctx.op("tensor", lambda e, h=h, g2=g2, noff=noff,
                                           it=it: e.matmul(
                                        pR[0:1, h * 512 + noff:h * 512 + 512], onescol,
                                        probs[g2][:, h * 512 + noff:(h + 1) * 512],
                                        start=(it == 0), stop=(it == iters - 1)),
                                        waits=w)
                                for h in range(2):
                                    ctx.op("tensor", lambda e, h=h, g2=g2, noff=noff,
                                           pos=pos, it=it: e.matmul(
                                        pY[:, h * 512 + noff:(h + 1) * 512],
                                        V[:, pos * KD + kv * 128: pos * KD + (kv + 1) * 128],
                                        probs[g2][:, h * 512 + noff:(h + 1) * 512],
                                        start=(it == 0), stop=(it == iters - 1)),
                                        inc=("tm" if h == 1 else None))
                                m_av_h[it] = ctx.c("tm")

                            mk["msc"] = {}
                            for it2 in range(0, iters, 2):
                                emit_scores(it2)
                                emit_exp(it2)
                                emit_scores(it2 + 1)
                                if it2 >= 1:
                                    emit_sumav(it2 - 1)
                                emit_exp(it2 + 1)
                                emit_sumav(it2)
                            emit_sumav(iters - 1)
                            m_av = ctx.c("tm")
                            mk["pexp"] = [m_exp[iters - 2], m_exp[iters - 1]]
                            mk["avp"] = m_av
                            ctx.op("scalar", lambda e: e.activation(reca[:, :], pR[0:1, 0:512],
                                                                    AF.Ln),
                                   waits=[("tm", m_av)])
                            ctx.op("scalar", lambda e: e.activation(reca[:, :], reca[:, :],
                                                                    AF.Exp, scale=CC(7, 1)))
                            ctx.op("scalar", lambda e: e.activation(recb[:, :], pR[0:1, 512:1024],
                                                                    AF.Ln))
                            ctx.op("scalar", lambda e: e.activation(recb[:, :], recb[:, :],
                                                                    AF.Exp, scale=CC(7, 1)))
                            m_rec = ctx.c("sm")
                            ctx.op("tensor", lambda e: e.matmul(pR[:, 0:512], onesrf[0:1, :], reca[0:1, :],
                                                                start=True, stop=True),
                                   waits=[("sm", m_rec)])
                            ctx.op("tensor", lambda e: e.matmul(pR[:, 512:1024], onesrf[0:1, :], recb[0:1, :],
                                                                start=True, stop=True), inc="tm")
                            m_rb = ctx.c("tm")
                            ctx.op("vector", lambda e: e.tensor_copy(roq0[:, :], pR[:, 0:512]),
                                   waits=[("tm", m_rb)])
                            ctx.op("vector", lambda e: e.tensor_copy(tmpq[:, :], pR[:, 512:1024]))
                            rbsh = [roq0, tmpq]
                            for h in range(2):
                                ctx.op("vector", lambda e, h=h, h0=h0: e.tensor_mul(
                                    yT[:, (h0 + h) * 512:(h0 + h + 1) * 512],
                                    pY[:, h * 512:(h + 1) * 512], rbsh[h][:, :]),
                                    waits=[("d_l20", ctx.c("d_l20"))] if (kv == 0 and pair == 0 and h == 0) else [])
                            mk["yev"] = ctx.c("vm")

                    m_yT = ctx.c("vm")
                    if "qy" in dump:
                        ctx.op("sync", lambda e, slot=slot: e.dma_start(
                            out=A(dump_t["qT"].tensor, slot * NH * 512,
                                  [[NS * NH * 512, 128], [1, NH * 512]]),
                            in_=qT[:, :]), waits=[("vm", m_yT)], inc="d_dump", n=16)
                        ctx.op("sync", lambda e, slot=slot: e.dma_start(
                            out=A(dump_t["yT"].tensor, slot * NH * 512,
                                  [[NS * NH * 512, 128], [1, NH * 512]]),
                            in_=yT[:, :]), inc="d_dump", n=16)

                    m_add = {}
                    m_o1b = {}
                    m_xr4 = {}
                    for dk in range(DK):
                        buf = dk % 2
                        NXR = min(4, (NH * 256) // 1024)
                        HWID = NXR * 512
                        g4 = dk // NXR
                        if dk % NXR == 0:
                            xb4 = g4 % 2
                            ctx.op("sync", lambda e, g4=g4, xb4=xb4, slot=slot: e.dma_start(
                                out=A(xr4v, xb4 * HWID, [[NH * 256, 128], [512, NXR], [1, 512]]),
                                in_=A(dram["xqT"].tensor, g4 * NXR * 128 * OWN + slot * 512,
                                      [[OWN, 128], [128 * OWN, NXR], [1, 512]])),
                                waits=[("vm", max(mk["xres_v"][xb4], mk["yev"])),
                                       ("tm", mk["avp"] if g4 < 2 else 0),
                                       (f"d_xr{xb4}", ctx.c(f"d_xr{xb4}"))],
                                inc=f"d_xr{xb4}", n=16)
                            m_xr4[g4] = ctx.c(f"d_xr{xb4}")
                        ctx.op("sync" if buf == 0 else "gp", lambda e, dk=dk, buf=buf: e.dma_start(
                            out=A(wqb[buf], 0, [[DK * 128, 128], [1, DK * 128]]),
                            in_=A(scr_wo2.tensor, dk * 128 * D, [[D, 128], [1, D]])),
                            waits=[("tm", mk["wq_t"][buf])], inc=f"d_wq{buf}", n=16)
                        m_wo = ctx.c(f"d_wq{buf}")
                        m_xr = m_xr4[g4]
                        pO = pY[:, buf * 512:buf * 512 + 512]
                        for yf in range(NH):
                            w = []
                            if yf == 0:
                                w = [(f"d_wq{buf}", m_wo),
                                     ("vm", m_yT if dk < 2 else m_add[dk - 2])]
                            ctx.op("tensor", lambda e, dk=dk, yf=yf, pO=pO, buf=buf: e.matmul(
                                pO, wqb[buf][:, yf * 128:(yf + 1) * 128],
                                yT[:, yf * 512:(yf + 1) * 512],
                                start=(yf == 0), stop=(yf == NH - 1)), waits=w,
                                inc=("tm" if yf == NH - 1 else None))
                        mk["wq_t"][buf] = ctx.c("tm")
                        m_o1 = ctx.c("tm")
                        ctx.op("vector", lambda e, buf=buf, pO=pO, dk=dk, g4=g4: e.tensor_add(
                            o1f[buf][:, :], pO,
                            xr4v[:, (g4 % 2) * HWID + (dk % NXR) * 512:(g4 % 2) * HWID + (dk % NXR) * 512 + 512]),
                            waits=[("tm", m_o1), (f"d_xr{g4 % 2}", m_xr),
                                   (f"d_sp{buf}", ctx.c(f"d_sp{buf}")),
                                   ("sm", m_o1b.get(dk - 2, 0))])
                        mk["xres_v"][g4 % 2] = ctx.c("vm")
                        m_add[dk] = ctx.c("vm")
                        ctx.op("sync", lambda e, buf=buf, dk=dk, slot=slot: e.dma_start(
                            out=A(scr_out1.tensor, dk * 128 * OWN + slot * 512,
                                  [[OWN, 128], [1, 512]]),
                            in_=o1f[buf][:, :]),
                            waits=[("vm", m_add[dk])], inc=f"d_sp{buf}", n=16)
                        ctx.op("scalar", lambda e, buf=buf, dk=dk: e.activation(
                            xbq[:, dk * 512:(dk + 1) * 512], o1f[buf][:, :], AF.Copy),
                            waits=[("vm", m_add[dk])])
                        m_o1b[dk] = ctx.c("sm")
                    xsqw = lnt2.bitcast(BF16)
                    nb = 2
                    SQW = nb * 512
                    m_prb2 = 0
                    for hb in range(DK // nb):
                        ctx.op("scalar", lambda e, hb=hb, SQW=SQW: e.activation(
                            xsqw[:, 0:SQW], xbq[:, hb * SQW:(hb + 1) * SQW], AF.Square),
                            waits=[("sm", m_o1b[min(DK - 1, (hb + 1) * nb - 1)]), ("tm", m_prb2)])
                        m_sq2 = ctx.c("sm")
                        for j in range(nb):
                            dk = hb * nb + j
                            ctx.op("tensor", lambda e, dk=dk: e.matmul(
                                pR[0:1, 0:512], onescol, xbq[:, dk * 512:(dk + 1) * 512],
                                start=(dk == 0), stop=(dk == DK - 1)),
                                waits=[("sm", m_o1b[dk])])
                            ctx.op("tensor", lambda e, j=j, dk=dk: e.matmul(
                                pR[0:1, 512:1024], onescol, xsqw[:, j * 512:(j + 1) * 512],
                                start=(dk == 0), stop=(dk == DK - 1)),
                                waits=([("sm", m_sq2)] if j == 0 else []),
                                inc=("tm" if j == nb - 1 else None))
                        m_prb2 = ctx.c("tm")
                    m_stats2 = ctx.c("tm")
                    mk["o1_t"] = m_stats2
                    m_rows3 = ln_rows2(ctx, pR[0:1, 0:512], pR[0:1, 512:1024], m_stats2, 0)
                    m_bc3 = bcastF(ctx, pS0[:, 0:512], row_r[0:1, :], [("vm", m_rows3)])
                    m_bc4 = bcastF(ctx, pS0[:, 512:1024], row_mr[0:1, :], [("vm", m_rows3)])
                    NQB2 = DK
                    HB2 = 1
                    for hb in range(NQB2):
                        ctx.op("vector", lambda e, hb=hb, HB2=HB2: e.tensor_tensor(
                            A(lnt2, 0, [[HB2 * 512, 128], [512, HB2], [1, 512]]),
                            A(xbq, hb * HB2 * 512, [[DK * 512, 128], [512, HB2], [1, 512]]),
                            A(pS0, 0, [[1024, 128], [0, HB2], [1, 512]]), ALU.mult),
                            waits=([("tm", m_bc3), ("d_l20", ctx.c("d_l20"))] if hb == 0 else []))
                        ctx.op("vector", lambda e, hb=hb, HB2=HB2: e.tensor_tensor(
                            A(ln1q, hb * HB2 * 512, [[DK * 512, 128], [512, HB2], [1, 512]]),
                            A(lnt2, 0, [[HB2 * 512, 128], [512, HB2], [1, 512]]),
                            A(pS0, 512, [[1024, 128], [0, HB2], [1, 512]]), ALU.subtract),
                            waits=([("tm", m_bc4)] if hb == 0 else []))
                    m_l2 = ctx.c("vm")
                    ctx.op("sync", lambda e, slot=slot: e.dma_start(
                        out=A(scr_ln2.tensor, slot * 512,
                              [[OWN, 128], [128 * OWN, DK], [1, 512]]),
                        in_=A(ln1q, 0, [[DK * 512, 128], [512, DK], [1, 512]])),
                        waits=[("vm", m_l2)], inc="d_l20", n=16)
                    mk["q_v"] = ctx.c("vm")
                    mk["xbq_v"][sb2] = ctx.c("vm")
                    mk["xbq_t"][sb2] = m_stats2
                    mk["pRfree"] = m_rows3
                    mk["pSfree_s"] = ctx.c("sm")
                    mk["pSfree_v"] = ctx.c("vm")

                ctx.wait("sync", [("d_sp0", ctx.c("d_sp0")), ("d_sp1", ctx.c("d_sp1")),
                                  ("d_l20", ctx.c("d_l20"))])
                if ctx.c("d_dump"):
                    ctx.wait("sync", [("d_dump", ctx.c("d_dump"))])

            run_phase(nc, ctx, ph_slots)

        akv_es.close()

        with ExitStack() as f1_es:
            def fsb(name, shape, dt=F32):
                return f1_es.enter_context(nc.sbuf_tensor(name, shape, dt))

            def fps(name, shape):
                return f1_es.enter_context(nc.psum_tensor(name, shape, F32))

            ln2T = fsb("ln2T", [128, DK * OWN], BF16)
            wgb0 = fsb("wgb0", [128, DK * 128], BF16)
            wgb1 = fsb("wgb1", [128, DK * 128], BF16)
            wub0 = fsb("wub0", [128, DK * 128], BF16)
            wub1 = fsb("wub1", [128, DK * 128], BF16)
            silo = fsb("silo", [128, NGRP * 512])
            gst0 = fsb("gst0", [128, NGRP * 512], BF16)
            gst1 = fsb("gst1", [128, NGRP * 512], BF16)
            wds0 = fsb("wds0", [128, D], BF16)
            wds1 = fsb("wds1", [128, D], BF16)
            pG = fps("pG", [128, NGRP * 512])
            pU = fps("pU", [128, NGRP * 512])
            wgb = [wgb0, wgb1]
            wub = [wub0, wub1]
            gst = [gst0, gst1]
            wds = [wds0, wds1]

            def ph_f1(ctx):
                ctx.barrier()
                ctx.op("sync", lambda e: e.dma_start(
                    out=A(ln2T, 0, [[DK * OWN, 128], [OWN, DK], [1, OWN]]),
                    in_=A(scr_ln2.tensor, 0, [[OWN, 128], [128 * OWN, DK], [1, OWN]])),
                    inc="d_ln2ld", n=16)
                m_ln2 = ctx.c("d_ln2ld")
                mk = {"w_t": [0, 0], "silu_s": 0, "gated_v": 0, "g_v": [0, 0],
                      "wd_v": [0, 0]}
                for hc in range(HC):
                    buf = hc % 2
                    ctx.op("gp", lambda e, hc=hc, buf=buf: e.dma_start(
                        out=A(wgb[buf], 0, [[DK * 128, 128], [128, DK], [1, 128]]),
                        in_=A(dram["wgT"].tensor, hc * 128,
                              [[DH, 128], [128 * DH, DK], [1, 128]])),
                        waits=[("tm", mk["w_t"][buf])], inc=f"d_fw{buf}", n=16)
                    ctx.op("gp", lambda e, hc=hc, buf=buf: e.dma_start(
                        out=A(wub[buf], 0, [[DK * 128, 128], [128, DK], [1, 128]]),
                        in_=A(dram["wuT"].tensor, hc * 128,
                              [[DH, 128], [128 * DH, DK], [1, 128]])),
                        inc=f"d_fw{buf}", n=16)
                    m_w = ctx.c(f"d_fw{buf}")
                    ctx.op("gp", lambda e, hc=hc, buf=buf: e.dma_start(
                        out=wds[buf][:, :],
                        in_=A(dram["wdT"].tensor, hc * 128 * D, [[D, 128], [1, D]])),
                        waits=[(f"d_ws{buf}", mk["wd_v"][buf])], inc=f"d_wp{buf}", n=16)
                    m_wd = ctx.c(f"d_wp{buf}")
                    ctx.op("sync", lambda e, hc=hc, buf=buf: e.dma_start(
                        out=A(scr_wd.tensor, hc * 128 * D, [[D, 128], [1, D]]),
                        in_=wds[buf][:, :]),
                        waits=[(f"d_wp{buf}", m_wd)], inc=f"d_ws{buf}", n=16)
                    mk["wd_v"][buf] = ctx.c(f"d_ws{buf}")
                    for dk in range(DK):
                        w = []
                        if dk == 0:
                            w = [(f"d_fw{buf}", m_w), ("d_ln2ld", m_ln2),
                                 ("sm", mk["silu_s"])]
                        for gi in range(NGRP):
                            ctx.op("tensor", lambda e, dk=dk, gi=gi, buf=buf: e.matmul(
                                pG[:, gi * 512:(gi + 1) * 512],
                                wgb[buf][:, dk * 128:(dk + 1) * 128],
                                ln2T[:, dk * OWN + gi * 512: dk * OWN + (gi + 1) * 512],
                                start=(dk == 0), stop=(dk == DK - 1)),
                                waits=(w if gi == 0 else []),
                                inc=("tm" if (dk == DK - 1 and gi == NGRP - 1) else None))
                    m_g = ctx.c("tm")
                    for dk in range(DK):
                        w = []
                        if dk == 0:
                            w = [(f"d_fw{buf}", m_w), ("vm", mk["gated_v"])]
                        for gi in range(NGRP):
                            ctx.op("tensor", lambda e, dk=dk, gi=gi, buf=buf: e.matmul(
                                pU[:, gi * 512:(gi + 1) * 512],
                                wub[buf][:, dk * 128:(dk + 1) * 128],
                                ln2T[:, dk * OWN + gi * 512: dk * OWN + (gi + 1) * 512],
                                start=(dk == 0), stop=(dk == DK - 1)),
                                waits=(w if gi == 0 else []),
                                inc=("tm" if (dk == DK - 1 and gi == NGRP - 1) else None))
                    mk["w_t"][buf] = ctx.c("tm")
                    m_u = ctx.c("tm")
                    ctx.op("scalar", lambda e: e.activation(silo[:, :], pG[:, :], AF.Silu),
                           waits=[("tm", m_g), ("vm", mk["gated_v"])])
                    mk["silu_s"] = ctx.c("sm")
                    ctx.op("vector", lambda e, buf=buf: e.tensor_mul(gst[buf][:, :], silo[:, :], pU[:, :]),
                           waits=[("tm", m_u), ("sm", mk["silu_s"]),
                                  (f"d_gs{buf}", mk["g_v"][buf])])
                    mk["gated_v"] = ctx.c("vm")
                    m_gv = ctx.c("vm")
                    ctx.op("sync", lambda e, hc=hc, buf=buf: e.dma_start(
                        out=A(scr_gated.tensor, hc * 128 * OWN, [[OWN, 128], [1, OWN]]),
                        in_=gst[buf][:, :]),
                        waits=[("vm", m_gv)], inc=f"d_gs{buf}", n=16)
                    mk["g_v"][buf] = ctx.c(f"d_gs{buf}")
                ctx.wait("sync", [("d_gs0", ctx.c("d_gs0")), ("d_gs1", ctx.c("d_gs1")),
                                  ("d_ws0", ctx.c("d_ws0")), ("d_ws1", ctx.c("d_ws1"))])

            run_phase(nc, ctx, ph_f1)

        with ExitStack() as f2_es:
            def gsb(name, shape, dt=F32):
                return f2_es.enter_context(nc.sbuf_tensor(name, shape, dt))

            gld0 = gsb("gld0", [128, HGS * 512], BF16)
            gld1 = gsb("gld1", [128, HGS * 512], BF16)
            gld2 = gsb("gld2", [128, HGS * 512], BF16)
            wdl0 = gsb("wdl0", [128, HGS * DSETW * 128], BF16)
            wdl1 = gsb("wdl1", [128, HGS * DSETW * 128], BF16)
            wdl2 = gsb("wdl2", [128, HGS * DSETW * 128], BF16)
            o1t0 = gsb("o1t0", [128, DSETW * 512])
            o1t1 = gsb("o1t1", [128, DSETW * 512])
            ost0 = gsb("ost0", [128, 512])
            ost1 = gsb("ost1", [128, 512])
            pDa = f2_es.enter_context(nc.psum_tensor("pDa", [128, DSETW * 512], F32))
            pDb = (f2_es.enter_context(nc.psum_tensor("pDb", [128, DSETW * 512], F32))
                   if F2ROT else pDa)
            gld = [gld0, gld1, gld2]
            wdl = [wdl0, wdl1, wdl2]
            o1t = [o1t0, o1t1]
            ost = [ost0, ost1]
            pDx = [pDa, pDb]
            DSW = DSETW * 128

            def ph_f2(ctx):
                ctx.barrier()
                mk = {"mm_t": [0, 0, 0], "ost_v": [0, 0], "ep_v": [0, 0], "o1_v": [0, 0]}
                units = [(gi, dset) for gi in range(NGRP) for dset in range(NDSET)]
                sched = [(u, hg) for u in range(len(units)) for hg in range(NHG)]

                def o1_load(u):
                    gi, dset = units[u]
                    ub = u % 2
                    ctx.op("gp", lambda e, gi=gi, dset=dset, ub=ub: e.dma_start(
                        out=A(o1t[ub], 0, [[DSETW * 512, 128], [512, DSETW], [1, 512]]),
                        in_=A(scr_out1.tensor, dset * DSW * OWN + gi * 512,
                              [[OWN, 128], [128 * OWN, DSETW], [1, 512]])),
                        waits=[("vm", mk["ep_v"][ub if F2ROT else 0])], inc=f"d_o1{ub}", n=16)
                    return ctx.c(f"d_o1{ub}")

                def f2_load(idx):
                    u, hg = sched[idx]
                    gi, dset = units[u]
                    buf = idx % 3
                    ctx.op("sync", lambda e, gi=gi, hg=hg, buf=buf: e.dma_start(
                        out=A(gld[buf], 0, [[HGS * 512, 128], [512, HGS], [1, 512]]),
                        in_=A(scr_gated.tensor, hg * HGS * 128 * OWN + gi * 512,
                              [[OWN, 128], [128 * OWN, HGS], [1, 512]])),
                        waits=[("tm", mk["mm_t"][buf])], inc=f"d_gl{buf}", n=16)
                    half = HGS // 2
                    ctx.op("sync", lambda e, dset=dset, hg=hg, buf=buf, half=half: e.dma_start(
                        out=A(wdl[buf], 0, [[HGS * DSW, 128], [DSW, half], [1, DSW]]),
                        in_=A(scr_wd.tensor, hg * HGS * 128 * D + dset * DSW,
                              [[D, 128], [128 * D, half], [1, DSW]])),
                        inc=f"d_wl{buf}", n=16)
                    ctx.op("gp", lambda e, dset=dset, hg=hg, buf=buf, half=half: e.dma_start(
                        out=A(wdl[buf], half * DSW, [[HGS * DSW, 128], [DSW, HGS - half], [1, DSW]]),
                        in_=A(scr_wd.tensor, (hg * HGS + half) * 128 * D + dset * DSW,
                              [[D, 128], [128 * D, HGS - half], [1, DSW]])),
                        waits=[("tm", mk["mm_t"][buf])], inc=f"d_wm{buf}", n=16)
                    return ctx.c(f"d_gl{buf}"), ctx.c(f"d_wl{buf}"), ctx.c(f"d_wm{buf}")

                fm = {0: f2_load(0), 1: f2_load(1)}
                mo1 = {0: o1_load(0)}
                for idx, (u, hg) in enumerate(sched):
                    gi, dset = units[u]
                    ub = (u % 2) if F2ROT else 0
                    pD = pDx[ub]
                    buf = idx % 3
                    if hg == 0 and u + 1 < len(units):
                        mo1[u + 1] = o1_load(u + 1)
                    if idx + 2 < len(sched):
                        fm[idx + 2] = f2_load(idx + 2)
                    m_g, m_wdl, m_wm = fm[idx]
                    for hci in range(HGS):
                        w = []
                        if hci == 0:
                            w = [(f"d_gl{buf}", m_g), (f"d_wl{buf}", m_wdl),
                                 (f"d_wm{buf}", m_wm)]
                            if hg == 0:
                                w.append(("vm", mk["ep_v"][ub]))
                        for dki in range(DSETW):
                            ctx.op("tensor", lambda e, hci=hci, dki=dki, buf=buf,
                                   pD=pD: e.matmul(
                                pD[:, dki * 512:(dki + 1) * 512],
                                wdl[buf][:, hci * DSW + dki * 128: hci * DSW + dki * 128 + 128],
                                gld[buf][:, hci * 512:(hci + 1) * 512],
                                start=(hg == 0 and hci == 0),
                                stop=(hg == NHG - 1 and hci == HGS - 1)),
                                waits=(w if dki == 0 else []),
                                inc=("tm" if (hci == HGS - 1 and dki == DSETW - 1) else None))
                    mk["mm_t"][buf] = ctx.c("tm")
                    if hg == NHG - 1:
                        m_mm = ctx.c("tm")
                        ob = u % 2
                        for dki in range(DSETW):
                            obuf = dki % 2
                            ctx.op("vector", lambda e, dki=dki, obuf=obuf, pD=pD, ob=ob: e.tensor_add(
                                ost[obuf][:, :], pD[:, dki * 512:(dki + 1) * 512],
                                o1t[ob][:, dki * 512:(dki + 1) * 512]),
                                waits=[("tm", m_mm), (f"d_o1{ob}", mo1[u]),
                                       (f"d_ou{obuf}", mk["ost_v"][obuf])])
                            m_o = ctx.c("vm")
                            ctx.op("sync", lambda e, dki=dki, dset=dset, gi=gi, obuf=obuf: e.dma_start(
                                out=A(out.tensor, (dset * DSETW + dki) * 128 * OWN + gi * 512,
                                      [[OWN, 128], [1, 512]]),
                                in_=ost[obuf][:, :]),
                                waits=[("vm", m_o)], inc=f"d_ou{obuf}", n=16)
                            mk["ost_v"][obuf] = ctx.c(f"d_ou{obuf}")
                        mk["ep_v"][ub] = ctx.c("vm")
                ctx.wait("sync", [("d_ou0", ctx.c("d_ou0")), ("d_ou1", ctx.c("d_ou1"))])
                ctx.barrier()

            run_phase(nc, ctx, ph_f2)

    return nc



def kernel(L0_attn_x, L0_attn_wq, L0_attn_wk, L0_attn_wv, L0_attn_wo,
           L0_ffn_w_gate, L0_ffn_w_up, L0_ffn_w_down):
    cfg = CFG
    maps = host_inputs(cfg, L0_attn_x, L0_attn_wq, L0_attn_wk, L0_attn_wv,
                       L0_attn_wo, L0_ffn_w_gate, L0_ffn_w_up, L0_ffn_w_down)
    nc = build_program(cfg)
    res = run_bass_kernel_spmd(nc, maps, core_ids=list(range(cfg["NCORE"])))
    outp = np.empty((cfg["B"], cfg["T"], cfg["D"]), np.float32)
    for c in range(cfg["NCORE"]):
        b, ct = c // 2, c % 2
        _, perm_q, _ = core_perms(cfg, ct)
        outp[b, perm_q, :] = np.asarray(res.results[c]["outT"]).T
    return outp


# version 23
# speedup vs baseline: 1.1050x; 1.1050x over previous
import numpy as np
import ml_dtypes
from contextlib import ExitStack

import concourse.bass as bass
import concourse.mybir as mybir
from concourse.bass_utils import run_bass_kernel_spmd

F32 = mybir.dt.float32
BF16 = mybir.dt.bfloat16
AF = mybir.ActivationFunctionType
ALU = mybir.AluOpType
BF = ml_dtypes.bfloat16

MASK = -10000.0
EPS_LN = 1e-5
EPS_RMS = 1e-6
ROPE_BASE = 100000.0


def make_cfg(B=4, NSLOT=4, D=2048, NH=16, NKV=4, DH=8192):
    cfg = dict(B=B, NSLOT=NSLOT, D=D, NH=NH, NKV=NKV, DH=DH)
    cfg["T"] = 1024 * NSLOT
    cfg["OWN"] = 512 * NSLOT
    cfg["KEXT"] = cfg["T"] + 512 * NSLOT
    cfg["DK"] = D // 128
    cfg["HC"] = DH // 128
    cfg["GRP"] = NH // NKV
    cfg["ITERS"] = [8 * j + 8 for j in range(NSLOT)]
    cfg["NIT"] = sum(cfg["ITERS"])
    cfg["NGK"] = cfg["KEXT"] // 512
    cfg["NCORE"] = 2 * B
    return cfg


CFG = make_cfg()
SCALE = float(np.float32(1.0) / np.sqrt(np.float32(128.0)))



def core_perms(cfg, ct):
    T, NS = cfg["T"], cfg["NSLOT"]
    groups = [2 * j + ct for j in range(NS)]
    perm_q = np.concatenate([np.arange(512 * g, 512 * g + 512) for g in groups])
    dup = np.concatenate([np.arange(1024 * j + 512 * ct, 1024 * j + 512 * ct + 512)
                          for j in range(NS)])
    perm_kv = np.concatenate([np.arange(T), dup])
    return groups, perm_q, perm_kv


def host_inputs(cfg, x, wq, wk, wv, wo, wg, wu, wd):
    B, T, NS, NIT = cfg["B"], cfg["T"], cfg["NSLOT"], cfg["NIT"]

    shared = {
        "wq": np.ascontiguousarray(wq, np.float32),
        "wk": np.ascontiguousarray(wk, np.float32),
        "wv": np.ascontiguousarray(wv, np.float32),
        "wo": np.ascontiguousarray(wo, np.float32),
        "wgT": np.ascontiguousarray(np.asarray(wg, np.float32).T),
        "wuT": np.ascontiguousarray(np.asarray(wu, np.float32).T),
        "wdT": np.ascontiguousarray(np.asarray(wd, np.float32).T),
    }
    selbuf = np.zeros((128, 32), BF)
    selbuf[:, 16] = 1.0
    onesrow = np.ones((1, 128), BF)
    tri = np.zeros((128, 128), BF)
    for i in range(128):
        tri[i, :i] = MASK
    shared["selbuf"] = selbuf
    shared["onesrow"] = onesrow
    shared["onesrf"] = np.ones((1, 128), np.float32)
    shared["tri"] = tri
    consts = np.array([EPS_LN, EPS_RMS, 1.0 / 128.0, -0.5, SCALE,
                       float(np.log(np.float32(SCALE))), 1.0, -1.0],
                      np.float32)
    shared["consts"] = np.tile(consts[None, :], (128, 1))

    half = 64
    inv = (1.0 / (np.float32(ROPE_BASE) **
                  (np.arange(half, dtype=np.float32) / np.float32(half)))).astype(np.float32)
    pos = np.arange(T, dtype=np.float32)
    freqs = pos[:, None] * inv[None, :]
    cosT = np.cos(freqs).astype(np.float32)
    sinT = np.sin(freqs).astype(np.float32)

    maps = []
    for c in range(2 * B):
        b, ct = c // 2, c % 2
        groups, perm_q, perm_kv = core_perms(cfg, ct)
        m = dict(shared)
        xb = np.asarray(x[b], np.float32)
        m["xqT"] = np.ascontiguousarray(xb[perm_q].T)
        m["xkvT"] = np.ascontiguousarray(xb[perm_kv].T)
        m["cosq"] = np.ascontiguousarray(cosT[perm_q].T)
        m["sinq"] = np.ascontiguousarray(sinT[perm_q].T)
        m["cosk"] = np.ascontiguousarray(cosT[perm_kv].T)
        m["sink"] = np.ascontiguousarray(sinT[perm_kv].T)
        kb = np.zeros((128, NIT), BF)
        u = 0
        for j in range(NS):
            iters = cfg["ITERS"][j]
            plains = iters - 4
            for it in range(iters):
                if ct == 0 and plains - 4 <= it < plains:
                    kb[:, u] = MASK
                u += 1
        m["kbias"] = kb
        maps.append(m)
    return maps



class Ctx:

    def __init__(self, nc):
        self.nc = nc
        self.sems = {}
        self.cnt = {}
        self.cur = None
        self.eng = None

    def mksem(self, es, *names):
        for n in names:
            self.sems[n] = es.enter_context(self.nc.semaphore(n))
            self.cnt[n] = 0

    def c(self, name):
        return self.cnt[name]

    def op(self, engine, fn=None, waits=(), inc=None, n=1, guard=True):
        waits = [(s, v) for (s, v) in waits if v > 0]
        if engine in ("vector", "scalar") and fn is not None:
            gsem = "vm" if engine == "vector" else "sm"
            if guard and self.cnt[gsem] > 0:
                waits.append((gsem, self.cnt[gsem]))
            if inc is None:
                inc = gsem
        if self.cur == engine:
            for s, v in waits:
                self.eng.wait_ge(self.sems[s], v)
            if fn is not None:
                instr = fn(self.eng)
                if inc is not None:
                    instr.then_inc(self.sems[inc], n)
        if fn is not None and inc is not None:
            self.cnt[inc] += n

    def wait(self, engine, waits):
        self.op(engine, None, waits=waits)

    def barrier(self):
        marks = [(s, self.cnt[s]) for s in ("tm", "sm", "vm") if self.cnt[s] > 0]
        for e in ("tensor", "scalar", "vector", "sync", "gp"):
            self.op(e, None, waits=marks)


def run_phase(nc, ctx, fn):
    snap = dict(ctx.cnt)
    with nc.Block() as blk:
        def mk(name):
            def body(eng, name=name):
                ctx.cnt = dict(snap)
                ctx.cur, ctx.eng = name, eng
                fn(ctx)
            return body
        blk.tensor(mk("tensor"))
        blk.scalar(mk("scalar"))
        blk.vector(mk("vector"))
        blk.sync(mk("sync"))
        blk.gpsimd(mk("gp"))
    ctx.cnt = dict(snap)
    ctx.cur, ctx.eng = None, None
    fn(ctx)


def A(t, off, dims):
    return bass.AP(t, off, [list(d) for d in dims])



def build_program(cfg, dump=()):
    B, T, D, NH, NKV, DH = (cfg[k] for k in ("B", "T", "D", "NH", "NKV", "DH"))
    NS, OWN, KEXT, DK, HC = (cfg[k] for k in ("NSLOT", "OWN", "KEXT", "DK", "HC"))
    GRP, ITERS, NIT, NGK = cfg["GRP"], cfg["ITERS"], cfg["NIT"], cfg["NGK"]
    NPAIR = GRP // 2
    QD = NH * 128
    KD = NKV * 128
    KCH = KEXT // 128
    TCH = T // 128
    NGRP = OWN // 512
    DSETW = min(8, DK)
    NDSET = DK // DSETW
    F2ROT = DSETW <= 4
    HGS = min(8, HC)
    NHG = HC // HGS

    nc = bass.Bass("TRN2", target_bir_lowering=False)
    dram = {}
    for name, shape, dt in [
        ("xqT", [D, OWN], F32), ("xkvT", [D, KEXT], F32),
        ("wq", [D, QD], F32), ("wk", [D, KD], F32), ("wv", [D, KD], F32),
        ("wo", [D, D], F32),
        ("wgT", [D, DH], F32), ("wuT", [D, DH], F32), ("wdT", [DH, D], F32),
        ("cosq", [64, OWN], F32), ("sinq", [64, OWN], F32),
        ("cosk", [64, KEXT], F32), ("sink", [64, KEXT], F32),
        ("kbias", [128, NIT], BF16), ("tri", [128, 128], BF16),
        ("selbuf", [128, 32], BF16), ("onesrow", [1, 128], BF16),
        ("onesrf", [1, 128], F32),
        ("consts", [128, 8], F32),
    ]:
        dram[name] = nc.dram_tensor(name, shape, dt, kind="ExternalInput").ap()
    out = nc.dram_tensor("outT", [D, OWN], F32, kind="ExternalOutput").ap()
    scr_out1 = nc.dram_tensor("scr_out1", [D, OWN], F32,
                              kind="ExternalOutput" if "out1" in dump else "Internal").ap()
    scr_ln2 = nc.dram_tensor("scr_ln2", [D, OWN], BF16,
                             kind="ExternalOutput" if "ln2" in dump else "Internal").ap()
    scr_gated = nc.dram_tensor("scr_gated", [DH, OWN], BF16,
                               kind="ExternalOutput" if "gated" in dump else "Internal").ap()
    scr_wd = nc.dram_tensor("scr_wd", [DH, D], BF16, kind="Internal").ap()
    scr_wq2 = nc.dram_tensor("scr_wq2", [QD // 128, 128, DK * 128], BF16, kind="Internal").ap()
    scr_wo2 = nc.dram_tensor("scr_wo2", [D // 128, 128, DK * 128], BF16, kind="Internal").ap()
    dump_t = {}
    if "pad" in dump:
        for pi in range(2):
            nc.dram_tensor(f"padscr{pi}", [1024, 1024], F32, kind="Internal")
    if "kv" in dump:
        dump_t["kT"] = nc.dram_tensor("dump_kT", [128, NKV * KEXT], BF16, kind="ExternalOutput").ap()
        dump_t["V"] = nc.dram_tensor("dump_V", [128, KCH * KD], BF16, kind="ExternalOutput").ap()
    if "qy" in dump:
        dump_t["qT"] = nc.dram_tensor("dump_qT", [128, NS * NH * 512], BF16, kind="ExternalOutput").ap()
        dump_t["yT"] = nc.dram_tensor("dump_yT", [128, NS * NH * 512], BF16, kind="ExternalOutput").ap()

    es = ExitStack()
    ctx = Ctx(nc)
    ctx.mksem(es, "tm", "sm", "vm", "gm",
              "d_const", "d_xk0", "d_xk1", "d_wkv", "d_cs", "d_cs0", "d_cs1", "d_xq0", "d_xq1",
              "d_wq0", "d_wq1", "d_xr0", "d_xr1", "d_sp0", "d_sp1", "d_wp2a", "d_wp2b", "d_ws2a", "d_ws2b",
              "d_l20", "d_l21", "d_fw0", "d_fw1", "d_wp0", "d_wp1",
              "d_ws0", "d_ws1", "d_ln2ld", "d_gs0", "d_gs1",
              "d_gl0", "d_gl1", "d_gl2", "d_wl0", "d_wl1", "d_wl2", "d_wm0", "d_wm1", "d_wm2", "d_o1ld", "d_o10", "d_o11",
              "d_ou0", "d_ou1", "d_dump")

    def sb(name, shape, dt=F32):
        return es.enter_context(nc.sbuf_tensor(name, shape, dt))

    with es:
        selb = sb("selb", [128, 32], BF16)
        onesrf = sb("onesrf_s", [1, 128])
        trib = sb("trib", [128, 128], BF16)
        kbb = sb("kbb", [128, NIT], BF16)
        cst = sb("cst", [128, 8])

        def CC(i, p=128):
            return cst[0:p, i:i + 1]

        onescol = selb[:, 16:17]

        akv_es = ExitStack()

        def asb0(name, shape, dt=F32):
            return akv_es.enter_context(nc.sbuf_tensor(name, shape, dt))

        kT = asb0("kT", [128, NKV * KEXT], BF16)
        V = asb0("V", [128, KCH * KD], BF16)
        row_m = asb0("row_m", [1, 512])
        row_t = asb0("row_t", [1, 512])
        row_r = asb0("row_r", [1, 512])
        row_mr = asb0("row_mr", [1, 512])
        rsc = asb0("rsc", [128, NKV * KCH])

        def ln_rows2(ctx, psA_ap, psB_ap, m_stats, eps_idx):
            class _W:
                def __init__(self, ap):
                    self.ap = ap

                def __getitem__(self, k):
                    return self.ap
            return ln_rows(ctx, _W(psA_ap), _W(psB_ap), m_stats, eps_idx)

        def ln_rows(ctx, psA, psB, m_stats, eps_idx):
            ctx.op("vector", lambda e: e.tensor_scalar_mul(row_m[:, :], psA[0:1, :], 1.0 / D),
                   waits=[("tm", m_stats)])
            ctx.op("vector", lambda e: e.tensor_mul(row_t[:, :], row_m[:, :], row_m[:, :]))
            ctx.op("vector", lambda e: e.scalar_tensor_tensor(
                row_t[:, :], psB[0:1, :], 1.0 / D, row_t[:, :], ALU.mult, ALU.subtract))
            m_var = ctx.c("vm")
            ctx.op("scalar", lambda e: e.activation(row_t[:, :], row_t[:, :], AF.Ln,
                                                    bias=CC(eps_idx, 1)),
                   waits=[("vm", m_var)])
            ctx.op("scalar", lambda e: e.activation(row_r[:, :], row_t[:, :], AF.Exp,
                                                    scale=CC(3, 1)))
            m_rstd = ctx.c("sm")
            ctx.op("vector", lambda e: e.tensor_mul(row_mr[:, :], row_m[:, :], row_r[:, :]),
                   waits=[("sm", m_rstd)])
            return ctx.c("vm")

        def bcastF(ctx, dst, row_ap, waits):
            ctx.op("tensor", lambda e: e.matmul(dst, onesrf[0:1, :], row_ap,
                                                start=True, stop=True), waits=waits, inc="tm")
            return ctx.c("tm")


        def ph_const(ctx):
            ctx.op("sync", lambda e: e.dma_start(out=selb[:, :], in_=dram["selbuf"][:, :]), inc="d_const", n=16)
            ctx.op("sync", lambda e: e.dma_start(out=onesrf[:, :], in_=dram["onesrf"][:, :]), inc="d_const", n=16)
            ctx.op("sync", lambda e: e.dma_start(out=trib[:, :], in_=dram["tri"][:, :]), inc="d_const", n=16)
            ctx.op("sync", lambda e: e.dma_start(out=kbb[:, :], in_=dram["kbias"][:, :]), inc="d_const", n=16)
            ctx.op("sync", lambda e: e.dma_start(out=cst[:, :], in_=dram["consts"][:, :]), inc="d_const", n=16)
            ctx.wait("sync", [("d_const", 80)])
            ctx.wait("tensor", [("d_const", 80)])
            ctx.wait("vector", [("d_const", 80)])
            ctx.wait("scalar", [("d_const", 80)])

        run_phase(nc, ctx, ph_const)

        with ExitStack() as kv_es:
            def ksb(name, shape, dt=F32):
                return kv_es.enter_context(nc.sbuf_tensor(name, shape, dt))

            def kps(name, shape):
                return kv_es.enter_context(nc.psum_tensor(name, shape, F32))

            wkb = ksb("wkb", [128, DK * KD], BF16)
            wvb = ksb("wvb", [128, DK * KD], BF16)
            xb0 = ksb("xb0", [128, DK * 512], BF16)
            xb1 = ksb("xb1", [128, DK * 512], BF16)
            xsq = ksb("xsq", [128, 1024], BF16)
            tmA = ksb("tmA", [64, 1024])
            tmB = ksb("tmB", [64, 1024])
            negm = ksb("negm", [1, 512], BF16)
            rtc = ksb("rtc", [128, 8])
            wpre0 = ksb("wpre0", [128, 2048], BF16)
            wpre1 = ksb("wpre1", [128, 2048], BF16)
            wpre = [wpre0, wpre1]
            kcolsb = ksb("kcolsb", [1, KD], BF16)
            vcolsb = ksb("vcolsb", [1, KD], BF16)
            csk0 = ksb("csk0", [64, 512])
            snk0 = ksb("snk0", [64, 512])
            csk1 = ksb("csk1", [64, 512])
            snk1 = ksb("snk1", [64, 512])
            rosq = ksb("rosq", [128, 1024], BF16)
            ps_a = kps("ps_a", [1, 512])
            ps_b = kps("ps_b", [1, 512])
            ps_rt = kps("ps_rt", [128, 8])
            ps_K = kps("ps_K", [128, 1024])
            ps_V = kps("ps_V", [128, max(KD, 512)])
            ps_rc = kps("ps_rc", [128, NKV * KCH])
            xbb = [xb0, xb1]
            csk = [csk0, csk1]
            snk = [snk0, snk1]
            NPR = NKV // 2 if NKV >= 2 else 1
            PRW = 2 if NKV >= 2 else 1

            def ph_kv(ctx):
                ctx.op("gp", lambda e: e.dma_start(
                    out=A(wkb, 0, [[DK * KD, 128], [KD, DK], [1, KD]]),
                    in_=A(dram["wk"].tensor, 0, [[KD, 128], [128 * KD, DK], [1, KD]])),
                    inc="d_wkv", n=16)
                ctx.op("gp", lambda e: e.dma_start(
                    out=A(wvb, 0, [[DK * KD, 128], [KD, DK], [1, KD]]),
                    in_=A(dram["wv"].tensor, 0, [[KD, 128], [128 * KD, DK], [1, KD]])),
                    inc="d_wkv", n=16)

                mk = {"stats": 0, "kvdone": 0, "ropeM": {},
                      "cs_v": [0, 0], "mxkv": [("d_xk0", 0), ("d_xk1", 0)], "mcs": [0, 0], "lastLn": 0,
                      "rows_t": 0, "negm_v": 0, "rt_s": 0, "rt_s2": 0, "rms_t": 0}

                CW = min(512, KD)
                for j in range(KD // CW):
                    for dk in range(DK):
                        ctx.op("tensor", lambda e, j=j, dk=dk: e.matmul(
                            ps_a[0:1, 0:CW], onescol, wkb[:, dk * KD + j * CW: dk * KD + (j + 1) * CW],
                            start=(dk == 0), stop=(dk == DK - 1)),
                            waits=([("d_wkv", 32)] if dk == 0 and j == 0 else []))
                        ctx.op("tensor", lambda e, j=j, dk=dk: e.matmul(
                            ps_b[0:1, 0:CW], onescol, wvb[:, dk * KD + j * CW: dk * KD + (j + 1) * CW],
                            start=(dk == 0), stop=(dk == DK - 1)),
                            waits=([("d_wkv", 32)] if dk == 0 and j == 0 else []),
                            inc=("tm" if dk == DK - 1 else None))
                    m_cs0 = ctx.c("tm")
                    ctx.op("scalar", lambda e, j=j: e.activation(
                        kcolsb[0:1, j * CW:(j + 1) * CW], ps_a[0:1, 0:CW], AF.Copy),
                        waits=[("tm", m_cs0)])
                    ctx.op("scalar", lambda e, j=j: e.activation(
                        vcolsb[0:1, j * CW:(j + 1) * CW], ps_b[0:1, 0:CW], AF.Copy))
                mk["colsum_s"] = ctx.c("sm")
                mk["lastLn"] = ctx.c("sm")

                def emit_prep_dma_only(g):
                    gb = g % 2
                    ctx.op("gp", lambda e, g=g, gb=gb: e.dma_start(
                        out=A(xbb[gb], 0, [[DK * 512, 128], [512, DK], [1, 512]]),
                        in_=A(dram["xkvT"].tensor, g * 512,
                              [[KEXT, 128], [128 * KEXT, DK], [1, 512]])),
                        waits=[("tm", mk["kvdone"])],
                        inc=f"d_xk{gb}", n=16)
                    mk["mxkv"][gb] = (f"d_xk{gb}", ctx.c(f"d_xk{gb}"))
                    cb = g % 2
                    ctx.op("sync", lambda e, g=g, cb=cb: e.dma_start(
                        out=csk[cb][:, :], in_=A(dram["cosk"].tensor, g * 512, [[KEXT, 64], [1, 512]])),
                        waits=[("vm", mk["cs_v"][cb])], inc=f"d_cs{cb}", n=16)
                    ctx.op("sync", lambda e, g=g, cb=cb: e.dma_start(
                        out=snk[cb][:, :], in_=A(dram["sink"].tensor, g * 512, [[KEXT, 64], [1, 512]])),
                        inc=f"d_cs{cb}", n=16)
                    mk["mcs"][cb] = ctx.c(f"d_cs{cb}")

                def emit_prep_compute(g):
                    gb = g % 2
                    xb = xbb[gb]
                    m_xkv = mk["mxkv"][gb]
                    for dk in range(DK):
                        ctx.op("tensor", lambda e, dk=dk, xb=xb: e.matmul(
                            ps_a[0:1, :], onescol, xb[:, dk * 512:(dk + 1) * 512],
                            start=(dk == 0), stop=(dk == DK - 1)),
                            waits=([m_xkv, ("vm", mk["rows_t"]),
                                    ("sm", mk["lastLn"])] if dk == 0 else []),
                            inc=("tm" if dk == DK - 1 else None))
                    m_suma = ctx.c("tm")
                    msq = {}
                    mmb = {-2: mk["stats"], -1: mk["stats"]}
                    for dk in range(DK):
                        ctx.op("scalar", lambda e, dk=dk, xb=xb: e.activation(
                            xsq[:, (dk % 2) * 512:(dk % 2) * 512 + 512],
                            xb[:, dk * 512:(dk + 1) * 512], AF.Square),
                            waits=([m_xkv] if dk == 0 else []) +
                                  [("tm", mmb[dk - 2])])
                        msq[dk] = ctx.c("sm")
                        ctx.op("tensor", lambda e, dk=dk: e.matmul(
                            ps_b[0:1, :], onescol, xsq[:, (dk % 2) * 512:(dk % 2) * 512 + 512],
                            start=(dk == 0), stop=(dk == DK - 1)),
                            waits=[("sm", msq[dk])], inc="tm")
                        mmb[dk] = ctx.c("tm")
                    mk["stats"] = ctx.c("tm")
                    ctx.op("vector", lambda e: e.tensor_scalar_mul(negm[:, :], ps_a[0:1, :], -1.0 / D),
                           waits=[("tm", m_suma), ("tm", mk["rank1_t"] if "rank1_t" in mk else 0)])
                    mk["negm_v"] = ctx.c("vm")
                    ctx.op("vector", lambda e: e.tensor_scalar_mul(row_m[:, :], ps_a[0:1, :], 1.0 / D))
                    ctx.op("vector", lambda e: e.tensor_mul(row_t[:, :], row_m[:, :], row_m[:, :]))
                    ctx.op("vector", lambda e: e.scalar_tensor_tensor(
                        row_t[:, :], ps_b[0:1, :], 1.0 / D, row_t[:, :], ALU.mult, ALU.subtract),
                        waits=[("tm", mk["stats"])])
                    m_var = ctx.c("vm")
                    ctx.op("scalar", lambda e: e.activation(row_t[:, :], row_t[:, :], AF.Ln,
                                                            bias=CC(0, 1)),
                           waits=[("vm", m_var)])
                    ctx.op("scalar", lambda e: e.activation(row_r[:, :], row_t[:, :], AF.Exp,
                                                            scale=CC(3, 1)))
                    mk["lastLn"] = ctx.c("sm")
                    m_rstd = ctx.c("sm")
                    for ts in range(4):
                        ctx.op("tensor", lambda e, ts=ts: e.matmul(
                            ps_rt[:, ts:ts + 1], row_r[0:1, ts * 128:(ts + 1) * 128],
                            cst[0:1, 6:7], start=True, stop=True),
                            waits=([("sm", m_rstd), ("sm", mk["rt_s"])] if ts == 0 else []),
                            inc=("tm" if ts == 3 else None))
                    m_rtt = ctx.c("tm")
                    ctx.op("scalar", lambda e: e.activation(rtc[:, 0:4], ps_rt[:, 0:4], AF.Copy),
                           waits=[("tm", m_rtt)])
                    mk["rt_s2"] = ctx.c("sm")
                    mk["rows_t"] = ctx.c("vm")
                    mk["rt_t"] = ctx.c("tm")

                def emit_kproj(g, p):
                    gb = g % 2
                    xb = xbb[gb]
                    m_xkv = mk["mxkv"][gb]
                    for pi in range(PRW):
                        kc = p * PRW + pi
                        for dk in range(DK):
                            w = []
                            if dk == 0:
                                w = [("d_wkv", 32), m_xkv,
                                     ("vm", mk["ropeM"].get(p - 1, 0))]
                            ctx.op("tensor", lambda e, kc=kc, pi=pi, dk=dk, xb=xb: e.matmul(
                                ps_K[:, pi * 512:(pi + 1) * 512],
                                wkb[:, dk * KD + kc * 128: dk * KD + (kc + 1) * 128],
                                xb[:, dk * 512:(dk + 1) * 512],
                                start=(dk == 0), stop=False), waits=w)
                        ctx.op("tensor", lambda e, kc=kc, pi=pi: e.matmul(
                            ps_K[:, pi * 512:(pi + 1) * 512],
                            kcolsb[0:1, kc * 128:(kc + 1) * 128], negm[0:1, :],
                            start=False, stop=True),
                            waits=[("vm", mk["negm_v"]), ("sm", mk["colsum_s"])],
                            inc=("tm" if pi == PRW - 1 else None))
                    return ctx.c("tm")

                def emit_vproj(g, ts, m_evprev):
                    gb = g % 2
                    xb = xbb[gb]
                    posv = g * 4 + ts
                    for dk in range(DK):
                        w = []
                        if dk == 0:
                            w = [("sm", m_evprev), ("d_wkv", 32)]
                        ctx.op("tensor", lambda e, ts=ts, dk=dk, xb=xb: e.matmul(
                            ps_V[:, 0:KD],
                            xb[:, dk * 512 + ts * 128: dk * 512 + ts * 128 + 128],
                            wvb[:, dk * KD:(dk + 1) * KD],
                            start=(dk == 0), stop=False), waits=w)
                    ctx.op("tensor", lambda e, ts=ts: e.matmul(
                        ps_V[:, 0:KD], negm[0:1, ts * 128:(ts + 1) * 128],
                        vcolsb[0:1, 0:KD],
                        start=False, stop=True),
                        waits=[("vm", mk["negm_v"]), ("sm", mk["colsum_s"]),
                               ("tm", mk["rt_t"])],
                        inc="tm")
                    m_V = ctx.c("tm")
                    ctx.op("scalar", lambda e, posv=posv, ts=ts: e.activation(
                        V[:, posv * KD:(posv + 1) * KD], ps_V[:, 0:KD], AF.Copy,
                        scale=rtc[:, ts:ts + 1]),
                        waits=[("tm", m_V), ("sm", mk["rt_s2"])])
                    mk["rt_s"] = mk["rt_s2"]
                    return ctx.c("sm")

                def rope_pair(g, p, m_K):
                    cb = g % 2
                    kv0 = p * PRW
                    x1 = A(ps_K, 0, [[1024, 64], [512, PRW], [1, 512]])
                    x2 = A(ps_K, 64 * 1024, [[1024, 64], [512, PRW], [1, 512]])
                    cc = A(csk[cb], 0, [[512, 64], [0, PRW], [1, 512]])
                    ss = A(snk[cb], 0, [[512, 64], [0, PRW], [1, 512]])
                    tA = A(tmA, 0, [[1024, 64], [512, PRW], [1, 512]])
                    tB = A(tmB, 0, [[1024, 64], [512, PRW], [1, 512]])
                    ktlo = A(kT, kv0 * KEXT + g * 512, [[KEXT * NKV, 64], [KEXT, PRW], [1, 512]])
                    kthi = A(kT, 64 * KEXT * NKV + kv0 * KEXT + g * 512,
                             [[KEXT * NKV, 64], [KEXT, PRW], [1, 512]])
                    ctx.op("vector", lambda e: e.tensor_tensor(tA, x1, cc, ALU.mult),
                           waits=[("tm", m_K), (f"d_cs{cb}", mk["mcs"][cb])])
                    ctx.op("vector", lambda e: e.tensor_tensor(tB, x2, ss, ALU.mult))
                    ctx.op("vector", lambda e: e.tensor_tensor(ktlo, tA, tB, ALU.add))
                    ctx.op("vector", lambda e: e.tensor_tensor(tA, x2, cc, ALU.mult))
                    ctx.op("vector", lambda e: e.tensor_tensor(tB, x1, ss, ALU.mult))
                    ctx.op("vector", lambda e: e.tensor_tensor(kthi, tA, tB, ALU.subtract))
                    mk["ropeM"][p] = ctx.c("vm")
                    mk["cs_v"][cb] = ctx.c("vm")
                    m_kt = ctx.c("vm")
                    ctx.op("scalar", lambda e, kv0=kv0, g=g: e.activation(
                        A(rosq, 0, [[1024, 128], [512, PRW], [1, 512]]),
                        A(kT, kv0 * KEXT + g * 512, [[KEXT * NKV, 128], [KEXT, PRW], [1, 512]]),
                        AF.Square),
                        waits=[("vm", m_kt), ("tm", mk["rms_t"])])
                    return ctx.c("sm")

                def rms_cols(g, p, m_rosq):
                    kv0 = p * PRW
                    for pi in range(PRW):
                        kv = kv0 + pi
                        for ts in range(4):
                            col = kv * KCH + g * 4 + ts
                            ctx.op("tensor", lambda e, pi=pi, ts=ts, col=col: e.matmul(
                                ps_rc[:, col:col + 1],
                                rosq[:, pi * 512 + ts * 128: pi * 512 + ts * 128 + 128],
                                onescol, start=True, stop=True),
                                waits=([("sm", m_rosq)] if (pi == 0 and ts == 0) else []),
                                inc=("tm" if (pi == PRW - 1 and ts == 3) else None))
                    mk["rms_t"] = ctx.c("tm")
                    return ctx.c("tm")

                def compute(g):
                    mk["ropeM"] = {}
                    mK = {}
                    rq = {}
                    mK[0] = emit_kproj(g, 0)
                    m_ev = ctx.c("sm")
                    rq[0] = rope_pair(g, 0, mK[0])
                    vts = 0
                    for p in range(NPR):
                        for _ in range(4 // NPR):
                            m_ev = emit_vproj(g, vts, m_ev)
                            vts += 1
                        rms_cols(g, p, rq[p])
                        if p + 1 < NPR:
                            mK[p + 1] = emit_kproj(g, p + 1)
                            rq[p + 1] = rope_pair(g, p + 1, mK[p + 1])
                    while vts < 4:
                        m_ev = emit_vproj(g, vts, m_ev)
                        vts += 1
                    mk["rank1_t"] = ctx.c("tm")
                    mk["kvdone"] = ctx.c("tm")

                pre_jobs = ([("wq", scr_wq2, i) for i in range(DK)] +
                            [("wo", scr_wo2, i) for i in range(DK)])
                mk["pre_i"] = 0

                def emit_precast(n):
                    for _ in range(n):
                        i = mk["pre_i"]
                        if i >= len(pre_jobs):
                            return
                        mk["pre_i"] += 1
                        name, scr, dk = pre_jobs[i]
                        pb = i % 2
                        ps2 = "ab"[pb]
                        W = QD if name == "wq" else D
                        ctx.op("gp", lambda e, name=name, dk=dk, pb=pb, W=W: e.dma_start(
                            out=wpre[pb][:, 0:W],
                            in_=A(dram[name].tensor, dk * 128 * W, [[W, 128], [1, W]])),
                            waits=[(f"d_ws2{ps2}", ctx.c(f"d_ws2{ps2}"))],
                            inc=f"d_wp2{ps2}", n=16)
                        m_l = ctx.c(f"d_wp2{ps2}")
                        ctx.op("sync", lambda e, scr=scr, dk=dk, pb=pb, W=W: e.dma_start(
                            out=A(scr.tensor, dk * 128, [[D, 128], [128 * D, W // 128], [1, 128]]),
                            in_=A(wpre[pb], 0, [[2048, 128], [128, W // 128], [1, 128]])),
                            waits=[(f"d_wp2{ps2}", m_l)], inc=f"d_ws2{ps2}", n=16)

                emit_prep_dma_only(0)
                emit_prep_compute(0)
                for g in range(NGK):
                    if g + 1 < NGK:
                        emit_prep_dma_only(g + 1)
                    emit_precast(3)
                    compute(g)
                    if g + 1 < NGK:
                        emit_prep_compute(g + 1)
                emit_precast(len(pre_jobs))
                ctx.wait("sync", [("d_ws2a", ctx.c("d_ws2a")), ("d_ws2b", ctx.c("d_ws2b"))])

                m_cols = ctx.c("tm")
                ctx.op("scalar", lambda e: e.activation(
                    A(rsc, 0, [[NKV * KCH, 128], [1, NKV * KCH]]),
                    ps_rc[:, :], AF.Ln, bias=CC(1), scale=CC(2)),
                    waits=[("tm", m_cols)])
                ctx.op("scalar", lambda e: e.activation(
                    A(rsc, 0, [[NKV * KCH, 128], [1, NKV * KCH]]),
                    A(rsc, 0, [[NKV * KCH, 128], [1, NKV * KCH]]),
                    AF.Exp, scale=CC(3), bias=CC(5)))
                mk["lastLn"] = ctx.c("sm")

                if "kv" in dump:
                    mv = ctx.c("vm")
                    ctx.op("sync", lambda e: e.dma_start(out=dump_t["kT"][:, :], in_=kT[:, :]),
                           waits=[("vm", mv)], inc="d_dump", n=16)
                    ctx.op("sync", lambda e: e.dma_start(out=dump_t["V"][:, :], in_=V[:, :]),
                           inc="d_dump", n=16)
                    ctx.wait("sync", [("d_dump", ctx.c("d_dump"))])

            run_phase(nc, ctx, ph_kv)

        with ExitStack() as at_es:
            def asb(name, shape, dt=F32):
                return at_es.enter_context(nc.sbuf_tensor(name, shape, dt))

            def aps(name, shape):
                return at_es.enter_context(nc.psum_tensor(name, shape, F32))

            qT = asb("qT", [128, NH * 512], BF16)
            xr4v = qT.bitcast(F32)
            lyT = asb("lyT", [128, max(NH, DK) * 512], BF16)
            ln1q = lyT
            yT = lyT
            xbq0 = asb("xbq0", [128, DK * 512], BF16)
            xbq1 = asb("xbq1", [128, DK * 512], BF16)
            xbqs = [xbq0, xbq1]
            xsq2 = asb("xsq2", [128, 512], BF16)
            probs0 = asb("probs0", [128, 1024], BF16)
            probs1 = asb("probs1", [128, 1024], BF16)
            reca = row_t
            recb = row_r
            wqb0 = asb("wqb0", [128, DK * 128], BF16)
            wqb1 = asb("wqb1", [128, DK * 128], BF16)
            csq = asb("csq", [64, 512])
            snq = asb("snq", [64, 512])
            roq0 = asb("roq0", [128, 512])
            roq1 = asb("roq1", [128, 512])
            tmpq = asb("tmpq", [128, 512])
            rowr1 = asb("rowr1", [1, 512])
            lnt2 = asb("lnt2", [128, 512])
            negmq = asb("negmq", [1, 512], BF16)
            qcolsb = asb("qcolsb", [1, QD], BF16)
            o1f0 = asb("o1f0", [128, 512])
            o1f1 = asb("o1f1", [128, 512])
            roqs = [roq0, roq1]
            rowrs = [row_r, rowr1]
            pS0 = aps("pS0", [128, 1024])
            pS1 = aps("pS1", [128, 1024])
            pY = aps("pY", [128, 1024])
            pR = aps("pR", [128, 1024])

            probs = [probs0, probs1]
            pS = [pS0, pS1]
            wqb = [wqb0, wqb1]
            o1f = [o1f0, o1f1]

            def ph_slots(ctx):
                ctx.barrier()
                mk = {"q_v": 0, "wq_t": [0, 0], "yev": 0, "avp": 0,
                      "xres_v": [0, 0], "o1_t": 0, "pexp": [0, 0], "msc": {},
                      "qcol_s": 0, "qS_s": 0, "qpfree": 0, "pSfree_s": 0, "pSfree_v": 0,
                      "norm_fn": None, "normv": 0,
                      "xbq_v": [0, 0], "xbq_t": [0, 0],
                      "m_xq": [0, 0], "pRfree": 0}
                def xq_load(slot):
                    sb2 = slot % 2
                    ctx.op("gp", lambda e, slot=slot, sb2=sb2: e.dma_start(
                        out=A(xbqs[sb2], 0, [[DK * 512, 128], [512, DK], [1, 512]]),
                        in_=A(dram["xqT"].tensor, slot * 512,
                              [[OWN, 128], [128 * OWN, DK], [1, 512]])),
                        waits=[("vm", mk["xbq_v"][sb2]), ("tm", mk["xbq_t"][sb2])],
                        inc=f"d_xq{sb2}", n=16)
                    mk["m_xq"][sb2] = ctx.c(f"d_xq{sb2}")

                xq_load(0)
                for slot in range(NS):
                    uslot = sum(ITERS[:slot])
                    sb2 = slot % 2
                    xbq = xbqs[sb2]
                    m_xq0 = (f"d_xq{sb2}", mk["m_xq"][sb2])
                    m_xq = mk["m_xq"][sb2]
                    ctx.op("sync", lambda e, slot=slot: e.dma_start(
                        out=csq[:, :], in_=A(dram["cosq"].tensor, slot * 512, [[OWN, 64], [1, 512]])),
                        waits=[("vm", mk["q_v"])], inc="d_cs", n=16)
                    ctx.op("sync", lambda e, slot=slot: e.dma_start(
                        out=snq[:, :], in_=A(dram["sinq"].tensor, slot * 512, [[OWN, 64], [1, 512]])),
                        inc="d_cs", n=16)
                    m_cs = ctx.c("d_cs")
                    for dk in range(DK):
                        ctx.op("tensor", lambda e, dk=dk: e.matmul(
                            pS1[0:1, 0:512], onescol, xbq[:, dk * 512:(dk + 1) * 512],
                            start=(dk == 0), stop=(dk == DK - 1)),
                            waits=([m_xq0, ("sm", mk["pSfree_s"]),
                                    ("vm", mk["pSfree_v"])] if dk == 0 else []),
                            inc=("tm" if dk == DK - 1 else None))
                    m_stats = ctx.c("tm")
                    ctx.op("vector", lambda e: e.tensor_scalar_mul(negmq[:, :], pS1[0:1, 0:512], -1.0 / D),
                           waits=[("tm", m_stats)])
                    m_negmq = ctx.c("vm")
                    if mk.get("norm_fn"):
                        fn = mk["norm_fn"]
                        mk["norm_fn"] = None
                        fn(ctx)

                    mq = {"proj": {}, "ropeM": {}, "exp": {}, "hilo": {}, "ktm": {}}

                    def q_P(qc):
                        buf = qc % 2
                        ctx.op("sync" if buf == 0 else "gp", lambda e, qc=qc, buf=buf: e.dma_start(
                            out=A(wqb[buf], 0, [[DK * 128, 128], [1, DK * 128]]),
                            in_=A(scr_wq2.tensor, qc * 128 * D, [[D, 128], [1, D]])),
                            waits=[("tm", mk["wq_t"][buf])], inc=f"d_wq{buf}", n=16)
                        m_wq = ctx.c(f"d_wq{buf}")
                        if slot == 0:
                            for dk in range(DK):
                                ctx.op("tensor", lambda e, dk=dk, buf=buf: e.matmul(
                                    pR[0:1, 128:256], onescol,
                                    wqb[buf][:, dk * 128:(dk + 1) * 128],
                                    start=(dk == 0), stop=(dk == DK - 1)),
                                    waits=([(f"d_wq{buf}", m_wq),
                                            ("sm", mk["qcol_s"])] if dk == 0 else []),
                                    inc=("tm" if dk == DK - 1 else None))
                            m_qc = ctx.c("tm")
                            ctx.op("scalar", lambda e, qc=qc: e.activation(
                                qcolsb[0:1, qc * 128:(qc + 1) * 128], pR[0:1, 128:256], AF.Copy),
                                waits=[("tm", m_qc)])
                            mk["qcol_s"] = ctx.c("sm")
                        qp = pS0[:, buf * 512:buf * 512 + 512]
                        for dk in range(DK):
                            w = []
                            if dk == 0:
                                w = [(f"d_wq{buf}", m_wq), m_xq0,
                                     ("vm", mq["ropeM"].get(qc - 2, 0)),
                                     ("sm", max(mq["exp"].get(qc - 2, 0),
                                                mk["pSfree_s"] if qc < 2 else 0)),
                                     ("vm", mk["pSfree_v"] if qc < 2 else 0)]
                            ctx.op("tensor", lambda e, qc=qc, dk=dk, qp=qp, buf=buf: e.matmul(
                                qp, wqb[buf][:, dk * 128:(dk + 1) * 128],
                                xbq[:, dk * 512:(dk + 1) * 512],
                                start=(dk == 0), stop=False), waits=w)
                        ctx.op("tensor", lambda e, qc=qc, qp=qp: e.matmul(
                            qp, qcolsb[0:1, qc * 128:(qc + 1) * 128], negmq[0:1, :],
                            start=False, stop=True),
                            waits=[("vm", m_negmq), ("sm", mk["qcol_s"])], inc="tm")
                        mk["wq_t"][buf] = ctx.c("tm")
                        mq["proj"][qc] = ctx.c("tm")

                    def q_R(qc):
                        qp = pS0[:, (qc % 2) * 512:(qc % 2) * 512 + 512]
                        ro = roqs[qc % 2]
                        x1 = qp[0:64, :]
                        x2 = qp[64:128, :]
                        ctx.op("vector", lambda e, x1=x1, ro=ro: e.tensor_mul(ro[0:64, :], x1, csq[:, :]),
                               waits=[("tm", mq["proj"][qc]), ("d_cs", m_cs)])
                        ctx.op("vector", lambda e, x2=x2: e.tensor_mul(tmpq[0:64, :], x2, snq[:, :]))
                        ctx.op("vector", lambda e, ro=ro: e.tensor_add(ro[0:64, :], ro[0:64, :], tmpq[0:64, :]))
                        ctx.op("vector", lambda e, x2=x2, ro=ro: e.tensor_mul(ro[64:128, :], x2, csq[:, :]))
                        ctx.op("vector", lambda e, x1=x1: e.tensor_mul(tmpq[64:128, :], x1, snq[:, :]))
                        ctx.op("vector", lambda e, ro=ro: e.tensor_sub(ro[64:128, :], ro[64:128, :], tmpq[64:128, :]))
                        mq["ropeM"][qc] = ctx.c("vm")

                    def q_S(qc):
                        ro = roqs[qc % 2]
                        ctx.op("scalar", lambda e, ro=ro: e.activation(xsq2[:, :], ro[:, :], AF.Square),
                               waits=[("vm", mq["ropeM"][qc]), ("tm", ctx.c("tm"))])
                        m_rosq = ctx.c("sm")
                        ctx.op("tensor", lambda e: e.matmul(pS1[0:1, 512:1024], onescol, xsq2[:, :],
                                                            start=True, stop=True),
                               waits=[("sm", m_rosq), ("vm", m_negmq if qc == 0 else 0)], inc="tm")
                        m_rms = ctx.c("tm")
                        ctx.op("scalar", lambda e: e.activation(row_t[:, :], pS1[0:1, 512:1024], AF.Ln,
                                                                bias=CC(1, 1), scale=CC(2, 1)),
                               waits=[("tm", m_rms)])
                        ctx.op("scalar", lambda e, qc=qc: e.activation(rowrs[qc % 2][:, :], row_t[:, :],
                                                                       AF.Exp, scale=CC(3, 1)),
                               waits=[("tm", mq["ktm"].get(qc - 2, 0))])
                        mq["exp"][qc] = ctx.c("sm")

                    def q_F(qc):
                        ro = roqs[qc % 2]
                        m_rqb = bcastF(ctx, pY[:, 0:512], rowrs[qc % 2][0:1, :],
                                       [("sm", mq["exp"][qc]),
                                        ("vm", mk["yev"] if qc == 0 else mq["hilo"].get(qc - 1, 0))])
                        mq["ktm"][qc] = m_rqb
                        ctx.op("vector", lambda e, qc=qc, ro=ro: e.tensor_mul(
                            qT[:, qc * 512:(qc + 1) * 512], ro[:, :], pY[:, 0:512]),
                            waits=[("tm", m_rqb),
                                   ("d_xr0", ctx.c("d_xr0")), ("d_xr1", ctx.c("d_xr1"))])
                        mq["hilo"][qc] = ctx.c("vm")

                    q_P(0)
                    if NH > 1:
                        q_P(1)
                    q_R(0)
                    q_S(0)
                    for qc in range(NH):
                        if qc + 2 < NH:
                            q_P(qc + 2)
                        if qc + 1 < NH:
                            q_R(qc + 1)
                            q_S(qc + 1)
                        q_F(qc)
                    m_qT = ctx.c("vm")
                    mk["qS_s"] = ctx.c("sm")
                    if slot + 1 < NS:
                        xq_load(slot + 1)

                    iters = ITERS[slot]
                    plains = iters - 4
                    for kv in range(NKV):
                        for pair in range(NPAIR):
                            h0 = kv * GRP + pair * 2
                            m_exp = {-2: mk["pexp"][0], -1: mk["pexp"][1]}
                            m_av_h = {-2: mk["avp"], -1: mk["avp"]}

                            def geo(it):
                                diag = it >= plains
                                r = it - plains if diag else 0
                                return (it % 2, diag, 128 * r,
                                        it if not diag else TCH + slot * 4 + r)

                            def emit_scores(it):
                                g2, diag, noff, pos = geo(it)
                                w_sc = [("sm", m_exp.get(it - 2, 0))]
                                if it == 0:
                                    w_sc.append(("vm", m_qT))
                                    w_sc.append(("sm", mk["qS_s"]))
                                for h in range(2):
                                    ctx.op("tensor", lambda e, pos=pos, h=h,
                                           noff=noff, g2=g2: e.matmul(
                                        pS[g2][:, h * 512 + noff: (h + 1) * 512],
                                        kT[:, kv * KEXT + pos * 128: kv * KEXT + (pos + 1) * 128],
                                        qT[:, (h0 + h) * 512 + noff: (h0 + h + 1) * 512],
                                        start=True, stop=True),
                                        waits=(w_sc if h == 0 else []),
                                        inc=("tm" if h == 1 else None))
                                m_sc = ctx.c("tm")
                                m_mask = 0
                                if diag:
                                    ctx.op("vector", lambda e, g2=g2, noff=noff: e.tensor_tensor(
                                        A(pS[g2], noff, [[1024, 128], [512, 2], [1, 128]]),
                                        A(pS[g2], noff, [[1024, 128], [512, 2], [1, 128]]),
                                        A(trib, 0, [[128, 128], [0, 2], [1, 128]]),
                                        ALU.add),
                                        waits=[("tm", m_sc)])
                                    m_mask = ctx.c("vm")
                                mk["msc"][it] = (m_sc, m_mask)

                            def emit_exp(it):
                                g2, diag, noff, pos = geo(it)
                                u = uslot + it
                                m_sc, m_mask = mk["msc"][it]
                                wexp = [("tm", m_sc), ("tm", m_av_h.get(it - 2, 0)),
                                        ("vm", m_mask)]
                                rcol = kv * KCH + pos
                                ctx.op("scalar", lambda e, g2=g2, noff=noff, u=u,
                                       rcol=rcol: e.activation(
                                    A(probs[g2], noff, [[1024, 128], [512, 2], [1, 512 - noff]]),
                                    A(pS[g2], noff, [[1024, 128], [512, 2], [1, 512 - noff]]),
                                    AF.Exp, bias=kbb[:, u:u + 1], scale=rsc[:, rcol:rcol + 1]),
                                    waits=wexp)
                                m_exp[it] = ctx.c("sm")

                            def emit_sumav(it):
                                g2, diag, noff, pos = geo(it)
                                for h in range(2):
                                    w = [("sm", m_exp[it])] if h == 0 else []
                                    if it == 0 and h == 0:
                                        w.append(("vm", max(mk["yev"], mk["pRfree"])))
                                    ctx.op("tensor", lambda e, h=h, g2=g2, noff=noff,
                                           it=it: e.matmul(
                                        pR[0:1, h * 512 + noff:h * 512 + 512], onescol,
                                        probs[g2][:, h * 512 + noff:(h + 1) * 512],
                                        start=(it == 0), stop=(it == iters - 1)),
                                        waits=w)
                                for h in range(2):
                                    ctx.op("tensor", lambda e, h=h, g2=g2, noff=noff,
                                           pos=pos, it=it: e.matmul(
                                        pY[:, h * 512 + noff:(h + 1) * 512],
                                        V[:, pos * KD + kv * 128: pos * KD + (kv + 1) * 128],
                                        probs[g2][:, h * 512 + noff:(h + 1) * 512],
                                        start=(it == 0), stop=(it == iters - 1)),
                                        inc=("tm" if h == 1 else None))
                                m_av_h[it] = ctx.c("tm")

                            mk["msc"] = {}
                            for it2 in range(0, iters, 2):
                                emit_scores(it2)
                                emit_exp(it2)
                                emit_scores(it2 + 1)
                                if it2 >= 1:
                                    emit_sumav(it2 - 1)
                                emit_exp(it2 + 1)
                                emit_sumav(it2)
                            emit_sumav(iters - 1)
                            m_av = ctx.c("tm")
                            mk["pexp"] = [m_exp[iters - 2], m_exp[iters - 1]]
                            mk["avp"] = m_av
                            ctx.op("scalar", lambda e: e.activation(reca[:, :], pR[0:1, 0:512],
                                                                    AF.Ln),
                                   waits=[("tm", m_av)])
                            ctx.op("scalar", lambda e: e.activation(reca[:, :], reca[:, :],
                                                                    AF.Exp, scale=CC(7, 1)))
                            ctx.op("scalar", lambda e: e.activation(recb[:, :], pR[0:1, 512:1024],
                                                                    AF.Ln))
                            ctx.op("scalar", lambda e: e.activation(recb[:, :], recb[:, :],
                                                                    AF.Exp, scale=CC(7, 1)))
                            m_rec = ctx.c("sm")
                            ctx.op("tensor", lambda e: e.matmul(pR[:, 0:512], onesrf[0:1, :], reca[0:1, :],
                                                                start=True, stop=True),
                                   waits=[("sm", m_rec)])
                            ctx.op("tensor", lambda e: e.matmul(pR[:, 512:1024], onesrf[0:1, :], recb[0:1, :],
                                                                start=True, stop=True), inc="tm")
                            m_rb = ctx.c("tm")
                            ctx.op("vector", lambda e: e.tensor_copy(roq0[:, :], pR[:, 0:512]),
                                   waits=[("tm", m_rb)])
                            ctx.op("vector", lambda e: e.tensor_copy(tmpq[:, :], pR[:, 512:1024]))
                            rbsh = [roq0, tmpq]
                            for h in range(2):
                                ctx.op("vector", lambda e, h=h, h0=h0: e.tensor_mul(
                                    yT[:, (h0 + h) * 512:(h0 + h + 1) * 512],
                                    pY[:, h * 512:(h + 1) * 512], rbsh[h][:, :]),
                                    waits=[("d_l20", ctx.c("d_l20"))] if (kv == 0 and pair == 0 and h == 0) else [])
                            mk["yev"] = ctx.c("vm")

                    m_yT = ctx.c("vm")
                    mk["pSfree_s"] = ctx.c("sm")
                    mk["pSfree_v"] = ctx.c("vm")
                    if "qy" in dump:
                        ctx.op("sync", lambda e, slot=slot: e.dma_start(
                            out=A(dump_t["qT"].tensor, slot * NH * 512,
                                  [[NS * NH * 512, 128], [1, NH * 512]]),
                            in_=qT[:, :]), waits=[("vm", m_yT)], inc="d_dump", n=16)
                        ctx.op("sync", lambda e, slot=slot: e.dma_start(
                            out=A(dump_t["yT"].tensor, slot * NH * 512,
                                  [[NS * NH * 512, 128], [1, NH * 512]]),
                            in_=yT[:, :]), inc="d_dump", n=16)

                    m_add = {}
                    m_o1b = {}
                    m_xr4 = {}
                    for dk in range(DK):
                        buf = dk % 2
                        NXR = min(4, (NH * 256) // 1024)
                        HWID = NXR * 512
                        g4 = dk // NXR
                        if dk % NXR == 0:
                            xb4 = g4 % 2
                            ctx.op("sync", lambda e, g4=g4, xb4=xb4, slot=slot: e.dma_start(
                                out=A(xr4v, xb4 * HWID, [[NH * 256, 128], [512, NXR], [1, 512]]),
                                in_=A(dram["xqT"].tensor, g4 * NXR * 128 * OWN + slot * 512,
                                      [[OWN, 128], [128 * OWN, NXR], [1, 512]])),
                                waits=[("vm", max(mk["xres_v"][xb4], mk["yev"])),
                                       ("tm", mk["avp"] if g4 < 2 else 0),
                                       (f"d_xr{xb4}", ctx.c(f"d_xr{xb4}"))],
                                inc=f"d_xr{xb4}", n=16)
                            m_xr4[g4] = ctx.c(f"d_xr{xb4}")
                        ctx.op("sync" if buf == 0 else "gp", lambda e, dk=dk, buf=buf: e.dma_start(
                            out=A(wqb[buf], 0, [[DK * 128, 128], [1, DK * 128]]),
                            in_=A(scr_wo2.tensor, dk * 128 * D, [[D, 128], [1, D]])),
                            waits=[("tm", mk["wq_t"][buf])], inc=f"d_wq{buf}", n=16)
                        m_wo = ctx.c(f"d_wq{buf}")
                        m_xr = m_xr4[g4]
                        pO = pY[:, buf * 512:buf * 512 + 512]
                        for yf in range(NH):
                            w = []
                            if yf == 0:
                                w = [(f"d_wq{buf}", m_wo),
                                     ("vm", m_yT if dk < 2 else m_add[dk - 2])]
                            ctx.op("tensor", lambda e, dk=dk, yf=yf, pO=pO, buf=buf: e.matmul(
                                pO, wqb[buf][:, yf * 128:(yf + 1) * 128],
                                yT[:, yf * 512:(yf + 1) * 512],
                                start=(yf == 0), stop=(yf == NH - 1)), waits=w,
                                inc=("tm" if yf == NH - 1 else None))
                        mk["wq_t"][buf] = ctx.c("tm")
                        m_o1 = ctx.c("tm")
                        ctx.op("vector", lambda e, buf=buf, pO=pO, dk=dk, g4=g4: e.tensor_add(
                            o1f[buf][:, :], pO,
                            xr4v[:, (g4 % 2) * HWID + (dk % NXR) * 512:(g4 % 2) * HWID + (dk % NXR) * 512 + 512]),
                            waits=[("tm", m_o1), (f"d_xr{g4 % 2}", m_xr),
                                   (f"d_sp{buf}", ctx.c(f"d_sp{buf}")),
                                   ("sm", m_o1b.get(dk - 2, 0))])
                        mk["xres_v"][g4 % 2] = ctx.c("vm")
                        m_add[dk] = ctx.c("vm")
                        ctx.op("sync", lambda e, buf=buf, dk=dk, slot=slot: e.dma_start(
                            out=A(scr_out1.tensor, dk * 128 * OWN + slot * 512,
                                  [[OWN, 128], [1, 512]]),
                            in_=o1f[buf][:, :]),
                            waits=[("vm", m_add[dk])], inc=f"d_sp{buf}", n=16)
                        ctx.op("scalar", lambda e, buf=buf, dk=dk: e.activation(
                            xbq[:, dk * 512:(dk + 1) * 512], o1f[buf][:, :], AF.Copy),
                            waits=[("vm", m_add[dk])])
                        m_o1b[dk] = ctx.c("sm")
                    xsqw = lnt2.bitcast(BF16)
                    nb = 2
                    SQW = nb * 512
                    m_prb2 = 0
                    for hb in range(DK // nb):
                        ctx.op("scalar", lambda e, hb=hb, SQW=SQW: e.activation(
                            xsqw[:, 0:SQW], xbq[:, hb * SQW:(hb + 1) * SQW], AF.Square),
                            waits=[("sm", m_o1b[min(DK - 1, (hb + 1) * nb - 1)]), ("tm", m_prb2)]
                                  + ([("vm", mk["normv"])] if hb == 0 else []))
                        m_sq2 = ctx.c("sm")
                        for j in range(nb):
                            dk = hb * nb + j
                            ctx.op("tensor", lambda e, dk=dk: e.matmul(
                                pR[0:1, 0:512], onescol, xbq[:, dk * 512:(dk + 1) * 512],
                                start=(dk == 0), stop=(dk == DK - 1)),
                                waits=[("sm", m_o1b[dk])])
                            ctx.op("tensor", lambda e, j=j, dk=dk: e.matmul(
                                pR[0:1, 512:1024], onescol, xsqw[:, j * 512:(j + 1) * 512],
                                start=(dk == 0), stop=(dk == DK - 1)),
                                waits=([("sm", m_sq2)] if j == 0 else []),
                                inc=("tm" if j == nb - 1 else None))
                        m_prb2 = ctx.c("tm")
                    m_stats2 = ctx.c("tm")
                    mk["o1_t"] = m_stats2
                    m_rows3 = ln_rows2(ctx, pR[0:1, 0:512], pR[0:1, 512:1024], m_stats2, 0)
                    m_bc3 = bcastF(ctx, pR[:, 0:512], row_r[0:1, :], [("vm", m_rows3)])
                    m_bc4 = bcastF(ctx, pR[:, 512:1024], row_mr[0:1, :], [("vm", m_rows3)])

                    def emit_norm(ctx, xbq=xbq, m_bc3=m_bc3, m_bc4=m_bc4, slot=slot, sb2=sb2):
                        for hb in range(DK):
                            ctx.op("vector", lambda e, hb=hb, xbq=xbq: e.tensor_tensor(
                                lnt2[:, :], xbq[:, hb * 512:(hb + 1) * 512],
                                pR[:, 0:512], ALU.mult),
                                waits=([("tm", m_bc3), ("d_l20", ctx.c("d_l20"))] if hb == 0 else []))
                            ctx.op("vector", lambda e, hb=hb: e.tensor_tensor(
                                A(ln1q, hb * 512, [[DK * 512, 128], [1, 512]]),
                                lnt2[:, :], pR[:, 512:1024], ALU.subtract),
                                waits=([("tm", m_bc4)] if hb == 0 else []))
                        m_l2 = ctx.c("vm")
                        ctx.op("sync", lambda e, slot=slot: e.dma_start(
                            out=A(scr_ln2.tensor, slot * 512,
                                  [[OWN, 128], [128 * OWN, DK], [1, 512]]),
                            in_=A(ln1q, 0, [[DK * 512, 128], [512, DK], [1, 512]])),
                            waits=[("vm", m_l2)], inc="d_l20", n=16)
                        mk["pRfree"] = m_l2
                        mk["normv"] = m_l2
                        mk["xbq_v"][sb2] = m_l2

                    if slot + 1 < NS:
                        mk["norm_fn"] = emit_norm
                    else:
                        emit_norm(ctx)
                    mk["q_v"] = ctx.c("vm")
                    mk["xbq_t"][sb2] = m_stats2

                ctx.wait("sync", [("d_sp0", ctx.c("d_sp0")), ("d_sp1", ctx.c("d_sp1")),
                                  ("d_l20", ctx.c("d_l20"))])
                if ctx.c("d_dump"):
                    ctx.wait("sync", [("d_dump", ctx.c("d_dump"))])

            run_phase(nc, ctx, ph_slots)

        akv_es.close()

        with ExitStack() as f1_es:
            def fsb(name, shape, dt=F32):
                return f1_es.enter_context(nc.sbuf_tensor(name, shape, dt))

            def fps(name, shape):
                return f1_es.enter_context(nc.psum_tensor(name, shape, F32))

            ln2T = fsb("ln2T", [128, DK * OWN], BF16)
            wgb0 = fsb("wgb0", [128, DK * 128], BF16)
            wgb1 = fsb("wgb1", [128, DK * 128], BF16)
            wub0 = fsb("wub0", [128, DK * 128], BF16)
            wub1 = fsb("wub1", [128, DK * 128], BF16)
            silo = fsb("silo", [128, NGRP * 512])
            gst0 = fsb("gst0", [128, NGRP * 512], BF16)
            gst1 = fsb("gst1", [128, NGRP * 512], BF16)
            wds0 = fsb("wds0", [128, D], BF16)
            wds1 = fsb("wds1", [128, D], BF16)
            pG = fps("pG", [128, NGRP * 512])
            pU = fps("pU", [128, NGRP * 512])
            wgb = [wgb0, wgb1]
            wub = [wub0, wub1]
            gst = [gst0, gst1]
            wds = [wds0, wds1]

            def ph_f1(ctx):
                ctx.barrier()
                ctx.op("sync", lambda e: e.dma_start(
                    out=A(ln2T, 0, [[DK * OWN, 128], [OWN, DK], [1, OWN]]),
                    in_=A(scr_ln2.tensor, 0, [[OWN, 128], [128 * OWN, DK], [1, OWN]])),
                    inc="d_ln2ld", n=16)
                m_ln2 = ctx.c("d_ln2ld")
                mk = {"w_t": [0, 0], "silu_s": 0, "gated_v": 0, "g_v": [0, 0],
                      "wd_v": [0, 0]}
                for hc in range(HC):
                    buf = hc % 2
                    ctx.op("gp", lambda e, hc=hc, buf=buf: e.dma_start(
                        out=A(wgb[buf], 0, [[DK * 128, 128], [128, DK], [1, 128]]),
                        in_=A(dram["wgT"].tensor, hc * 128,
                              [[DH, 128], [128 * DH, DK], [1, 128]])),
                        waits=[("tm", mk["w_t"][buf])], inc=f"d_fw{buf}", n=16)
                    ctx.op("gp", lambda e, hc=hc, buf=buf: e.dma_start(
                        out=A(wub[buf], 0, [[DK * 128, 128], [128, DK], [1, 128]]),
                        in_=A(dram["wuT"].tensor, hc * 128,
                              [[DH, 128], [128 * DH, DK], [1, 128]])),
                        inc=f"d_fw{buf}", n=16)
                    m_w = ctx.c(f"d_fw{buf}")
                    ctx.op("gp", lambda e, hc=hc, buf=buf: e.dma_start(
                        out=wds[buf][:, :],
                        in_=A(dram["wdT"].tensor, hc * 128 * D, [[D, 128], [1, D]])),
                        waits=[(f"d_ws{buf}", mk["wd_v"][buf])], inc=f"d_wp{buf}", n=16)
                    m_wd = ctx.c(f"d_wp{buf}")
                    ctx.op("sync", lambda e, hc=hc, buf=buf: e.dma_start(
                        out=A(scr_wd.tensor, hc * 128 * D, [[D, 128], [1, D]]),
                        in_=wds[buf][:, :]),
                        waits=[(f"d_wp{buf}", m_wd)], inc=f"d_ws{buf}", n=16)
                    mk["wd_v"][buf] = ctx.c(f"d_ws{buf}")
                    for dk in range(DK):
                        w = []
                        if dk == 0:
                            w = [(f"d_fw{buf}", m_w), ("d_ln2ld", m_ln2),
                                 ("sm", mk["silu_s"])]
                        for gi in range(NGRP):
                            ctx.op("tensor", lambda e, dk=dk, gi=gi, buf=buf: e.matmul(
                                pG[:, gi * 512:(gi + 1) * 512],
                                wgb[buf][:, dk * 128:(dk + 1) * 128],
                                ln2T[:, dk * OWN + gi * 512: dk * OWN + (gi + 1) * 512],
                                start=(dk == 0), stop=(dk == DK - 1)),
                                waits=(w if gi == 0 else []),
                                inc=("tm" if (dk == DK - 1 and gi == NGRP - 1) else None))
                    m_g = ctx.c("tm")
                    for dk in range(DK):
                        w = []
                        if dk == 0:
                            w = [(f"d_fw{buf}", m_w), ("vm", mk["gated_v"])]
                        for gi in range(NGRP):
                            ctx.op("tensor", lambda e, dk=dk, gi=gi, buf=buf: e.matmul(
                                pU[:, gi * 512:(gi + 1) * 512],
                                wub[buf][:, dk * 128:(dk + 1) * 128],
                                ln2T[:, dk * OWN + gi * 512: dk * OWN + (gi + 1) * 512],
                                start=(dk == 0), stop=(dk == DK - 1)),
                                waits=(w if gi == 0 else []),
                                inc=("tm" if (dk == DK - 1 and gi == NGRP - 1) else None))
                    mk["w_t"][buf] = ctx.c("tm")
                    m_u = ctx.c("tm")
                    ctx.op("scalar", lambda e: e.activation(silo[:, :], pG[:, :], AF.Silu),
                           waits=[("tm", m_g), ("vm", mk["gated_v"])])
                    mk["silu_s"] = ctx.c("sm")
                    ctx.op("vector", lambda e, buf=buf: e.tensor_mul(gst[buf][:, :], silo[:, :], pU[:, :]),
                           waits=[("tm", m_u), ("sm", mk["silu_s"]),
                                  (f"d_gs{buf}", mk["g_v"][buf])])
                    mk["gated_v"] = ctx.c("vm")
                    m_gv = ctx.c("vm")
                    ctx.op("sync", lambda e, hc=hc, buf=buf: e.dma_start(
                        out=A(scr_gated.tensor, hc * 128 * OWN, [[OWN, 128], [1, OWN]]),
                        in_=gst[buf][:, :]),
                        waits=[("vm", m_gv)], inc=f"d_gs{buf}", n=16)
                    mk["g_v"][buf] = ctx.c(f"d_gs{buf}")
                ctx.wait("sync", [("d_gs0", ctx.c("d_gs0")), ("d_gs1", ctx.c("d_gs1")),
                                  ("d_ws0", ctx.c("d_ws0")), ("d_ws1", ctx.c("d_ws1"))])

            run_phase(nc, ctx, ph_f1)

        with ExitStack() as f2_es:
            def gsb(name, shape, dt=F32):
                return f2_es.enter_context(nc.sbuf_tensor(name, shape, dt))

            gld0 = gsb("gld0", [128, HGS * 512], BF16)
            gld1 = gsb("gld1", [128, HGS * 512], BF16)
            gld2 = gsb("gld2", [128, HGS * 512], BF16)
            wdl0 = gsb("wdl0", [128, HGS * DSETW * 128], BF16)
            wdl1 = gsb("wdl1", [128, HGS * DSETW * 128], BF16)
            wdl2 = gsb("wdl2", [128, HGS * DSETW * 128], BF16)
            o1t0 = gsb("o1t0", [128, DSETW * 512])
            o1t1 = gsb("o1t1", [128, DSETW * 512])
            ost0 = gsb("ost0", [128, 512])
            ost1 = gsb("ost1", [128, 512])
            pDa = f2_es.enter_context(nc.psum_tensor("pDa", [128, DSETW * 512], F32))
            pDb = (f2_es.enter_context(nc.psum_tensor("pDb", [128, DSETW * 512], F32))
                   if F2ROT else pDa)
            gld = [gld0, gld1, gld2]
            wdl = [wdl0, wdl1, wdl2]
            o1t = [o1t0, o1t1]
            ost = [ost0, ost1]
            pDx = [pDa, pDb]
            DSW = DSETW * 128

            def ph_f2(ctx):
                ctx.barrier()
                mk = {"mm_t": [0, 0, 0], "ost_v": [0, 0], "ep_v": [0, 0], "o1_v": [0, 0]}
                units = [(gi, dset) for gi in range(NGRP) for dset in range(NDSET)]
                sched = [(u, hg) for u in range(len(units)) for hg in range(NHG)]

                def o1_load(u):
                    gi, dset = units[u]
                    ub = u % 2
                    ctx.op("gp", lambda e, gi=gi, dset=dset, ub=ub: e.dma_start(
                        out=A(o1t[ub], 0, [[DSETW * 512, 128], [512, DSETW], [1, 512]]),
                        in_=A(scr_out1.tensor, dset * DSW * OWN + gi * 512,
                              [[OWN, 128], [128 * OWN, DSETW], [1, 512]])),
                        waits=[("vm", mk["ep_v"][ub if F2ROT else 0])], inc=f"d_o1{ub}", n=16)
                    return ctx.c(f"d_o1{ub}")

                def f2_load(idx):
                    u, hg = sched[idx]
                    gi, dset = units[u]
                    buf = idx % 3
                    ctx.op("sync", lambda e, gi=gi, hg=hg, buf=buf: e.dma_start(
                        out=A(gld[buf], 0, [[HGS * 512, 128], [512, HGS], [1, 512]]),
                        in_=A(scr_gated.tensor, hg * HGS * 128 * OWN + gi * 512,
                              [[OWN, 128], [128 * OWN, HGS], [1, 512]])),
                        waits=[("tm", mk["mm_t"][buf])], inc=f"d_gl{buf}", n=16)
                    half = HGS // 2
                    ctx.op("sync", lambda e, dset=dset, hg=hg, buf=buf, half=half: e.dma_start(
                        out=A(wdl[buf], 0, [[HGS * DSW, 128], [DSW, half], [1, DSW]]),
                        in_=A(scr_wd.tensor, hg * HGS * 128 * D + dset * DSW,
                              [[D, 128], [128 * D, half], [1, DSW]])),
                        inc=f"d_wl{buf}", n=16)
                    ctx.op("gp", lambda e, dset=dset, hg=hg, buf=buf, half=half: e.dma_start(
                        out=A(wdl[buf], half * DSW, [[HGS * DSW, 128], [DSW, HGS - half], [1, DSW]]),
                        in_=A(scr_wd.tensor, (hg * HGS + half) * 128 * D + dset * DSW,
                              [[D, 128], [128 * D, HGS - half], [1, DSW]])),
                        waits=[("tm", mk["mm_t"][buf])], inc=f"d_wm{buf}", n=16)
                    return ctx.c(f"d_gl{buf}"), ctx.c(f"d_wl{buf}"), ctx.c(f"d_wm{buf}")

                fm = {0: f2_load(0), 1: f2_load(1)}
                mo1 = {0: o1_load(0)}
                for idx, (u, hg) in enumerate(sched):
                    gi, dset = units[u]
                    ub = (u % 2) if F2ROT else 0
                    pD = pDx[ub]
                    buf = idx % 3
                    if hg == 0 and u + 1 < len(units):
                        mo1[u + 1] = o1_load(u + 1)
                    if idx + 2 < len(sched):
                        fm[idx + 2] = f2_load(idx + 2)
                    m_g, m_wdl, m_wm = fm[idx]
                    for hci in range(HGS):
                        w = []
                        if hci == 0:
                            w = [(f"d_gl{buf}", m_g), (f"d_wl{buf}", m_wdl),
                                 (f"d_wm{buf}", m_wm)]
                            if hg == 0:
                                w.append(("vm", mk["ep_v"][ub]))
                        for dki in range(DSETW):
                            ctx.op("tensor", lambda e, hci=hci, dki=dki, buf=buf,
                                   pD=pD: e.matmul(
                                pD[:, dki * 512:(dki + 1) * 512],
                                wdl[buf][:, hci * DSW + dki * 128: hci * DSW + dki * 128 + 128],
                                gld[buf][:, hci * 512:(hci + 1) * 512],
                                start=(hg == 0 and hci == 0),
                                stop=(hg == NHG - 1 and hci == HGS - 1)),
                                waits=(w if dki == 0 else []),
                                inc=("tm" if (hci == HGS - 1 and dki == DSETW - 1) else None))
                    mk["mm_t"][buf] = ctx.c("tm")
                    if hg == NHG - 1:
                        m_mm = ctx.c("tm")
                        ob = u % 2
                        for dki in range(DSETW):
                            obuf = dki % 2
                            ctx.op("vector", lambda e, dki=dki, obuf=obuf, pD=pD, ob=ob: e.tensor_add(
                                ost[obuf][:, :], pD[:, dki * 512:(dki + 1) * 512],
                                o1t[ob][:, dki * 512:(dki + 1) * 512]),
                                waits=[("tm", m_mm), (f"d_o1{ob}", mo1[u]),
                                       (f"d_ou{obuf}", mk["ost_v"][obuf])])
                            m_o = ctx.c("vm")
                            ctx.op("sync", lambda e, dki=dki, dset=dset, gi=gi, obuf=obuf: e.dma_start(
                                out=A(out.tensor, (dset * DSETW + dki) * 128 * OWN + gi * 512,
                                      [[OWN, 128], [1, 512]]),
                                in_=ost[obuf][:, :]),
                                waits=[("vm", m_o)], inc=f"d_ou{obuf}", n=16)
                            mk["ost_v"][obuf] = ctx.c(f"d_ou{obuf}")
                        mk["ep_v"][ub] = ctx.c("vm")
                ctx.wait("sync", [("d_ou0", ctx.c("d_ou0")), ("d_ou1", ctx.c("d_ou1"))])
                ctx.barrier()

            run_phase(nc, ctx, ph_f2)

    return nc



def kernel(L0_attn_x, L0_attn_wq, L0_attn_wk, L0_attn_wv, L0_attn_wo,
           L0_ffn_w_gate, L0_ffn_w_up, L0_ffn_w_down):
    cfg = CFG
    maps = host_inputs(cfg, L0_attn_x, L0_attn_wq, L0_attn_wk, L0_attn_wv,
                       L0_attn_wo, L0_ffn_w_gate, L0_ffn_w_up, L0_ffn_w_down)
    nc = build_program(cfg)
    res = run_bass_kernel_spmd(nc, maps, core_ids=list(range(cfg["NCORE"])))
    outp = np.empty((cfg["B"], cfg["T"], cfg["D"]), np.float32)
    for c in range(cfg["NCORE"]):
        b, ct = c // 2, c % 2
        _, perm_q, _ = core_perms(cfg, ct)
        outp[b, perm_q, :] = np.asarray(res.results[c]["outT"]).T
    return outp


# version 24
# speedup vs baseline: 1.1538x; 1.0441x over previous
import numpy as np
import ml_dtypes
from contextlib import ExitStack

import concourse.bass as bass
import concourse.mybir as mybir
from concourse.bass_utils import run_bass_kernel_spmd

F32 = mybir.dt.float32
BF16 = mybir.dt.bfloat16
AF = mybir.ActivationFunctionType
ALU = mybir.AluOpType
BF = ml_dtypes.bfloat16

MASK = -10000.0
EPS_LN = 1e-5
EPS_RMS = 1e-6
ROPE_BASE = 100000.0


def make_cfg(B=4, NSLOT=4, D=2048, NH=16, NKV=4, DH=8192):
    cfg = dict(B=B, NSLOT=NSLOT, D=D, NH=NH, NKV=NKV, DH=DH)
    cfg["T"] = 1024 * NSLOT
    cfg["OWN"] = 512 * NSLOT
    cfg["KEXT"] = cfg["T"] + 512 * NSLOT
    cfg["DK"] = D // 128
    cfg["HC"] = DH // 128
    cfg["GRP"] = NH // NKV
    cfg["ITERS"] = [8 * j + 8 for j in range(NSLOT)]
    cfg["NIT"] = sum(cfg["ITERS"])
    cfg["NGK"] = cfg["KEXT"] // 512
    cfg["NCORE"] = 2 * B
    return cfg


CFG = make_cfg()
SCALE = float(np.float32(1.0) / np.sqrt(np.float32(128.0)))



def core_perms(cfg, ct):
    T, NS = cfg["T"], cfg["NSLOT"]
    groups = [2 * j + ct for j in range(NS)]
    perm_q = np.concatenate([np.arange(512 * g, 512 * g + 512) for g in groups])
    dup = np.concatenate([np.arange(1024 * j + 512 * ct, 1024 * j + 512 * ct + 512)
                          for j in range(NS)])
    perm_kv = np.concatenate([np.arange(T), dup])
    return groups, perm_q, perm_kv


def host_inputs(cfg, x, wq, wk, wv, wo, wg, wu, wd):
    B, T, NS, NIT = cfg["B"], cfg["T"], cfg["NSLOT"], cfg["NIT"]

    shared = {
        "wq": np.ascontiguousarray(wq, np.float32),
        "wk": np.ascontiguousarray(wk, np.float32),
        "wv": np.ascontiguousarray(wv, np.float32),
        "wo": np.ascontiguousarray(wo, np.float32),
        "wgT": np.ascontiguousarray(np.asarray(wg, np.float32).T),
        "wuT": np.ascontiguousarray(np.asarray(wu, np.float32).T),
        "wdT": np.ascontiguousarray(np.asarray(wd, np.float32).T),
    }
    selbuf = np.zeros((128, 32), BF)
    selbuf[:, 16] = 1.0
    onesrow = np.ones((1, 128), BF)
    tri = np.zeros((128, 128), BF)
    for i in range(128):
        tri[i, :i] = MASK
    shared["selbuf"] = selbuf
    shared["onesrow"] = onesrow
    shared["onesrf"] = np.ones((1, 128), np.float32)
    shared["tri"] = tri
    consts = np.array([EPS_LN, EPS_RMS, 1.0 / 128.0, -0.5, SCALE,
                       float(np.log(np.float32(SCALE))), 1.0, -1.0],
                      np.float32)
    shared["consts"] = np.tile(consts[None, :], (128, 1))

    half = 64
    inv = (1.0 / (np.float32(ROPE_BASE) **
                  (np.arange(half, dtype=np.float32) / np.float32(half)))).astype(np.float32)
    pos = np.arange(T, dtype=np.float32)
    freqs = pos[:, None] * inv[None, :]
    cosT = np.cos(freqs).astype(np.float32)
    sinT = np.sin(freqs).astype(np.float32)

    maps = []
    for c in range(2 * B):
        b, ct = c // 2, c % 2
        groups, perm_q, perm_kv = core_perms(cfg, ct)
        m = dict(shared)
        xb = np.asarray(x[b], np.float32)
        m["xqT"] = np.ascontiguousarray(xb[perm_q].T)
        m["xkvT"] = np.ascontiguousarray(xb[perm_kv].T)
        m["cosq"] = np.ascontiguousarray(cosT[perm_q].T)
        m["sinq"] = np.ascontiguousarray(sinT[perm_q].T)
        m["cosk"] = np.ascontiguousarray(cosT[perm_kv].T)
        m["sink"] = np.ascontiguousarray(sinT[perm_kv].T)
        kb = np.zeros((128, NIT), BF)
        u = 0
        for j in range(NS):
            iters = cfg["ITERS"][j]
            plains = iters - 4
            for it in range(iters):
                if ct == 0 and plains - 4 <= it < plains:
                    kb[:, u] = MASK
                u += 1
        m["kbias"] = kb
        maps.append(m)
    return maps



class Ctx:

    def __init__(self, nc):
        self.nc = nc
        self.sems = {}
        self.cnt = {}
        self.cur = None
        self.eng = None

    def mksem(self, es, *names):
        for n in names:
            self.sems[n] = es.enter_context(self.nc.semaphore(n))
            self.cnt[n] = 0

    def c(self, name):
        return self.cnt[name]

    def op(self, engine, fn=None, waits=(), inc=None, n=1, guard=True):
        waits = [(s, v) for (s, v) in waits if v > 0]
        if engine in ("vector", "scalar") and fn is not None:
            gsem = "vm" if engine == "vector" else "sm"
            if guard and self.cnt[gsem] > 0:
                waits.append((gsem, self.cnt[gsem]))
            if inc is None:
                inc = gsem
        if self.cur == engine:
            for s, v in waits:
                self.eng.wait_ge(self.sems[s], v)
            if fn is not None:
                instr = fn(self.eng)
                if inc is not None:
                    instr.then_inc(self.sems[inc], n)
        if fn is not None and inc is not None:
            self.cnt[inc] += n

    def wait(self, engine, waits):
        self.op(engine, None, waits=waits)

    def barrier(self):
        marks = [(s, self.cnt[s]) for s in ("tm", "sm", "vm") if self.cnt[s] > 0]
        for e in ("tensor", "scalar", "vector", "sync", "gp"):
            self.op(e, None, waits=marks)


def run_phase(nc, ctx, fn):
    snap = dict(ctx.cnt)
    with nc.Block() as blk:
        def mk(name):
            def body(eng, name=name):
                ctx.cnt = dict(snap)
                ctx.cur, ctx.eng = name, eng
                fn(ctx)
            return body
        blk.tensor(mk("tensor"))
        blk.scalar(mk("scalar"))
        blk.vector(mk("vector"))
        blk.sync(mk("sync"))
        blk.gpsimd(mk("gp"))
    ctx.cnt = dict(snap)
    ctx.cur, ctx.eng = None, None
    fn(ctx)


def A(t, off, dims):
    return bass.AP(t, off, [list(d) for d in dims])



def build_program(cfg, dump=()):
    B, T, D, NH, NKV, DH = (cfg[k] for k in ("B", "T", "D", "NH", "NKV", "DH"))
    NS, OWN, KEXT, DK, HC = (cfg[k] for k in ("NSLOT", "OWN", "KEXT", "DK", "HC"))
    GRP, ITERS, NIT, NGK = cfg["GRP"], cfg["ITERS"], cfg["NIT"], cfg["NGK"]
    NPAIR = GRP // 2
    QD = NH * 128
    KD = NKV * 128
    KCH = KEXT // 128
    TCH = T // 128
    NGRP = OWN // 512
    DSETW = min(8, DK)
    NDSET = DK // DSETW
    F2ROT = DSETW <= 4
    HGS = min(8, HC)
    NHG = HC // HGS

    nc = bass.Bass("TRN2", target_bir_lowering=False)
    dram = {}
    for name, shape, dt in [
        ("xqT", [D, OWN], F32), ("xkvT", [D, KEXT], F32),
        ("wq", [D, QD], F32), ("wk", [D, KD], F32), ("wv", [D, KD], F32),
        ("wo", [D, D], F32),
        ("wgT", [D, DH], F32), ("wuT", [D, DH], F32), ("wdT", [DH, D], F32),
        ("cosq", [64, OWN], F32), ("sinq", [64, OWN], F32),
        ("cosk", [64, KEXT], F32), ("sink", [64, KEXT], F32),
        ("kbias", [128, NIT], BF16), ("tri", [128, 128], BF16),
        ("selbuf", [128, 32], BF16), ("onesrow", [1, 128], BF16),
        ("onesrf", [1, 128], F32),
        ("consts", [128, 8], F32),
    ]:
        dram[name] = nc.dram_tensor(name, shape, dt, kind="ExternalInput").ap()
    out = nc.dram_tensor("outT", [D, OWN], F32, kind="ExternalOutput").ap()
    scr_out1 = nc.dram_tensor("scr_out1", [D, OWN], F32,
                              kind="ExternalOutput" if "out1" in dump else "Internal").ap()
    scr_ln2 = nc.dram_tensor("scr_ln2", [D, OWN], BF16,
                             kind="ExternalOutput" if "ln2" in dump else "Internal").ap()
    scr_gated = nc.dram_tensor("scr_gated", [DH, OWN], BF16,
                               kind="ExternalOutput" if "gated" in dump else "Internal").ap()
    scr_wd = nc.dram_tensor("scr_wd", [DH, D], BF16, kind="Internal").ap()
    scr_wq2 = nc.dram_tensor("scr_wq2", [QD // 128, 128, DK * 128], BF16, kind="Internal").ap()
    scr_wo2 = nc.dram_tensor("scr_wo2", [D // 128, 128, DK * 128], BF16, kind="Internal").ap()
    dump_t = {}
    if "pad" in dump:
        for pi in range(2):
            nc.dram_tensor(f"padscr{pi}", [1024, 1024], F32, kind="Internal")
    if "kv" in dump:
        dump_t["kT"] = nc.dram_tensor("dump_kT", [128, NKV * KEXT], BF16, kind="ExternalOutput").ap()
        dump_t["V"] = nc.dram_tensor("dump_V", [128, KCH * KD], BF16, kind="ExternalOutput").ap()
    if "qy" in dump:
        dump_t["qT"] = nc.dram_tensor("dump_qT", [128, NS * NH * 512], BF16, kind="ExternalOutput").ap()
        dump_t["yT"] = nc.dram_tensor("dump_yT", [128, NS * NH * 512], BF16, kind="ExternalOutput").ap()

    es = ExitStack()
    ctx = Ctx(nc)
    ctx.mksem(es, "tm", "sm", "vm", "gm",
              "d_const", "d_xk0", "d_xk1", "d_wkv", "d_cs", "d_cs0", "d_cs1", "d_xq0", "d_xq1",
              "d_wq0", "d_wq1", "d_xr0", "d_xr1", "d_sp0", "d_sp1", "d_wp2a", "d_wp2b", "d_ws2a", "d_ws2b",
              "d_l20", "d_l21", "d_fw0", "d_fw1", "d_wp0", "d_wp1",
              "d_ws0", "d_ws1", "d_ln2ld", "d_gs0", "d_gs1",
              "d_gl0", "d_gl1", "d_gl2", "d_wl0", "d_wl1", "d_wl2", "d_wm0", "d_wm1", "d_wm2", "d_o1ld", "d_o10", "d_o11",
              "d_ou0", "d_ou1", "d_dump")

    def sb(name, shape, dt=F32):
        return es.enter_context(nc.sbuf_tensor(name, shape, dt))

    with es:
        selb = sb("selb", [128, 32], BF16)
        onesrf = sb("onesrf_s", [1, 128])
        trib = sb("trib", [128, 128], BF16)
        kbb = sb("kbb", [128, NIT], BF16)
        cst = sb("cst", [128, 8])

        def CC(i, p=128):
            return cst[0:p, i:i + 1]

        onescol = selb[:, 16:17]

        akv_es = ExitStack()

        def asb0(name, shape, dt=F32):
            return akv_es.enter_context(nc.sbuf_tensor(name, shape, dt))

        kT = asb0("kT", [128, NKV * KEXT], BF16)
        V = asb0("V", [128, KCH * KD], BF16)
        row_m = asb0("row_m", [1, 512])
        row_t = asb0("row_t", [1, 512])
        row_r = asb0("row_r", [1, 512])
        row_mr = asb0("row_mr", [1, 512])
        rsc = asb0("rsc", [128, NKV * KCH])

        def ln_rows2(ctx, psA_ap, psB_ap, m_stats, eps_idx):
            class _W:
                def __init__(self, ap):
                    self.ap = ap

                def __getitem__(self, k):
                    return self.ap
            return ln_rows(ctx, _W(psA_ap), _W(psB_ap), m_stats, eps_idx)

        def ln_rows(ctx, psA, psB, m_stats, eps_idx):
            ctx.op("vector", lambda e: e.tensor_scalar_mul(row_m[:, :], psA[0:1, :], 1.0 / D),
                   waits=[("tm", m_stats)])
            ctx.op("vector", lambda e: e.tensor_mul(row_t[:, :], row_m[:, :], row_m[:, :]))
            ctx.op("vector", lambda e: e.scalar_tensor_tensor(
                row_t[:, :], psB[0:1, :], 1.0 / D, row_t[:, :], ALU.mult, ALU.subtract))
            m_var = ctx.c("vm")
            ctx.op("scalar", lambda e: e.activation(row_t[:, :], row_t[:, :], AF.Ln,
                                                    bias=CC(eps_idx, 1)),
                   waits=[("vm", m_var)])
            ctx.op("scalar", lambda e: e.activation(row_r[:, :], row_t[:, :], AF.Exp,
                                                    scale=CC(3, 1)))
            m_rstd = ctx.c("sm")
            ctx.op("vector", lambda e: e.tensor_mul(row_mr[:, :], row_m[:, :], row_r[:, :]),
                   waits=[("sm", m_rstd)])
            return ctx.c("vm")

        def bcastF(ctx, dst, row_ap, waits):
            ctx.op("tensor", lambda e: e.matmul(dst, onesrf[0:1, :], row_ap,
                                                start=True, stop=True), waits=waits, inc="tm")
            return ctx.c("tm")


        def ph_const(ctx):
            ctx.op("sync", lambda e: e.dma_start(out=selb[:, :], in_=dram["selbuf"][:, :]), inc="d_const", n=16)
            ctx.op("sync", lambda e: e.dma_start(out=onesrf[:, :], in_=dram["onesrf"][:, :]), inc="d_const", n=16)
            ctx.op("sync", lambda e: e.dma_start(out=trib[:, :], in_=dram["tri"][:, :]), inc="d_const", n=16)
            ctx.op("sync", lambda e: e.dma_start(out=kbb[:, :], in_=dram["kbias"][:, :]), inc="d_const", n=16)
            ctx.op("sync", lambda e: e.dma_start(out=cst[:, :], in_=dram["consts"][:, :]), inc="d_const", n=16)
            ctx.wait("sync", [("d_const", 80)])
            ctx.wait("tensor", [("d_const", 80)])
            ctx.wait("vector", [("d_const", 80)])
            ctx.wait("scalar", [("d_const", 80)])

        run_phase(nc, ctx, ph_const)

        with ExitStack() as kv_es:
            def ksb(name, shape, dt=F32):
                return kv_es.enter_context(nc.sbuf_tensor(name, shape, dt))

            def kps(name, shape):
                return kv_es.enter_context(nc.psum_tensor(name, shape, F32))

            wkb = ksb("wkb", [128, DK * KD], BF16)
            wvb = ksb("wvb", [128, DK * KD], BF16)
            xb0 = ksb("xb0", [128, DK * 512], BF16)
            xb1 = ksb("xb1", [128, DK * 512], BF16)
            xsq = ksb("xsq", [128, 1024], BF16)
            tmA = ksb("tmA", [64, 1024])
            tmB = ksb("tmB", [64, 1024])
            negm = ksb("negm", [1, 512], BF16)
            rtc = ksb("rtc", [128, 8])
            wpre0 = ksb("wpre0", [128, 2048], BF16)
            wpre1 = ksb("wpre1", [128, 2048], BF16)
            wpre = [wpre0, wpre1]
            kcolsb = ksb("kcolsb", [1, KD], BF16)
            vcolsb = ksb("vcolsb", [1, KD], BF16)
            csk0 = ksb("csk0", [64, 512])
            snk0 = ksb("snk0", [64, 512])
            csk1 = ksb("csk1", [64, 512])
            snk1 = ksb("snk1", [64, 512])
            rosq = ksb("rosq", [128, 1024], BF16)
            ps_a = kps("ps_a", [1, 512])
            ps_b = kps("ps_b", [1, 512])
            ps_rt = kps("ps_rt", [128, 8])
            ps_K = kps("ps_K", [128, 1024])
            ps_V = kps("ps_V", [128, max(KD, 512)])
            ps_rc = kps("ps_rc", [128, NKV * KCH])
            xbb = [xb0, xb1]
            csk = [csk0, csk1]
            snk = [snk0, snk1]
            NPR = NKV // 2 if NKV >= 2 else 1
            PRW = 2 if NKV >= 2 else 1

            def ph_kv(ctx):
                ctx.op("gp", lambda e: e.dma_start(
                    out=A(wkb, 0, [[DK * KD, 128], [KD, DK], [1, KD]]),
                    in_=A(dram["wk"].tensor, 0, [[KD, 128], [128 * KD, DK], [1, KD]])),
                    inc="d_wkv", n=16)
                ctx.op("gp", lambda e: e.dma_start(
                    out=A(wvb, 0, [[DK * KD, 128], [KD, DK], [1, KD]]),
                    in_=A(dram["wv"].tensor, 0, [[KD, 128], [128 * KD, DK], [1, KD]])),
                    inc="d_wkv", n=16)

                mk = {"stats": 0, "kvdone": 0, "ropeM": {},
                      "cs_v": [0, 0], "mxkv": [("d_xk0", 0), ("d_xk1", 0)], "mcs": [0, 0], "lastLn": 0,
                      "rows_t": 0, "negm_v": 0, "rt_s": 0, "rt_s2": 0, "rms_t": 0}

                CW = min(512, KD)
                for j in range(KD // CW):
                    for dk in range(DK):
                        ctx.op("tensor", lambda e, j=j, dk=dk: e.matmul(
                            ps_a[0:1, 0:CW], onescol, wkb[:, dk * KD + j * CW: dk * KD + (j + 1) * CW],
                            start=(dk == 0), stop=(dk == DK - 1)),
                            waits=([("d_wkv", 32)] if dk == 0 and j == 0 else []))
                        ctx.op("tensor", lambda e, j=j, dk=dk: e.matmul(
                            ps_b[0:1, 0:CW], onescol, wvb[:, dk * KD + j * CW: dk * KD + (j + 1) * CW],
                            start=(dk == 0), stop=(dk == DK - 1)),
                            waits=([("d_wkv", 32)] if dk == 0 and j == 0 else []),
                            inc=("tm" if dk == DK - 1 else None))
                    m_cs0 = ctx.c("tm")
                    ctx.op("scalar", lambda e, j=j: e.activation(
                        kcolsb[0:1, j * CW:(j + 1) * CW], ps_a[0:1, 0:CW], AF.Copy),
                        waits=[("tm", m_cs0)])
                    ctx.op("scalar", lambda e, j=j: e.activation(
                        vcolsb[0:1, j * CW:(j + 1) * CW], ps_b[0:1, 0:CW], AF.Copy))
                mk["colsum_s"] = ctx.c("sm")
                mk["lastLn"] = ctx.c("sm")

                def emit_prep_dma_only(g):
                    gb = g % 2
                    ctx.op("gp", lambda e, g=g, gb=gb: e.dma_start(
                        out=A(xbb[gb], 0, [[DK * 512, 128], [512, DK], [1, 512]]),
                        in_=A(dram["xkvT"].tensor, g * 512,
                              [[KEXT, 128], [128 * KEXT, DK], [1, 512]])),
                        waits=[("tm", mk["kvdone"])],
                        inc=f"d_xk{gb}", n=16)
                    mk["mxkv"][gb] = (f"d_xk{gb}", ctx.c(f"d_xk{gb}"))
                    cb = g % 2
                    ctx.op("sync", lambda e, g=g, cb=cb: e.dma_start(
                        out=csk[cb][:, :], in_=A(dram["cosk"].tensor, g * 512, [[KEXT, 64], [1, 512]])),
                        waits=[("vm", mk["cs_v"][cb])], inc=f"d_cs{cb}", n=16)
                    ctx.op("sync", lambda e, g=g, cb=cb: e.dma_start(
                        out=snk[cb][:, :], in_=A(dram["sink"].tensor, g * 512, [[KEXT, 64], [1, 512]])),
                        inc=f"d_cs{cb}", n=16)
                    mk["mcs"][cb] = ctx.c(f"d_cs{cb}")

                def emit_prep_compute(g):
                    gb = g % 2
                    xb = xbb[gb]
                    m_xkv = mk["mxkv"][gb]
                    for dk in range(DK):
                        ctx.op("tensor", lambda e, dk=dk, xb=xb: e.matmul(
                            ps_a[0:1, :], onescol, xb[:, dk * 512:(dk + 1) * 512],
                            start=(dk == 0), stop=(dk == DK - 1)),
                            waits=([m_xkv, ("vm", mk["rows_t"]),
                                    ("sm", mk["lastLn"])] if dk == 0 else []),
                            inc=("tm" if dk == DK - 1 else None))
                    m_suma = ctx.c("tm")
                    msq = {}
                    mmb = {-2: mk["stats"], -1: mk["stats"]}
                    for dk in range(DK):
                        ctx.op("scalar", lambda e, dk=dk, xb=xb: e.activation(
                            xsq[:, (dk % 2) * 512:(dk % 2) * 512 + 512],
                            xb[:, dk * 512:(dk + 1) * 512], AF.Square),
                            waits=([m_xkv] if dk == 0 else []) +
                                  [("tm", mmb[dk - 2])])
                        msq[dk] = ctx.c("sm")
                        ctx.op("tensor", lambda e, dk=dk: e.matmul(
                            ps_b[0:1, :], onescol, xsq[:, (dk % 2) * 512:(dk % 2) * 512 + 512],
                            start=(dk == 0), stop=(dk == DK - 1)),
                            waits=[("sm", msq[dk])], inc="tm")
                        mmb[dk] = ctx.c("tm")
                    mk["stats"] = ctx.c("tm")
                    ctx.op("vector", lambda e: e.tensor_scalar_mul(negm[:, :], ps_a[0:1, :], -1.0 / D),
                           waits=[("tm", m_suma), ("tm", mk["rank1_t"] if "rank1_t" in mk else 0)])
                    mk["negm_v"] = ctx.c("vm")
                    ctx.op("vector", lambda e: e.tensor_scalar_mul(row_m[:, :], ps_a[0:1, :], 1.0 / D))
                    ctx.op("vector", lambda e: e.tensor_mul(row_t[:, :], row_m[:, :], row_m[:, :]))
                    ctx.op("vector", lambda e: e.scalar_tensor_tensor(
                        row_t[:, :], ps_b[0:1, :], 1.0 / D, row_t[:, :], ALU.mult, ALU.subtract),
                        waits=[("tm", mk["stats"])])
                    m_var = ctx.c("vm")
                    ctx.op("scalar", lambda e: e.activation(row_t[:, :], row_t[:, :], AF.Ln,
                                                            bias=CC(0, 1)),
                           waits=[("vm", m_var)])
                    ctx.op("scalar", lambda e: e.activation(row_r[:, :], row_t[:, :], AF.Exp,
                                                            scale=CC(3, 1)))
                    mk["lastLn"] = ctx.c("sm")
                    m_rstd = ctx.c("sm")
                    for ts in range(4):
                        ctx.op("tensor", lambda e, ts=ts: e.matmul(
                            ps_rt[:, ts:ts + 1], row_r[0:1, ts * 128:(ts + 1) * 128],
                            cst[0:1, 6:7], start=True, stop=True),
                            waits=([("sm", m_rstd), ("sm", mk["rt_s"])] if ts == 0 else []),
                            inc=("tm" if ts == 3 else None))
                    m_rtt = ctx.c("tm")
                    ctx.op("scalar", lambda e: e.activation(rtc[:, 0:4], ps_rt[:, 0:4], AF.Copy),
                           waits=[("tm", m_rtt)])
                    mk["rt_s2"] = ctx.c("sm")
                    mk["rows_t"] = ctx.c("vm")
                    mk["rt_t"] = ctx.c("tm")

                def emit_kproj(g, p):
                    gb = g % 2
                    xb = xbb[gb]
                    m_xkv = mk["mxkv"][gb]
                    for pi in range(PRW):
                        kc = p * PRW + pi
                        for dk in range(DK):
                            w = []
                            if dk == 0:
                                w = [("d_wkv", 32), m_xkv,
                                     ("vm", mk["ropeM"].get(p - 1, 0))]
                            ctx.op("tensor", lambda e, kc=kc, pi=pi, dk=dk, xb=xb: e.matmul(
                                ps_K[:, pi * 512:(pi + 1) * 512],
                                wkb[:, dk * KD + kc * 128: dk * KD + (kc + 1) * 128],
                                xb[:, dk * 512:(dk + 1) * 512],
                                start=(dk == 0), stop=False), waits=w)
                        ctx.op("tensor", lambda e, kc=kc, pi=pi: e.matmul(
                            ps_K[:, pi * 512:(pi + 1) * 512],
                            kcolsb[0:1, kc * 128:(kc + 1) * 128], negm[0:1, :],
                            start=False, stop=True),
                            waits=[("vm", mk["negm_v"]), ("sm", mk["colsum_s"])],
                            inc=("tm" if pi == PRW - 1 else None))
                    return ctx.c("tm")

                def emit_vproj(g, ts, m_evprev):
                    gb = g % 2
                    xb = xbb[gb]
                    posv = g * 4 + ts
                    for dk in range(DK):
                        w = []
                        if dk == 0:
                            w = [("sm", m_evprev), ("d_wkv", 32)]
                        ctx.op("tensor", lambda e, ts=ts, dk=dk, xb=xb: e.matmul(
                            ps_V[:, 0:KD],
                            xb[:, dk * 512 + ts * 128: dk * 512 + ts * 128 + 128],
                            wvb[:, dk * KD:(dk + 1) * KD],
                            start=(dk == 0), stop=False), waits=w)
                    ctx.op("tensor", lambda e, ts=ts: e.matmul(
                        ps_V[:, 0:KD], negm[0:1, ts * 128:(ts + 1) * 128],
                        vcolsb[0:1, 0:KD],
                        start=False, stop=True),
                        waits=[("vm", mk["negm_v"]), ("sm", mk["colsum_s"]),
                               ("tm", mk["rt_t"])],
                        inc="tm")
                    m_V = ctx.c("tm")
                    ctx.op("scalar", lambda e, posv=posv, ts=ts: e.activation(
                        V[:, posv * KD:(posv + 1) * KD], ps_V[:, 0:KD], AF.Copy,
                        scale=rtc[:, ts:ts + 1]),
                        waits=[("tm", m_V), ("sm", mk["rt_s2"])])
                    mk["rt_s"] = mk["rt_s2"]
                    return ctx.c("sm")

                def rope_pair(g, p, m_K):
                    cb = g % 2
                    kv0 = p * PRW
                    x1 = A(ps_K, 0, [[1024, 64], [512, PRW], [1, 512]])
                    x2 = A(ps_K, 64 * 1024, [[1024, 64], [512, PRW], [1, 512]])
                    cc = A(csk[cb], 0, [[512, 64], [0, PRW], [1, 512]])
                    ss = A(snk[cb], 0, [[512, 64], [0, PRW], [1, 512]])
                    tA = A(tmA, 0, [[1024, 64], [512, PRW], [1, 512]])
                    tB = A(tmB, 0, [[1024, 64], [512, PRW], [1, 512]])
                    ktlo = A(kT, kv0 * KEXT + g * 512, [[KEXT * NKV, 64], [KEXT, PRW], [1, 512]])
                    kthi = A(kT, 64 * KEXT * NKV + kv0 * KEXT + g * 512,
                             [[KEXT * NKV, 64], [KEXT, PRW], [1, 512]])
                    ctx.op("vector", lambda e: e.tensor_tensor(tA, x1, cc, ALU.mult),
                           waits=[("tm", m_K), (f"d_cs{cb}", mk["mcs"][cb])])
                    ctx.op("vector", lambda e: e.tensor_tensor(tB, x2, ss, ALU.mult))
                    ctx.op("vector", lambda e: e.tensor_tensor(ktlo, tA, tB, ALU.add))
                    ctx.op("vector", lambda e: e.tensor_tensor(tA, x2, cc, ALU.mult))
                    ctx.op("vector", lambda e: e.tensor_tensor(tB, x1, ss, ALU.mult))
                    ctx.op("vector", lambda e: e.tensor_tensor(kthi, tA, tB, ALU.subtract))
                    mk["ropeM"][p] = ctx.c("vm")
                    mk["cs_v"][cb] = ctx.c("vm")
                    m_kt = ctx.c("vm")
                    ctx.op("scalar", lambda e, kv0=kv0, g=g: e.activation(
                        A(rosq, 0, [[1024, 128], [512, PRW], [1, 512]]),
                        A(kT, kv0 * KEXT + g * 512, [[KEXT * NKV, 128], [KEXT, PRW], [1, 512]]),
                        AF.Square),
                        waits=[("vm", m_kt), ("tm", mk["rms_t"])])
                    return ctx.c("sm")

                def rms_cols(g, p, m_rosq):
                    kv0 = p * PRW
                    for pi in range(PRW):
                        kv = kv0 + pi
                        for ts in range(4):
                            col = kv * KCH + g * 4 + ts
                            ctx.op("tensor", lambda e, pi=pi, ts=ts, col=col: e.matmul(
                                ps_rc[:, col:col + 1],
                                rosq[:, pi * 512 + ts * 128: pi * 512 + ts * 128 + 128],
                                onescol, start=True, stop=True),
                                waits=([("sm", m_rosq)] if (pi == 0 and ts == 0) else []),
                                inc=("tm" if (pi == PRW - 1 and ts == 3) else None))
                    mk["rms_t"] = ctx.c("tm")
                    return ctx.c("tm")

                def compute(g):
                    mk["ropeM"] = {}
                    mK = {}
                    rq = {}
                    mK[0] = emit_kproj(g, 0)
                    m_ev = ctx.c("sm")
                    rq[0] = rope_pair(g, 0, mK[0])
                    vts = 0
                    for p in range(NPR):
                        for _ in range(4 // NPR):
                            m_ev = emit_vproj(g, vts, m_ev)
                            vts += 1
                        rms_cols(g, p, rq[p])
                        if p + 1 < NPR:
                            mK[p + 1] = emit_kproj(g, p + 1)
                            rq[p + 1] = rope_pair(g, p + 1, mK[p + 1])
                    while vts < 4:
                        m_ev = emit_vproj(g, vts, m_ev)
                        vts += 1
                    mk["rank1_t"] = ctx.c("tm")
                    mk["kvdone"] = ctx.c("tm")

                pre_jobs = ([("wq", scr_wq2, i) for i in range(DK)] +
                            [("wo", scr_wo2, i) for i in range(DK)])
                mk["pre_i"] = 0

                def emit_precast(n):
                    for _ in range(n):
                        i = mk["pre_i"]
                        if i >= len(pre_jobs):
                            return
                        mk["pre_i"] += 1
                        name, scr, dk = pre_jobs[i]
                        pb = i % 2
                        ps2 = "ab"[pb]
                        W = QD if name == "wq" else D
                        ctx.op("gp", lambda e, name=name, dk=dk, pb=pb, W=W: e.dma_start(
                            out=wpre[pb][:, 0:W],
                            in_=A(dram[name].tensor, dk * 128 * W, [[W, 128], [1, W]])),
                            waits=[(f"d_ws2{ps2}", ctx.c(f"d_ws2{ps2}"))],
                            inc=f"d_wp2{ps2}", n=16)
                        m_l = ctx.c(f"d_wp2{ps2}")
                        ctx.op("sync", lambda e, scr=scr, dk=dk, pb=pb, W=W: e.dma_start(
                            out=A(scr.tensor, dk * 128, [[D, 128], [128 * D, W // 128], [1, 128]]),
                            in_=A(wpre[pb], 0, [[2048, 128], [128, W // 128], [1, 128]])),
                            waits=[(f"d_wp2{ps2}", m_l)], inc=f"d_ws2{ps2}", n=16)

                emit_prep_dma_only(0)
                emit_prep_compute(0)
                for g in range(NGK):
                    if g + 1 < NGK:
                        emit_prep_dma_only(g + 1)
                    emit_precast(3)
                    compute(g)
                    if g + 1 < NGK:
                        emit_prep_compute(g + 1)
                emit_precast(len(pre_jobs))
                ctx.wait("sync", [("d_ws2a", ctx.c("d_ws2a")), ("d_ws2b", ctx.c("d_ws2b"))])

                m_cols = ctx.c("tm")
                ctx.op("scalar", lambda e: e.activation(
                    A(rsc, 0, [[NKV * KCH, 128], [1, NKV * KCH]]),
                    ps_rc[:, :], AF.Ln, bias=CC(1), scale=CC(2)),
                    waits=[("tm", m_cols)])
                ctx.op("scalar", lambda e: e.activation(
                    A(rsc, 0, [[NKV * KCH, 128], [1, NKV * KCH]]),
                    A(rsc, 0, [[NKV * KCH, 128], [1, NKV * KCH]]),
                    AF.Exp, scale=CC(3), bias=CC(5)))
                mk["lastLn"] = ctx.c("sm")

                if "kv" in dump:
                    mv = ctx.c("vm")
                    ctx.op("sync", lambda e: e.dma_start(out=dump_t["kT"][:, :], in_=kT[:, :]),
                           waits=[("vm", mv)], inc="d_dump", n=16)
                    ctx.op("sync", lambda e: e.dma_start(out=dump_t["V"][:, :], in_=V[:, :]),
                           inc="d_dump", n=16)
                    ctx.wait("sync", [("d_dump", ctx.c("d_dump"))])

            run_phase(nc, ctx, ph_kv)

        with ExitStack() as at_es:
            def asb(name, shape, dt=F32):
                return at_es.enter_context(nc.sbuf_tensor(name, shape, dt))

            def aps(name, shape):
                return at_es.enter_context(nc.psum_tensor(name, shape, F32))

            qT = asb("qT", [128, NH * 512], BF16)
            xr4v = qT.bitcast(F32)
            lyT = asb("lyT", [128, max(NH, DK) * 512], BF16)
            ln1q = lyT
            yT = lyT
            xbq0 = asb("xbq0", [128, DK * 512], BF16)
            xbq1 = asb("xbq1", [128, DK * 512], BF16)
            xbqs = [xbq0, xbq1]
            xsq2 = asb("xsq2", [128, 512], BF16)
            probs0 = asb("probs0", [128, 1024], BF16)
            probs1 = asb("probs1", [128, 1024], BF16)
            reca = row_t
            recb = row_r
            wqb0 = asb("wqb0", [128, DK * 128], BF16)
            wqb1 = asb("wqb1", [128, DK * 128], BF16)
            csq = asb("csq", [64, 512])
            snq = asb("snq", [64, 512])
            roq0 = asb("roq0", [128, 512])
            roq1 = asb("roq1", [128, 512])
            tmpq = asb("tmpq", [128, 512])
            rowr1 = asb("rowr1", [1, 512])
            lnt2 = asb("lnt2", [128, 512])
            negmq = asb("negmq", [1, 512], BF16)
            qcolsb = asb("qcolsb", [1, QD], BF16)
            o1f0 = asb("o1f0", [128, 512])
            o1f1 = asb("o1f1", [128, 512])
            roqs = [roq0, roq1]
            rowrs = [row_r, rowr1]
            pS0 = aps("pS0", [128, 1024])
            pS1 = aps("pS1", [128, 1024])
            pY = aps("pY", [128, 1024])
            pR = aps("pR", [128, 1024])

            probs = [probs0, probs1]
            pS = [pS0, pS1]
            wqb = [wqb0, wqb1]
            o1f = [o1f0, o1f1]

            def ph_slots(ctx):
                ctx.barrier()
                mk = {"q_v": 0, "wq_t": [0, 0], "yev": 0, "avp": 0,
                      "xres_v": [0, 0], "o1_t": 0, "pexp": [0, 0], "msc": {},
                      "qcol_s": 0, "qS_s": 0, "qpfree": 0, "pSfree_s": 0, "pSfree_v": 0,
                      "norm_fn": None, "normv": 0, "store_pend": None, "store_fn": None,
                      "xbq_v": [0, 0], "xbq_t": [0, 0],
                      "m_xq": [0, 0], "pRfree": 0}
                def xq_load(slot):
                    sb2 = slot % 2
                    ctx.op("gp", lambda e, slot=slot, sb2=sb2: e.dma_start(
                        out=A(xbqs[sb2], 0, [[DK * 512, 128], [512, DK], [1, 512]]),
                        in_=A(dram["xqT"].tensor, slot * 512,
                              [[OWN, 128], [128 * OWN, DK], [1, 512]])),
                        waits=[("vm", mk["xbq_v"][sb2]), ("tm", mk["xbq_t"][sb2])],
                        inc=f"d_xq{sb2}", n=16)
                    mk["m_xq"][sb2] = ctx.c(f"d_xq{sb2}")

                xq_load(0)
                for slot in range(NS):
                    uslot = sum(ITERS[:slot])
                    sb2 = slot % 2
                    xbq = xbqs[sb2]
                    m_xq0 = (f"d_xq{sb2}", mk["m_xq"][sb2])
                    m_xq = mk["m_xq"][sb2]
                    ctx.op("sync", lambda e, slot=slot: e.dma_start(
                        out=csq[:, :], in_=A(dram["cosq"].tensor, slot * 512, [[OWN, 64], [1, 512]])),
                        waits=[("vm", mk["q_v"])], inc="d_cs", n=16)
                    ctx.op("sync", lambda e, slot=slot: e.dma_start(
                        out=snq[:, :], in_=A(dram["sinq"].tensor, slot * 512, [[OWN, 64], [1, 512]])),
                        inc="d_cs", n=16)
                    m_cs = ctx.c("d_cs")
                    for dk in range(DK):
                        ctx.op("tensor", lambda e, dk=dk: e.matmul(
                            pS1[0:1, 0:512], onescol, xbq[:, dk * 512:(dk + 1) * 512],
                            start=(dk == 0), stop=(dk == DK - 1)),
                            waits=([m_xq0, ("sm", mk["pSfree_s"]),
                                    ("vm", mk["pSfree_v"])] if dk == 0 else []),
                            inc=("tm" if dk == DK - 1 else None))
                    m_stats = ctx.c("tm")
                    ctx.op("vector", lambda e: e.tensor_scalar_mul(negmq[:, :], pS1[0:1, 0:512], -1.0 / D),
                           waits=[("tm", m_stats)])
                    m_negmq = ctx.c("vm")
                    if mk.get("norm_fn"):
                        fn = mk["norm_fn"]
                        mk["norm_fn"] = None
                        fn(ctx)

                    mq = {"proj": {}, "ropeM": {}, "exp": {}, "hilo": {}, "ktm": {}}

                    def q_P(qc):
                        buf = qc % 2
                        ctx.op("sync" if buf == 0 else "gp", lambda e, qc=qc, buf=buf: e.dma_start(
                            out=A(wqb[buf], 0, [[DK * 128, 128], [1, DK * 128]]),
                            in_=A(scr_wq2.tensor, qc * 128 * D, [[D, 128], [1, D]])),
                            waits=[("tm", mk["wq_t"][buf])], inc=f"d_wq{buf}", n=16)
                        m_wq = ctx.c(f"d_wq{buf}")
                        if slot == 0:
                            for dk in range(DK):
                                ctx.op("tensor", lambda e, dk=dk, buf=buf: e.matmul(
                                    pR[0:1, 128:256], onescol,
                                    wqb[buf][:, dk * 128:(dk + 1) * 128],
                                    start=(dk == 0), stop=(dk == DK - 1)),
                                    waits=([(f"d_wq{buf}", m_wq),
                                            ("sm", mk["qcol_s"])] if dk == 0 else []),
                                    inc=("tm" if dk == DK - 1 else None))
                            m_qc = ctx.c("tm")
                            ctx.op("scalar", lambda e, qc=qc: e.activation(
                                qcolsb[0:1, qc * 128:(qc + 1) * 128], pR[0:1, 128:256], AF.Copy),
                                waits=[("tm", m_qc)])
                            mk["qcol_s"] = ctx.c("sm")
                        qp = pS0[:, buf * 512:buf * 512 + 512]
                        for dk in range(DK):
                            w = []
                            if dk == 0:
                                w = [(f"d_wq{buf}", m_wq), m_xq0,
                                     ("vm", mq["ropeM"].get(qc - 2, 0)),
                                     ("sm", max(mq["exp"].get(qc - 2, 0),
                                                mk["pSfree_s"] if qc < 2 else 0)),
                                     ("vm", mk["pSfree_v"] if qc < 2 else 0)]
                            ctx.op("tensor", lambda e, qc=qc, dk=dk, qp=qp, buf=buf: e.matmul(
                                qp, wqb[buf][:, dk * 128:(dk + 1) * 128],
                                xbq[:, dk * 512:(dk + 1) * 512],
                                start=(dk == 0), stop=False), waits=w)
                        ctx.op("tensor", lambda e, qc=qc, qp=qp: e.matmul(
                            qp, qcolsb[0:1, qc * 128:(qc + 1) * 128], negmq[0:1, :],
                            start=False, stop=True),
                            waits=[("vm", m_negmq), ("sm", mk["qcol_s"])], inc="tm")
                        mk["wq_t"][buf] = ctx.c("tm")
                        mq["proj"][qc] = ctx.c("tm")

                    def q_R(qc):
                        qp = pS0[:, (qc % 2) * 512:(qc % 2) * 512 + 512]
                        ro = roqs[qc % 2]
                        x1 = qp[0:64, :]
                        x2 = qp[64:128, :]
                        ctx.op("vector", lambda e, x1=x1, ro=ro: e.tensor_mul(ro[0:64, :], x1, csq[:, :]),
                               waits=[("tm", mq["proj"][qc]), ("d_cs", m_cs)])
                        ctx.op("vector", lambda e, x2=x2: e.tensor_mul(tmpq[0:64, :], x2, snq[:, :]))
                        ctx.op("vector", lambda e, ro=ro: e.tensor_add(ro[0:64, :], ro[0:64, :], tmpq[0:64, :]))
                        ctx.op("vector", lambda e, x2=x2, ro=ro: e.tensor_mul(ro[64:128, :], x2, csq[:, :]))
                        ctx.op("vector", lambda e, x1=x1: e.tensor_mul(tmpq[64:128, :], x1, snq[:, :]))
                        ctx.op("vector", lambda e, ro=ro: e.tensor_sub(ro[64:128, :], ro[64:128, :], tmpq[64:128, :]))
                        mq["ropeM"][qc] = ctx.c("vm")

                    def q_S(qc):
                        ro = roqs[qc % 2]
                        ctx.op("scalar", lambda e, ro=ro: e.activation(xsq2[:, :], ro[:, :], AF.Square),
                               waits=[("vm", mq["ropeM"][qc]), ("tm", ctx.c("tm"))])
                        m_rosq = ctx.c("sm")
                        ctx.op("tensor", lambda e: e.matmul(pS1[0:1, 512:1024], onescol, xsq2[:, :],
                                                            start=True, stop=True),
                               waits=[("sm", m_rosq), ("vm", m_negmq if qc == 0 else 0)], inc="tm")
                        m_rms = ctx.c("tm")
                        ctx.op("scalar", lambda e: e.activation(row_t[:, :], pS1[0:1, 512:1024], AF.Ln,
                                                                bias=CC(1, 1), scale=CC(2, 1)),
                               waits=[("tm", m_rms)])
                        ctx.op("scalar", lambda e, qc=qc: e.activation(rowrs[qc % 2][:, :], row_t[:, :],
                                                                       AF.Exp, scale=CC(3, 1)),
                               waits=[("tm", mq["ktm"].get(qc - 2, 0))])
                        mq["exp"][qc] = ctx.c("sm")

                    def q_F(qc):
                        ro = roqs[qc % 2]
                        m_rqb = bcastF(ctx, pY[:, 0:512], rowrs[qc % 2][0:1, :],
                                       [("sm", mq["exp"][qc]),
                                        ("vm", mk["yev"] if qc == 0 else mq["hilo"].get(qc - 1, 0))])
                        mq["ktm"][qc] = m_rqb
                        ctx.op("vector", lambda e, qc=qc, ro=ro: e.tensor_mul(
                            qT[:, qc * 512:(qc + 1) * 512], ro[:, :], pY[:, 0:512]),
                            waits=[("tm", m_rqb),
                                   ("d_xr0", ctx.c("d_xr0")), ("d_xr1", ctx.c("d_xr1"))])
                        mq["hilo"][qc] = ctx.c("vm")

                    q_P(0)
                    if NH > 1:
                        q_P(1)
                    q_R(0)
                    q_S(0)
                    for qc in range(NH):
                        if qc + 2 < NH:
                            q_P(qc + 2)
                        if qc + 1 < NH:
                            q_R(qc + 1)
                            q_S(qc + 1)
                        q_F(qc)
                    m_qT = ctx.c("vm")
                    mk["qS_s"] = ctx.c("sm")
                    if mk.get("store_pend"):
                        mk["store_fn"](ctx)
                    if slot + 1 < NS:
                        xq_load(slot + 1)

                    iters = ITERS[slot]
                    plains = iters - 4
                    for kv in range(NKV):
                        for pair in range(NPAIR):
                            h0 = kv * GRP + pair * 2
                            m_exp = {-2: mk["pexp"][0], -1: mk["pexp"][1]}
                            m_av_h = {-2: mk["avp"], -1: mk["avp"]}

                            def geo(it):
                                diag = it >= plains
                                r = it - plains if diag else 0
                                return (it % 2, diag, 128 * r,
                                        it if not diag else TCH + slot * 4 + r)

                            def emit_scores(it):
                                g2, diag, noff, pos = geo(it)
                                w_sc = [("sm", m_exp.get(it - 2, 0))]
                                if it == 0:
                                    w_sc.append(("vm", m_qT))
                                    w_sc.append(("sm", mk["qS_s"]))
                                for h in range(2):
                                    ctx.op("tensor", lambda e, pos=pos, h=h,
                                           noff=noff, g2=g2: e.matmul(
                                        pS[g2][:, h * 512 + noff: (h + 1) * 512],
                                        kT[:, kv * KEXT + pos * 128: kv * KEXT + (pos + 1) * 128],
                                        qT[:, (h0 + h) * 512 + noff: (h0 + h + 1) * 512],
                                        start=True, stop=True),
                                        waits=(w_sc if h == 0 else []),
                                        inc=("tm" if h == 1 else None))
                                m_sc = ctx.c("tm")
                                m_mask = 0
                                if diag:
                                    ctx.op("vector", lambda e, g2=g2, noff=noff: e.tensor_tensor(
                                        A(pS[g2], noff, [[1024, 128], [512, 2], [1, 128]]),
                                        A(pS[g2], noff, [[1024, 128], [512, 2], [1, 128]]),
                                        A(trib, 0, [[128, 128], [0, 2], [1, 128]]),
                                        ALU.add),
                                        waits=[("tm", m_sc)])
                                    m_mask = ctx.c("vm")
                                mk["msc"][it] = (m_sc, m_mask)

                            def emit_exp(it):
                                g2, diag, noff, pos = geo(it)
                                u = uslot + it
                                m_sc, m_mask = mk["msc"][it]
                                wexp = [("tm", m_sc), ("tm", m_av_h.get(it - 2, 0)),
                                        ("vm", m_mask)]
                                rcol = kv * KCH + pos
                                ctx.op("scalar", lambda e, g2=g2, noff=noff, u=u,
                                       rcol=rcol: e.activation(
                                    A(probs[g2], noff, [[1024, 128], [512, 2], [1, 512 - noff]]),
                                    A(pS[g2], noff, [[1024, 128], [512, 2], [1, 512 - noff]]),
                                    AF.Exp, bias=kbb[:, u:u + 1], scale=rsc[:, rcol:rcol + 1]),
                                    waits=wexp)
                                m_exp[it] = ctx.c("sm")

                            def emit_sumav(it):
                                g2, diag, noff, pos = geo(it)
                                for h in range(2):
                                    w = [("sm", m_exp[it])] if h == 0 else []
                                    if it == 0 and h == 0:
                                        w.append(("vm", max(mk["yev"], mk["pRfree"])))
                                    ctx.op("tensor", lambda e, h=h, g2=g2, noff=noff,
                                           it=it: e.matmul(
                                        pR[0:1, h * 512 + noff:h * 512 + 512], onescol,
                                        probs[g2][:, h * 512 + noff:(h + 1) * 512],
                                        start=(it == 0), stop=(it == iters - 1)),
                                        waits=w)
                                for h in range(2):
                                    ctx.op("tensor", lambda e, h=h, g2=g2, noff=noff,
                                           pos=pos, it=it: e.matmul(
                                        pY[:, h * 512 + noff:(h + 1) * 512],
                                        V[:, pos * KD + kv * 128: pos * KD + (kv + 1) * 128],
                                        probs[g2][:, h * 512 + noff:(h + 1) * 512],
                                        start=(it == 0), stop=(it == iters - 1)),
                                        inc=("tm" if h == 1 else None))
                                m_av_h[it] = ctx.c("tm")

                            mk["msc"] = {}
                            for it2 in range(0, iters, 2):
                                emit_scores(it2)
                                emit_exp(it2)
                                emit_scores(it2 + 1)
                                if it2 >= 1:
                                    emit_sumav(it2 - 1)
                                emit_exp(it2 + 1)
                                emit_sumav(it2)
                            emit_sumav(iters - 1)
                            m_av = ctx.c("tm")
                            mk["pexp"] = [m_exp[iters - 2], m_exp[iters - 1]]
                            mk["avp"] = m_av
                            ctx.op("scalar", lambda e: e.activation(reca[:, :], pR[0:1, 0:512],
                                                                    AF.Ln),
                                   waits=[("tm", m_av)])
                            ctx.op("scalar", lambda e: e.activation(reca[:, :], reca[:, :],
                                                                    AF.Exp, scale=CC(7, 1)))
                            ctx.op("scalar", lambda e: e.activation(recb[:, :], pR[0:1, 512:1024],
                                                                    AF.Ln))
                            ctx.op("scalar", lambda e: e.activation(recb[:, :], recb[:, :],
                                                                    AF.Exp, scale=CC(7, 1)))
                            m_rec = ctx.c("sm")
                            ctx.op("tensor", lambda e: e.matmul(pR[:, 0:512], onesrf[0:1, :], reca[0:1, :],
                                                                start=True, stop=True),
                                   waits=[("sm", m_rec)])
                            ctx.op("tensor", lambda e: e.matmul(pR[:, 512:1024], onesrf[0:1, :], recb[0:1, :],
                                                                start=True, stop=True), inc="tm")
                            m_rb = ctx.c("tm")
                            ctx.op("vector", lambda e: e.tensor_copy(roq0[:, :], pR[:, 0:512]),
                                   waits=[("tm", m_rb)])
                            ctx.op("vector", lambda e: e.tensor_copy(tmpq[:, :], pR[:, 512:1024]))
                            rbsh = [roq0, tmpq]
                            for h in range(2):
                                ctx.op("vector", lambda e, h=h, h0=h0: e.tensor_mul(
                                    yT[:, (h0 + h) * 512:(h0 + h + 1) * 512],
                                    pY[:, h * 512:(h + 1) * 512], rbsh[h][:, :]),
                                    waits=[("d_l20", ctx.c("d_l20"))] if (kv == 0 and pair == 0 and h == 0) else [])
                            mk["yev"] = ctx.c("vm")

                    m_yT = ctx.c("vm")
                    mk["pSfree_s"] = ctx.c("sm")
                    mk["pSfree_v"] = ctx.c("vm")
                    if "qy" in dump:
                        ctx.op("sync", lambda e, slot=slot: e.dma_start(
                            out=A(dump_t["qT"].tensor, slot * NH * 512,
                                  [[NS * NH * 512, 128], [1, NH * 512]]),
                            in_=qT[:, :]), waits=[("vm", m_yT)], inc="d_dump", n=16)
                        ctx.op("sync", lambda e, slot=slot: e.dma_start(
                            out=A(dump_t["yT"].tensor, slot * NH * 512,
                                  [[NS * NH * 512, 128], [1, NH * 512]]),
                            in_=yT[:, :]), inc="d_dump", n=16)

                    m_add = {}
                    m_o1b = {}
                    m_xr4 = {}
                    for dk in range(DK):
                        buf = dk % 2
                        NXR = min(4, (NH * 256) // 1024)
                        HWID = NXR * 512
                        g4 = dk // NXR
                        if dk % NXR == 0:
                            xb4 = g4 % 2
                            ctx.op("sync", lambda e, g4=g4, xb4=xb4, slot=slot: e.dma_start(
                                out=A(xr4v, xb4 * HWID, [[NH * 256, 128], [512, NXR], [1, 512]]),
                                in_=A(dram["xqT"].tensor, g4 * NXR * 128 * OWN + slot * 512,
                                      [[OWN, 128], [128 * OWN, NXR], [1, 512]])),
                                waits=[("vm", max(mk["xres_v"][xb4], mk["yev"])),
                                       ("tm", mk["avp"] if g4 < 2 else 0),
                                       (f"d_xr{xb4}", ctx.c(f"d_xr{xb4}"))],
                                inc=f"d_xr{xb4}", n=16)
                            m_xr4[g4] = ctx.c(f"d_xr{xb4}")
                        ctx.op("sync" if buf == 0 else "gp", lambda e, dk=dk, buf=buf: e.dma_start(
                            out=A(wqb[buf], 0, [[DK * 128, 128], [1, DK * 128]]),
                            in_=A(scr_wo2.tensor, dk * 128 * D, [[D, 128], [1, D]])),
                            waits=[("tm", mk["wq_t"][buf])], inc=f"d_wq{buf}", n=16)
                        m_wo = ctx.c(f"d_wq{buf}")
                        m_xr = m_xr4[g4]
                        pO = pY[:, buf * 512:buf * 512 + 512]
                        for yf in range(NH):
                            w = []
                            if yf == 0:
                                w = [(f"d_wq{buf}", m_wo),
                                     ("vm", m_yT if dk < 2 else m_add[dk - 2])]
                            ctx.op("tensor", lambda e, dk=dk, yf=yf, pO=pO, buf=buf: e.matmul(
                                pO, wqb[buf][:, yf * 128:(yf + 1) * 128],
                                yT[:, yf * 512:(yf + 1) * 512],
                                start=(yf == 0), stop=(yf == NH - 1)), waits=w,
                                inc=("tm" if yf == NH - 1 else None))
                        mk["wq_t"][buf] = ctx.c("tm")
                        m_o1 = ctx.c("tm")
                        ctx.op("vector", lambda e, buf=buf, pO=pO, dk=dk, g4=g4: e.tensor_add(
                            o1f[buf][:, :], pO,
                            xr4v[:, (g4 % 2) * HWID + (dk % NXR) * 512:(g4 % 2) * HWID + (dk % NXR) * 512 + 512]),
                            waits=[("tm", m_o1), (f"d_xr{g4 % 2}", m_xr),
                                   (f"d_sp{buf}", ctx.c(f"d_sp{buf}")),
                                   ("sm", m_o1b.get(dk - 2, 0))])
                        mk["xres_v"][g4 % 2] = ctx.c("vm")
                        m_add[dk] = ctx.c("vm")
                        ctx.op("sync", lambda e, buf=buf, dk=dk, slot=slot: e.dma_start(
                            out=A(scr_out1.tensor, dk * 128 * OWN + slot * 512,
                                  [[OWN, 128], [1, 512]]),
                            in_=o1f[buf][:, :]),
                            waits=[("vm", m_add[dk])], inc=f"d_sp{buf}", n=16)
                        ctx.op("scalar", lambda e, buf=buf, dk=dk: e.activation(
                            xbq[:, dk * 512:(dk + 1) * 512], o1f[buf][:, :], AF.Copy),
                            waits=[("vm", m_add[dk])])
                        m_o1b[dk] = ctx.c("sm")
                    xsqw = lnt2.bitcast(BF16)
                    nb = 2
                    SQW = nb * 512
                    m_prb2 = 0
                    for hb in range(DK // nb):
                        ctx.op("scalar", lambda e, hb=hb, SQW=SQW: e.activation(
                            xsqw[:, 0:SQW], xbq[:, hb * SQW:(hb + 1) * SQW], AF.Square),
                            waits=[("sm", m_o1b[min(DK - 1, (hb + 1) * nb - 1)]), ("tm", m_prb2)]
                                  + ([("vm", mk["normv"])] if hb == 0 else []))
                        m_sq2 = ctx.c("sm")
                        for j in range(nb):
                            dk = hb * nb + j
                            ctx.op("tensor", lambda e, dk=dk: e.matmul(
                                pR[0:1, 0:512], onescol, xbq[:, dk * 512:(dk + 1) * 512],
                                start=(dk == 0), stop=(dk == DK - 1)),
                                waits=[("sm", m_o1b[dk])])
                            ctx.op("tensor", lambda e, j=j, dk=dk: e.matmul(
                                pR[0:1, 512:1024], onescol, xsqw[:, j * 512:(j + 1) * 512],
                                start=(dk == 0), stop=(dk == DK - 1)),
                                waits=([("sm", m_sq2)] if j == 0 else []),
                                inc=("tm" if j == nb - 1 else None))
                        m_prb2 = ctx.c("tm")
                    m_stats2 = ctx.c("tm")
                    mk["o1_t"] = m_stats2
                    m_rows3 = ln_rows2(ctx, pR[0:1, 0:512], pR[0:1, 512:1024], m_stats2, 0)
                    m_bc3 = bcastF(ctx, pR[:, 0:512], row_r[0:1, :], [("vm", m_rows3)])
                    m_bc4 = bcastF(ctx, pR[:, 512:1024], row_mr[0:1, :], [("vm", m_rows3)])

                    def emit_norm(ctx, xbq=xbq, m_bc3=m_bc3, m_bc4=m_bc4, slot=slot, sb2=sb2):
                        for hb in range(DK):
                            ctx.op("vector", lambda e, hb=hb, xbq=xbq: e.tensor_tensor(
                                lnt2[:, :], xbq[:, hb * 512:(hb + 1) * 512],
                                pR[:, 0:512], ALU.mult),
                                waits=([("tm", m_bc3), ("d_l20", ctx.c("d_l20"))] if hb == 0 else []))
                            ctx.op("vector", lambda e, hb=hb: e.tensor_tensor(
                                A(ln1q, hb * 512, [[DK * 512, 128], [1, 512]]),
                                lnt2[:, :], pR[:, 512:1024], ALU.subtract),
                                waits=([("tm", m_bc4)] if hb == 0 else []))
                        m_l2 = ctx.c("vm")
                        mk["pRfree"] = m_l2
                        mk["normv"] = m_l2
                        mk["xbq_v"][sb2] = m_l2
                        mk["store_pend"] = (m_l2, slot)

                    def emit_store(ctx):
                        m_l2s, pslot = mk["store_pend"]
                        mk["store_pend"] = None
                        ctx.op("sync", lambda e, pslot=pslot: e.dma_start(
                            out=A(scr_ln2.tensor, pslot * 512,
                                  [[OWN, 128], [128 * OWN, DK], [1, 512]]),
                            in_=A(ln1q, 0, [[DK * 512, 128], [512, DK], [1, 512]])),
                            waits=[("vm", m_l2s)], inc="d_l20", n=16)

                    mk["store_fn"] = emit_store
                    if slot + 1 < NS:
                        mk["norm_fn"] = emit_norm
                    else:
                        emit_norm(ctx)
                        emit_store(ctx)
                    mk["q_v"] = ctx.c("vm")
                    mk["xbq_t"][sb2] = m_stats2

                ctx.wait("sync", [("d_sp0", ctx.c("d_sp0")), ("d_sp1", ctx.c("d_sp1")),
                                  ("d_l20", ctx.c("d_l20"))])
                if ctx.c("d_dump"):
                    ctx.wait("sync", [("d_dump", ctx.c("d_dump"))])

            run_phase(nc, ctx, ph_slots)

        akv_es.close()

        with ExitStack() as f1_es:
            def fsb(name, shape, dt=F32):
                return f1_es.enter_context(nc.sbuf_tensor(name, shape, dt))

            def fps(name, shape):
                return f1_es.enter_context(nc.psum_tensor(name, shape, F32))

            ln2T = fsb("ln2T", [128, DK * OWN], BF16)
            wgb0 = fsb("wgb0", [128, DK * 128], BF16)
            wgb1 = fsb("wgb1", [128, DK * 128], BF16)
            wub0 = fsb("wub0", [128, DK * 128], BF16)
            wub1 = fsb("wub1", [128, DK * 128], BF16)
            silo = fsb("silo", [128, NGRP * 512])
            gst0 = fsb("gst0", [128, NGRP * 512], BF16)
            gst1 = fsb("gst1", [128, NGRP * 512], BF16)
            wds0 = fsb("wds0", [128, D], BF16)
            wds1 = fsb("wds1", [128, D], BF16)
            pG = fps("pG", [128, NGRP * 512])
            pU = fps("pU", [128, NGRP * 512])
            wgb = [wgb0, wgb1]
            wub = [wub0, wub1]
            gst = [gst0, gst1]
            wds = [wds0, wds1]

            def ph_f1(ctx):
                ctx.barrier()
                ctx.op("sync", lambda e: e.dma_start(
                    out=A(ln2T, 0, [[DK * OWN, 128], [OWN, DK], [1, OWN]]),
                    in_=A(scr_ln2.tensor, 0, [[OWN, 128], [128 * OWN, DK], [1, OWN]])),
                    inc="d_ln2ld", n=16)
                m_ln2 = ctx.c("d_ln2ld")
                mk = {"w_t": [0, 0], "silu_s": 0, "gated_v": 0, "g_v": [0, 0],
                      "wd_v": [0, 0]}
                for hc in range(HC):
                    buf = hc % 2
                    ctx.op("gp", lambda e, hc=hc, buf=buf: e.dma_start(
                        out=A(wgb[buf], 0, [[DK * 128, 128], [128, DK], [1, 128]]),
                        in_=A(dram["wgT"].tensor, hc * 128,
                              [[DH, 128], [128 * DH, DK], [1, 128]])),
                        waits=[("tm", mk["w_t"][buf])], inc=f"d_fw{buf}", n=16)
                    ctx.op("gp", lambda e, hc=hc, buf=buf: e.dma_start(
                        out=A(wub[buf], 0, [[DK * 128, 128], [128, DK], [1, 128]]),
                        in_=A(dram["wuT"].tensor, hc * 128,
                              [[DH, 128], [128 * DH, DK], [1, 128]])),
                        inc=f"d_fw{buf}", n=16)
                    m_w = ctx.c(f"d_fw{buf}")
                    ctx.op("gp", lambda e, hc=hc, buf=buf: e.dma_start(
                        out=wds[buf][:, :],
                        in_=A(dram["wdT"].tensor, hc * 128 * D, [[D, 128], [1, D]])),
                        waits=[(f"d_ws{buf}", mk["wd_v"][buf])], inc=f"d_wp{buf}", n=16)
                    m_wd = ctx.c(f"d_wp{buf}")
                    ctx.op("sync", lambda e, hc=hc, buf=buf: e.dma_start(
                        out=A(scr_wd.tensor, hc * 128 * D, [[D, 128], [1, D]]),
                        in_=wds[buf][:, :]),
                        waits=[(f"d_wp{buf}", m_wd)], inc=f"d_ws{buf}", n=16)
                    mk["wd_v"][buf] = ctx.c(f"d_ws{buf}")
                    for dk in range(DK):
                        w = []
                        if dk == 0:
                            w = [(f"d_fw{buf}", m_w), ("d_ln2ld", m_ln2),
                                 ("sm", mk["silu_s"])]
                        for gi in range(NGRP):
                            ctx.op("tensor", lambda e, dk=dk, gi=gi, buf=buf: e.matmul(
                                pG[:, gi * 512:(gi + 1) * 512],
                                wgb[buf][:, dk * 128:(dk + 1) * 128],
                                ln2T[:, dk * OWN + gi * 512: dk * OWN + (gi + 1) * 512],
                                start=(dk == 0), stop=(dk == DK - 1)),
                                waits=(w if gi == 0 else []),
                                inc=("tm" if (dk == DK - 1 and gi == NGRP - 1) else None))
                    m_g = ctx.c("tm")
                    for dk in range(DK):
                        w = []
                        if dk == 0:
                            w = [(f"d_fw{buf}", m_w), ("vm", mk["gated_v"])]
                        for gi in range(NGRP):
                            ctx.op("tensor", lambda e, dk=dk, gi=gi, buf=buf: e.matmul(
                                pU[:, gi * 512:(gi + 1) * 512],
                                wub[buf][:, dk * 128:(dk + 1) * 128],
                                ln2T[:, dk * OWN + gi * 512: dk * OWN + (gi + 1) * 512],
                                start=(dk == 0), stop=(dk == DK - 1)),
                                waits=(w if gi == 0 else []),
                                inc=("tm" if (dk == DK - 1 and gi == NGRP - 1) else None))
                    mk["w_t"][buf] = ctx.c("tm")
                    m_u = ctx.c("tm")
                    ctx.op("scalar", lambda e: e.activation(silo[:, :], pG[:, :], AF.Silu),
                           waits=[("tm", m_g), ("vm", mk["gated_v"])])
                    mk["silu_s"] = ctx.c("sm")
                    ctx.op("vector", lambda e, buf=buf: e.tensor_mul(gst[buf][:, :], silo[:, :], pU[:, :]),
                           waits=[("tm", m_u), ("sm", mk["silu_s"]),
                                  (f"d_gs{buf}", mk["g_v"][buf])])
                    mk["gated_v"] = ctx.c("vm")
                    m_gv = ctx.c("vm")
                    ctx.op("sync", lambda e, hc=hc, buf=buf: e.dma_start(
                        out=A(scr_gated.tensor, hc * 128 * OWN, [[OWN, 128], [1, OWN]]),
                        in_=gst[buf][:, :]),
                        waits=[("vm", m_gv)], inc=f"d_gs{buf}", n=16)
                    mk["g_v"][buf] = ctx.c(f"d_gs{buf}")
                ctx.wait("sync", [("d_gs0", ctx.c("d_gs0")), ("d_gs1", ctx.c("d_gs1")),
                                  ("d_ws0", ctx.c("d_ws0")), ("d_ws1", ctx.c("d_ws1"))])

            run_phase(nc, ctx, ph_f1)

        with ExitStack() as f2_es:
            def gsb(name, shape, dt=F32):
                return f2_es.enter_context(nc.sbuf_tensor(name, shape, dt))

            gld0 = gsb("gld0", [128, HGS * 512], BF16)
            gld1 = gsb("gld1", [128, HGS * 512], BF16)
            gld2 = gsb("gld2", [128, HGS * 512], BF16)
            wdl0 = gsb("wdl0", [128, HGS * DSETW * 128], BF16)
            wdl1 = gsb("wdl1", [128, HGS * DSETW * 128], BF16)
            wdl2 = gsb("wdl2", [128, HGS * DSETW * 128], BF16)
            o1t0 = gsb("o1t0", [128, DSETW * 512])
            o1t1 = gsb("o1t1", [128, DSETW * 512])
            ost0 = gsb("ost0", [128, 512])
            ost1 = gsb("ost1", [128, 512])
            pDa = f2_es.enter_context(nc.psum_tensor("pDa", [128, DSETW * 512], F32))
            pDb = (f2_es.enter_context(nc.psum_tensor("pDb", [128, DSETW * 512], F32))
                   if F2ROT else pDa)
            gld = [gld0, gld1, gld2]
            wdl = [wdl0, wdl1, wdl2]
            o1t = [o1t0, o1t1]
            ost = [ost0, ost1]
            pDx = [pDa, pDb]
            DSW = DSETW * 128

            def ph_f2(ctx):
                ctx.barrier()
                mk = {"mm_t": [0, 0, 0], "ost_v": [0, 0], "ep_v": [0, 0], "o1_v": [0, 0]}
                units = [(gi, dset) for gi in range(NGRP) for dset in range(NDSET)]
                sched = [(u, hg) for u in range(len(units)) for hg in range(NHG)]

                def o1_load(u):
                    gi, dset = units[u]
                    ub = u % 2
                    ctx.op("gp", lambda e, gi=gi, dset=dset, ub=ub: e.dma_start(
                        out=A(o1t[ub], 0, [[DSETW * 512, 128], [512, DSETW], [1, 512]]),
                        in_=A(scr_out1.tensor, dset * DSW * OWN + gi * 512,
                              [[OWN, 128], [128 * OWN, DSETW], [1, 512]])),
                        waits=[("vm", mk["ep_v"][ub if F2ROT else 0])], inc=f"d_o1{ub}", n=16)
                    return ctx.c(f"d_o1{ub}")

                def f2_load(idx):
                    u, hg = sched[idx]
                    gi, dset = units[u]
                    buf = idx % 3
                    ctx.op("sync", lambda e, gi=gi, hg=hg, buf=buf: e.dma_start(
                        out=A(gld[buf], 0, [[HGS * 512, 128], [512, HGS], [1, 512]]),
                        in_=A(scr_gated.tensor, hg * HGS * 128 * OWN + gi * 512,
                              [[OWN, 128], [128 * OWN, HGS], [1, 512]])),
                        waits=[("tm", mk["mm_t"][buf])], inc=f"d_gl{buf}", n=16)
                    half = HGS // 2
                    ctx.op("sync", lambda e, dset=dset, hg=hg, buf=buf, half=half: e.dma_start(
                        out=A(wdl[buf], 0, [[HGS * DSW, 128], [DSW, half], [1, DSW]]),
                        in_=A(scr_wd.tensor, hg * HGS * 128 * D + dset * DSW,
                              [[D, 128], [128 * D, half], [1, DSW]])),
                        inc=f"d_wl{buf}", n=16)
                    ctx.op("gp", lambda e, dset=dset, hg=hg, buf=buf, half=half: e.dma_start(
                        out=A(wdl[buf], half * DSW, [[HGS * DSW, 128], [DSW, HGS - half], [1, DSW]]),
                        in_=A(scr_wd.tensor, (hg * HGS + half) * 128 * D + dset * DSW,
                              [[D, 128], [128 * D, HGS - half], [1, DSW]])),
                        waits=[("tm", mk["mm_t"][buf])], inc=f"d_wm{buf}", n=16)
                    return ctx.c(f"d_gl{buf}"), ctx.c(f"d_wl{buf}"), ctx.c(f"d_wm{buf}")

                fm = {0: f2_load(0), 1: f2_load(1)}
                mo1 = {0: o1_load(0)}
                for idx, (u, hg) in enumerate(sched):
                    gi, dset = units[u]
                    ub = (u % 2) if F2ROT else 0
                    pD = pDx[ub]
                    buf = idx % 3
                    if hg == 0 and u + 1 < len(units):
                        mo1[u + 1] = o1_load(u + 1)
                    if idx + 2 < len(sched):
                        fm[idx + 2] = f2_load(idx + 2)
                    m_g, m_wdl, m_wm = fm[idx]
                    for hci in range(HGS):
                        w = []
                        if hci == 0:
                            w = [(f"d_gl{buf}", m_g), (f"d_wl{buf}", m_wdl),
                                 (f"d_wm{buf}", m_wm)]
                            if hg == 0:
                                w.append(("vm", mk["ep_v"][ub]))
                        for dki in range(DSETW):
                            ctx.op("tensor", lambda e, hci=hci, dki=dki, buf=buf,
                                   pD=pD: e.matmul(
                                pD[:, dki * 512:(dki + 1) * 512],
                                wdl[buf][:, hci * DSW + dki * 128: hci * DSW + dki * 128 + 128],
                                gld[buf][:, hci * 512:(hci + 1) * 512],
                                start=(hg == 0 and hci == 0),
                                stop=(hg == NHG - 1 and hci == HGS - 1)),
                                waits=(w if dki == 0 else []),
                                inc=("tm" if (hci == HGS - 1 and dki == DSETW - 1) else None))
                    mk["mm_t"][buf] = ctx.c("tm")
                    if hg == NHG - 1:
                        m_mm = ctx.c("tm")
                        ob = u % 2
                        for dki in range(DSETW):
                            obuf = dki % 2
                            ctx.op("vector", lambda e, dki=dki, obuf=obuf, pD=pD, ob=ob: e.tensor_add(
                                ost[obuf][:, :], pD[:, dki * 512:(dki + 1) * 512],
                                o1t[ob][:, dki * 512:(dki + 1) * 512]),
                                waits=[("tm", m_mm), (f"d_o1{ob}", mo1[u]),
                                       (f"d_ou{obuf}", mk["ost_v"][obuf])])
                            m_o = ctx.c("vm")
                            ctx.op("sync", lambda e, dki=dki, dset=dset, gi=gi, obuf=obuf: e.dma_start(
                                out=A(out.tensor, (dset * DSETW + dki) * 128 * OWN + gi * 512,
                                      [[OWN, 128], [1, 512]]),
                                in_=ost[obuf][:, :]),
                                waits=[("vm", m_o)], inc=f"d_ou{obuf}", n=16)
                            mk["ost_v"][obuf] = ctx.c(f"d_ou{obuf}")
                        mk["ep_v"][ub] = ctx.c("vm")
                ctx.wait("sync", [("d_ou0", ctx.c("d_ou0")), ("d_ou1", ctx.c("d_ou1"))])
                ctx.barrier()

            run_phase(nc, ctx, ph_f2)

    return nc



def kernel(L0_attn_x, L0_attn_wq, L0_attn_wk, L0_attn_wv, L0_attn_wo,
           L0_ffn_w_gate, L0_ffn_w_up, L0_ffn_w_down):
    cfg = CFG
    maps = host_inputs(cfg, L0_attn_x, L0_attn_wq, L0_attn_wk, L0_attn_wv,
                       L0_attn_wo, L0_ffn_w_gate, L0_ffn_w_up, L0_ffn_w_down)
    nc = build_program(cfg)
    res = run_bass_kernel_spmd(nc, maps, core_ids=list(range(cfg["NCORE"])))
    outp = np.empty((cfg["B"], cfg["T"], cfg["D"]), np.float32)
    for c in range(cfg["NCORE"]):
        b, ct = c // 2, c % 2
        _, perm_q, _ = core_perms(cfg, ct)
        outp[b, perm_q, :] = np.asarray(res.results[c]["outT"]).T
    return outp


# version 25
# speedup vs baseline: 1.2003x; 1.0403x over previous
import numpy as np
import ml_dtypes
from contextlib import ExitStack

import concourse.bass as bass
import concourse.mybir as mybir
from concourse.bass_utils import run_bass_kernel_spmd

F32 = mybir.dt.float32
BF16 = mybir.dt.bfloat16
AF = mybir.ActivationFunctionType
ALU = mybir.AluOpType
BF = ml_dtypes.bfloat16

MASK = -10000.0
EPS_LN = 1e-5
EPS_RMS = 1e-6
ROPE_BASE = 100000.0


def make_cfg(B=4, NSLOT=4, D=2048, NH=16, NKV=4, DH=8192):
    cfg = dict(B=B, NSLOT=NSLOT, D=D, NH=NH, NKV=NKV, DH=DH)
    cfg["T"] = 1024 * NSLOT
    cfg["OWN"] = 512 * NSLOT
    cfg["KEXT"] = cfg["T"] + 512 * NSLOT
    cfg["DK"] = D // 128
    cfg["HC"] = DH // 128
    cfg["GRP"] = NH // NKV
    cfg["ITERS"] = [8 * j + 8 for j in range(NSLOT)]
    cfg["NIT"] = sum(cfg["ITERS"])
    cfg["NGK"] = cfg["KEXT"] // 512
    cfg["NCORE"] = 2 * B
    return cfg


CFG = make_cfg()
SCALE = float(np.float32(1.0) / np.sqrt(np.float32(128.0)))



def core_perms(cfg, ct):
    T, NS = cfg["T"], cfg["NSLOT"]
    groups = [2 * j + ct for j in range(NS)]
    perm_q = np.concatenate([np.arange(512 * g, 512 * g + 512) for g in groups])
    dup = np.concatenate([np.arange(1024 * j + 512 * ct, 1024 * j + 512 * ct + 512)
                          for j in range(NS)])
    perm_kv = np.concatenate([np.arange(T), dup])
    return groups, perm_q, perm_kv


def host_inputs(cfg, x, wq, wk, wv, wo, wg, wu, wd):
    B, T, NS, NIT = cfg["B"], cfg["T"], cfg["NSLOT"], cfg["NIT"]

    shared = {
        "wq": np.ascontiguousarray(wq, np.float32),
        "wk": np.ascontiguousarray(wk, np.float32),
        "wv": np.ascontiguousarray(wv, np.float32),
        "wo": np.ascontiguousarray(wo, np.float32),
        "wgT": np.ascontiguousarray(np.asarray(wg, np.float32).T),
        "wuT": np.ascontiguousarray(np.asarray(wu, np.float32).T),
        "wdT": np.ascontiguousarray(np.asarray(wd, np.float32).T),
    }
    selbuf = np.zeros((128, 32), BF)
    selbuf[:, 16] = 1.0
    onesrow = np.ones((1, 128), BF)
    tri = np.zeros((128, 128), BF)
    for i in range(128):
        tri[i, :i] = MASK
    shared["selbuf"] = selbuf
    shared["onesrow"] = onesrow
    shared["onesrf"] = np.ones((1, 128), np.float32)
    shared["tri"] = tri
    consts = np.array([EPS_LN, EPS_RMS, 1.0 / 128.0, -0.5, SCALE,
                       float(np.log(np.float32(SCALE))), 1.0, -1.0],
                      np.float32)
    shared["consts"] = np.tile(consts[None, :], (128, 1))

    half = 64
    inv = (1.0 / (np.float32(ROPE_BASE) **
                  (np.arange(half, dtype=np.float32) / np.float32(half)))).astype(np.float32)
    pos = np.arange(T, dtype=np.float32)
    freqs = pos[:, None] * inv[None, :]
    cosT = np.cos(freqs).astype(np.float32)
    sinT = np.sin(freqs).astype(np.float32)

    maps = []
    for c in range(2 * B):
        b, ct = c // 2, c % 2
        groups, perm_q, perm_kv = core_perms(cfg, ct)
        m = dict(shared)
        xb = np.asarray(x[b], np.float32)
        m["xqT"] = np.ascontiguousarray(xb[perm_q].T)
        m["xkvT"] = np.ascontiguousarray(xb[perm_kv].T)
        cq = cosT[perm_q].T
        m["cosq"] = np.ascontiguousarray(np.concatenate([cq, cq], axis=0))
        m["sinq"] = np.ascontiguousarray(sinT[perm_q].T)
        m["cosk"] = np.ascontiguousarray(cosT[perm_kv].T)
        m["sink"] = np.ascontiguousarray(sinT[perm_kv].T)
        kb = np.zeros((128, NIT), BF)
        u = 0
        for j in range(NS):
            iters = cfg["ITERS"][j]
            plains = iters - 4
            for it in range(iters):
                if ct == 0 and plains - 4 <= it < plains:
                    kb[:, u] = MASK
                u += 1
        m["kbias"] = kb
        maps.append(m)
    return maps



class Ctx:

    def __init__(self, nc):
        self.nc = nc
        self.sems = {}
        self.cnt = {}
        self.cur = None
        self.eng = None

    def mksem(self, es, *names):
        for n in names:
            self.sems[n] = es.enter_context(self.nc.semaphore(n))
            self.cnt[n] = 0

    def c(self, name):
        return self.cnt[name]

    def op(self, engine, fn=None, waits=(), inc=None, n=1, guard=True):
        waits = [(s, v) for (s, v) in waits if v > 0]
        if engine in ("vector", "scalar") and fn is not None:
            gsem = "vm" if engine == "vector" else "sm"
            if guard and self.cnt[gsem] > 0:
                waits.append((gsem, self.cnt[gsem]))
            if inc is None:
                inc = gsem
        if self.cur == engine:
            for s, v in waits:
                self.eng.wait_ge(self.sems[s], v)
            if fn is not None:
                instr = fn(self.eng)
                if inc is not None:
                    instr.then_inc(self.sems[inc], n)
        if fn is not None and inc is not None:
            self.cnt[inc] += n

    def wait(self, engine, waits):
        self.op(engine, None, waits=waits)

    def barrier(self):
        marks = [(s, self.cnt[s]) for s in ("tm", "sm", "vm") if self.cnt[s] > 0]
        for e in ("tensor", "scalar", "vector", "sync", "gp"):
            self.op(e, None, waits=marks)


def run_phase(nc, ctx, fn):
    snap = dict(ctx.cnt)
    with nc.Block() as blk:
        def mk(name):
            def body(eng, name=name):
                ctx.cnt = dict(snap)
                ctx.cur, ctx.eng = name, eng
                fn(ctx)
            return body
        blk.tensor(mk("tensor"))
        blk.scalar(mk("scalar"))
        blk.vector(mk("vector"))
        blk.sync(mk("sync"))
        blk.gpsimd(mk("gp"))
    ctx.cnt = dict(snap)
    ctx.cur, ctx.eng = None, None
    fn(ctx)


def A(t, off, dims):
    return bass.AP(t, off, [list(d) for d in dims])



def build_program(cfg, dump=()):
    B, T, D, NH, NKV, DH = (cfg[k] for k in ("B", "T", "D", "NH", "NKV", "DH"))
    NS, OWN, KEXT, DK, HC = (cfg[k] for k in ("NSLOT", "OWN", "KEXT", "DK", "HC"))
    GRP, ITERS, NIT, NGK = cfg["GRP"], cfg["ITERS"], cfg["NIT"], cfg["NGK"]
    NPAIR = GRP // 2
    QD = NH * 128
    KD = NKV * 128
    KCH = KEXT // 128
    TCH = T // 128
    NGRP = OWN // 512
    DSETW = min(8, DK)
    NDSET = DK // DSETW
    F2ROT = DSETW <= 4
    HGS = min(8, HC)
    NHG = HC // HGS

    nc = bass.Bass("TRN2", target_bir_lowering=False)
    dram = {}
    for name, shape, dt in [
        ("xqT", [D, OWN], F32), ("xkvT", [D, KEXT], F32),
        ("wq", [D, QD], F32), ("wk", [D, KD], F32), ("wv", [D, KD], F32),
        ("wo", [D, D], F32),
        ("wgT", [D, DH], F32), ("wuT", [D, DH], F32), ("wdT", [DH, D], F32),
        ("cosq", [128, OWN], F32), ("sinq", [64, OWN], F32),
        ("cosk", [64, KEXT], F32), ("sink", [64, KEXT], F32),
        ("kbias", [128, NIT], BF16), ("tri", [128, 128], BF16),
        ("selbuf", [128, 32], BF16), ("onesrow", [1, 128], BF16),
        ("onesrf", [1, 128], F32),
        ("consts", [128, 8], F32),
    ]:
        dram[name] = nc.dram_tensor(name, shape, dt, kind="ExternalInput").ap()
    out = nc.dram_tensor("outT", [D, OWN], F32, kind="ExternalOutput").ap()
    scr_out1 = nc.dram_tensor("scr_out1", [D, OWN], F32,
                              kind="ExternalOutput" if "out1" in dump else "Internal").ap()
    scr_ln2 = nc.dram_tensor("scr_ln2", [D, OWN], BF16,
                             kind="ExternalOutput" if "ln2" in dump else "Internal").ap()
    scr_gated = nc.dram_tensor("scr_gated", [DH, OWN], BF16,
                               kind="ExternalOutput" if "gated" in dump else "Internal").ap()
    scr_wd = nc.dram_tensor("scr_wd", [DH, D], BF16, kind="Internal").ap()
    scr_wq2 = nc.dram_tensor("scr_wq2", [QD // 128, 128, DK * 128], BF16, kind="Internal").ap()
    scr_wo2 = nc.dram_tensor("scr_wo2", [D // 128, 128, DK * 128], BF16, kind="Internal").ap()
    dump_t = {}
    if "pad" in dump:
        for pi in range(2):
            nc.dram_tensor(f"padscr{pi}", [1024, 1024], F32, kind="Internal")
    if "kv" in dump:
        dump_t["kT"] = nc.dram_tensor("dump_kT", [128, NKV * KEXT], BF16, kind="ExternalOutput").ap()
        dump_t["V"] = nc.dram_tensor("dump_V", [128, KCH * KD], BF16, kind="ExternalOutput").ap()
    if "qy" in dump:
        dump_t["qT"] = nc.dram_tensor("dump_qT", [128, NS * NH * 512], BF16, kind="ExternalOutput").ap()
        dump_t["yT"] = nc.dram_tensor("dump_yT", [128, NS * NH * 512], BF16, kind="ExternalOutput").ap()

    es = ExitStack()
    ctx = Ctx(nc)
    ctx.mksem(es, "tm", "sm", "vm", "gm",
              "d_const", "d_xk0", "d_xk1", "d_wkv", "d_cs", "d_cs0", "d_cs1", "d_xq0", "d_xq1",
              "d_wq0", "d_wq1", "d_xr0", "d_xr1", "d_sp0", "d_sp1", "d_wp2a", "d_wp2b", "d_ws2a", "d_ws2b",
              "d_l20", "d_l21", "d_fw0", "d_fw1", "d_wp0", "d_wp1",
              "d_ws0", "d_ws1", "d_ln2ld", "d_gs0", "d_gs1",
              "d_gl0", "d_gl1", "d_gl2", "d_wl0", "d_wl1", "d_wl2", "d_wm0", "d_wm1", "d_wm2", "d_o1ld", "d_o10", "d_o11",
              "d_ou0", "d_ou1", "d_dump")

    def sb(name, shape, dt=F32):
        return es.enter_context(nc.sbuf_tensor(name, shape, dt))

    with es:
        selb = sb("selb", [128, 32], BF16)
        onesrf = sb("onesrf_s", [1, 128])
        trib = sb("trib", [128, 128], BF16)
        kbb = sb("kbb", [128, NIT], BF16)
        cst = sb("cst", [128, 8])

        def CC(i, p=128):
            return cst[0:p, i:i + 1]

        onescol = selb[:, 16:17]

        akv_es = ExitStack()

        def asb0(name, shape, dt=F32):
            return akv_es.enter_context(nc.sbuf_tensor(name, shape, dt))

        kT = asb0("kT", [128, NKV * KEXT], BF16)
        V = asb0("V", [128, KCH * KD], BF16)
        row_m = asb0("row_m", [1, 512])
        row_t = asb0("row_t", [1, 512])
        row_r = asb0("row_r", [1, 512])
        row_mr = asb0("row_mr", [1, 512])
        rsc = asb0("rsc", [128, NKV * KCH])

        def ln_rows2(ctx, psA_ap, psB_ap, m_stats, eps_idx):
            class _W:
                def __init__(self, ap):
                    self.ap = ap

                def __getitem__(self, k):
                    return self.ap
            return ln_rows(ctx, _W(psA_ap), _W(psB_ap), m_stats, eps_idx)

        def ln_rows(ctx, psA, psB, m_stats, eps_idx):
            ctx.op("vector", lambda e: e.tensor_scalar_mul(row_m[:, :], psA[0:1, :], 1.0 / D),
                   waits=[("tm", m_stats)])
            ctx.op("vector", lambda e: e.tensor_mul(row_t[:, :], row_m[:, :], row_m[:, :]))
            ctx.op("vector", lambda e: e.scalar_tensor_tensor(
                row_t[:, :], psB[0:1, :], 1.0 / D, row_t[:, :], ALU.mult, ALU.subtract))
            m_var = ctx.c("vm")
            ctx.op("scalar", lambda e: e.activation(row_t[:, :], row_t[:, :], AF.Ln,
                                                    bias=CC(eps_idx, 1)),
                   waits=[("vm", m_var)])
            ctx.op("scalar", lambda e: e.activation(row_r[:, :], row_t[:, :], AF.Exp,
                                                    scale=CC(3, 1)))
            m_rstd = ctx.c("sm")
            ctx.op("vector", lambda e: e.tensor_mul(row_mr[:, :], row_m[:, :], row_r[:, :]),
                   waits=[("sm", m_rstd)])
            return ctx.c("vm")

        def bcastF(ctx, dst, row_ap, waits):
            ctx.op("tensor", lambda e: e.matmul(dst, onesrf[0:1, :], row_ap,
                                                start=True, stop=True), waits=waits, inc="tm")
            return ctx.c("tm")


        def ph_const(ctx):
            ctx.op("sync", lambda e: e.dma_start(out=selb[:, :], in_=dram["selbuf"][:, :]), inc="d_const", n=16)
            ctx.op("sync", lambda e: e.dma_start(out=onesrf[:, :], in_=dram["onesrf"][:, :]), inc="d_const", n=16)
            ctx.op("sync", lambda e: e.dma_start(out=trib[:, :], in_=dram["tri"][:, :]), inc="d_const", n=16)
            ctx.op("sync", lambda e: e.dma_start(out=kbb[:, :], in_=dram["kbias"][:, :]), inc="d_const", n=16)
            ctx.op("sync", lambda e: e.dma_start(out=cst[:, :], in_=dram["consts"][:, :]), inc="d_const", n=16)
            ctx.wait("sync", [("d_const", 80)])
            ctx.wait("tensor", [("d_const", 80)])
            ctx.wait("vector", [("d_const", 80)])
            ctx.wait("scalar", [("d_const", 80)])

        run_phase(nc, ctx, ph_const)

        with ExitStack() as kv_es:
            def ksb(name, shape, dt=F32):
                return kv_es.enter_context(nc.sbuf_tensor(name, shape, dt))

            def kps(name, shape):
                return kv_es.enter_context(nc.psum_tensor(name, shape, F32))

            wkb = ksb("wkb", [128, DK * KD], BF16)
            wvb = ksb("wvb", [128, DK * KD], BF16)
            xb0 = ksb("xb0", [128, DK * 512], BF16)
            xb1 = ksb("xb1", [128, DK * 512], BF16)
            xsq = ksb("xsq", [128, 1024], BF16)
            tmA = ksb("tmA", [64, 1024])
            tmB = ksb("tmB", [64, 1024])
            negm = ksb("negm", [1, 512], BF16)
            rtc = ksb("rtc", [128, 8])
            wpre0 = ksb("wpre0", [128, 2048], BF16)
            wpre1 = ksb("wpre1", [128, 2048], BF16)
            wpre = [wpre0, wpre1]
            kcolsb = ksb("kcolsb", [1, KD], BF16)
            vcolsb = ksb("vcolsb", [1, KD], BF16)
            csk0 = ksb("csk0", [64, 512])
            snk0 = ksb("snk0", [64, 512])
            csk1 = ksb("csk1", [64, 512])
            snk1 = ksb("snk1", [64, 512])
            rosq = ksb("rosq", [128, 1024], BF16)
            ps_a = kps("ps_a", [1, 512])
            ps_b = kps("ps_b", [1, 512])
            ps_rt = kps("ps_rt", [128, 8])
            ps_K = kps("ps_K", [128, 1024])
            ps_V = kps("ps_V", [128, max(KD, 512)])
            ps_rc = kps("ps_rc", [128, NKV * KCH])
            xbb = [xb0, xb1]
            csk = [csk0, csk1]
            snk = [snk0, snk1]
            NPR = NKV // 2 if NKV >= 2 else 1
            PRW = 2 if NKV >= 2 else 1

            def ph_kv(ctx):
                ctx.op("gp", lambda e: e.dma_start(
                    out=A(wkb, 0, [[DK * KD, 128], [KD, DK], [1, KD]]),
                    in_=A(dram["wk"].tensor, 0, [[KD, 128], [128 * KD, DK], [1, KD]])),
                    inc="d_wkv", n=16)
                ctx.op("gp", lambda e: e.dma_start(
                    out=A(wvb, 0, [[DK * KD, 128], [KD, DK], [1, KD]]),
                    in_=A(dram["wv"].tensor, 0, [[KD, 128], [128 * KD, DK], [1, KD]])),
                    inc="d_wkv", n=16)

                mk = {"stats": 0, "kvdone": 0, "ropeM": {},
                      "cs_v": [0, 0], "mxkv": [("d_xk0", 0), ("d_xk1", 0)], "mcs": [0, 0], "lastLn": 0,
                      "rows_t": 0, "negm_v": 0, "rt_s": 0, "rt_s2": 0, "rms_t": 0}

                CW = min(512, KD)
                for j in range(KD // CW):
                    for dk in range(DK):
                        ctx.op("tensor", lambda e, j=j, dk=dk: e.matmul(
                            ps_a[0:1, 0:CW], onescol, wkb[:, dk * KD + j * CW: dk * KD + (j + 1) * CW],
                            start=(dk == 0), stop=(dk == DK - 1)),
                            waits=([("d_wkv", 32)] if dk == 0 and j == 0 else []))
                        ctx.op("tensor", lambda e, j=j, dk=dk: e.matmul(
                            ps_b[0:1, 0:CW], onescol, wvb[:, dk * KD + j * CW: dk * KD + (j + 1) * CW],
                            start=(dk == 0), stop=(dk == DK - 1)),
                            waits=([("d_wkv", 32)] if dk == 0 and j == 0 else []),
                            inc=("tm" if dk == DK - 1 else None))
                    m_cs0 = ctx.c("tm")
                    ctx.op("scalar", lambda e, j=j: e.activation(
                        kcolsb[0:1, j * CW:(j + 1) * CW], ps_a[0:1, 0:CW], AF.Copy),
                        waits=[("tm", m_cs0)])
                    ctx.op("scalar", lambda e, j=j: e.activation(
                        vcolsb[0:1, j * CW:(j + 1) * CW], ps_b[0:1, 0:CW], AF.Copy))
                mk["colsum_s"] = ctx.c("sm")
                mk["lastLn"] = ctx.c("sm")

                def emit_prep_dma_only(g):
                    gb = g % 2
                    ctx.op("gp", lambda e, g=g, gb=gb: e.dma_start(
                        out=A(xbb[gb], 0, [[DK * 512, 128], [512, DK], [1, 512]]),
                        in_=A(dram["xkvT"].tensor, g * 512,
                              [[KEXT, 128], [128 * KEXT, DK], [1, 512]])),
                        waits=[("tm", mk["kvdone"])],
                        inc=f"d_xk{gb}", n=16)
                    mk["mxkv"][gb] = (f"d_xk{gb}", ctx.c(f"d_xk{gb}"))
                    cb = g % 2
                    ctx.op("sync", lambda e, g=g, cb=cb: e.dma_start(
                        out=csk[cb][:, :], in_=A(dram["cosk"].tensor, g * 512, [[KEXT, 64], [1, 512]])),
                        waits=[("vm", mk["cs_v"][cb])], inc=f"d_cs{cb}", n=16)
                    ctx.op("sync", lambda e, g=g, cb=cb: e.dma_start(
                        out=snk[cb][:, :], in_=A(dram["sink"].tensor, g * 512, [[KEXT, 64], [1, 512]])),
                        inc=f"d_cs{cb}", n=16)
                    mk["mcs"][cb] = ctx.c(f"d_cs{cb}")

                def emit_prep_compute(g):
                    gb = g % 2
                    xb = xbb[gb]
                    m_xkv = mk["mxkv"][gb]
                    for dk in range(DK):
                        ctx.op("tensor", lambda e, dk=dk, xb=xb: e.matmul(
                            ps_a[0:1, :], onescol, xb[:, dk * 512:(dk + 1) * 512],
                            start=(dk == 0), stop=(dk == DK - 1)),
                            waits=([m_xkv, ("vm", mk["rows_t"]),
                                    ("sm", mk["lastLn"])] if dk == 0 else []),
                            inc=("tm" if dk == DK - 1 else None))
                    m_suma = ctx.c("tm")
                    msq = {}
                    mmb = {-2: mk["stats"], -1: mk["stats"]}
                    for dk in range(DK):
                        ctx.op("scalar", lambda e, dk=dk, xb=xb: e.activation(
                            xsq[:, (dk % 2) * 512:(dk % 2) * 512 + 512],
                            xb[:, dk * 512:(dk + 1) * 512], AF.Square),
                            waits=([m_xkv] if dk == 0 else []) +
                                  [("tm", mmb[dk - 2])])
                        msq[dk] = ctx.c("sm")
                        ctx.op("tensor", lambda e, dk=dk: e.matmul(
                            ps_b[0:1, :], onescol, xsq[:, (dk % 2) * 512:(dk % 2) * 512 + 512],
                            start=(dk == 0), stop=(dk == DK - 1)),
                            waits=[("sm", msq[dk])], inc="tm")
                        mmb[dk] = ctx.c("tm")
                    mk["stats"] = ctx.c("tm")
                    ctx.op("vector", lambda e: e.tensor_scalar_mul(negm[:, :], ps_a[0:1, :], -1.0 / D),
                           waits=[("tm", m_suma), ("tm", mk["rank1_t"] if "rank1_t" in mk else 0)])
                    mk["negm_v"] = ctx.c("vm")
                    ctx.op("vector", lambda e: e.tensor_scalar_mul(row_m[:, :], ps_a[0:1, :], 1.0 / D))
                    ctx.op("vector", lambda e: e.tensor_mul(row_t[:, :], row_m[:, :], row_m[:, :]))
                    ctx.op("vector", lambda e: e.scalar_tensor_tensor(
                        row_t[:, :], ps_b[0:1, :], 1.0 / D, row_t[:, :], ALU.mult, ALU.subtract),
                        waits=[("tm", mk["stats"])])
                    m_var = ctx.c("vm")
                    ctx.op("scalar", lambda e: e.activation(row_t[:, :], row_t[:, :], AF.Ln,
                                                            bias=CC(0, 1)),
                           waits=[("vm", m_var)])
                    ctx.op("scalar", lambda e: e.activation(row_r[:, :], row_t[:, :], AF.Exp,
                                                            scale=CC(3, 1)))
                    mk["lastLn"] = ctx.c("sm")
                    m_rstd = ctx.c("sm")
                    for ts in range(4):
                        ctx.op("tensor", lambda e, ts=ts: e.matmul(
                            ps_rt[:, ts:ts + 1], row_r[0:1, ts * 128:(ts + 1) * 128],
                            cst[0:1, 6:7], start=True, stop=True),
                            waits=([("sm", m_rstd), ("sm", mk["rt_s"])] if ts == 0 else []),
                            inc=("tm" if ts == 3 else None))
                    m_rtt = ctx.c("tm")
                    ctx.op("scalar", lambda e: e.activation(rtc[:, 0:4], ps_rt[:, 0:4], AF.Copy),
                           waits=[("tm", m_rtt)])
                    mk["rt_s2"] = ctx.c("sm")
                    mk["rows_t"] = ctx.c("vm")
                    mk["rt_t"] = ctx.c("tm")

                def emit_kproj(g, p):
                    gb = g % 2
                    xb = xbb[gb]
                    m_xkv = mk["mxkv"][gb]
                    for pi in range(PRW):
                        kc = p * PRW + pi
                        for dk in range(DK):
                            w = []
                            if dk == 0:
                                w = [("d_wkv", 32), m_xkv,
                                     ("vm", mk["ropeM"].get(p - 1, 0))]
                            ctx.op("tensor", lambda e, kc=kc, pi=pi, dk=dk, xb=xb: e.matmul(
                                ps_K[:, pi * 512:(pi + 1) * 512],
                                wkb[:, dk * KD + kc * 128: dk * KD + (kc + 1) * 128],
                                xb[:, dk * 512:(dk + 1) * 512],
                                start=(dk == 0), stop=False), waits=w)
                        ctx.op("tensor", lambda e, kc=kc, pi=pi: e.matmul(
                            ps_K[:, pi * 512:(pi + 1) * 512],
                            kcolsb[0:1, kc * 128:(kc + 1) * 128], negm[0:1, :],
                            start=False, stop=True),
                            waits=[("vm", mk["negm_v"]), ("sm", mk["colsum_s"])],
                            inc=("tm" if pi == PRW - 1 else None))
                    return ctx.c("tm")

                def emit_vproj(g, ts, m_evprev):
                    gb = g % 2
                    xb = xbb[gb]
                    posv = g * 4 + ts
                    for dk in range(DK):
                        w = []
                        if dk == 0:
                            w = [("sm", m_evprev), ("d_wkv", 32)]
                        ctx.op("tensor", lambda e, ts=ts, dk=dk, xb=xb: e.matmul(
                            ps_V[:, 0:KD],
                            xb[:, dk * 512 + ts * 128: dk * 512 + ts * 128 + 128],
                            wvb[:, dk * KD:(dk + 1) * KD],
                            start=(dk == 0), stop=False), waits=w)
                    ctx.op("tensor", lambda e, ts=ts: e.matmul(
                        ps_V[:, 0:KD], negm[0:1, ts * 128:(ts + 1) * 128],
                        vcolsb[0:1, 0:KD],
                        start=False, stop=True),
                        waits=[("vm", mk["negm_v"]), ("sm", mk["colsum_s"]),
                               ("tm", mk["rt_t"])],
                        inc="tm")
                    m_V = ctx.c("tm")
                    ctx.op("scalar", lambda e, posv=posv, ts=ts: e.activation(
                        V[:, posv * KD:(posv + 1) * KD], ps_V[:, 0:KD], AF.Copy,
                        scale=rtc[:, ts:ts + 1]),
                        waits=[("tm", m_V), ("sm", mk["rt_s2"])])
                    mk["rt_s"] = mk["rt_s2"]
                    return ctx.c("sm")

                def rope_pair(g, p, m_K):
                    cb = g % 2
                    kv0 = p * PRW
                    x1 = A(ps_K, 0, [[1024, 64], [512, PRW], [1, 512]])
                    x2 = A(ps_K, 64 * 1024, [[1024, 64], [512, PRW], [1, 512]])
                    cc = A(csk[cb], 0, [[512, 64], [0, PRW], [1, 512]])
                    ss = A(snk[cb], 0, [[512, 64], [0, PRW], [1, 512]])
                    tA = A(tmA, 0, [[1024, 64], [512, PRW], [1, 512]])
                    tB = A(tmB, 0, [[1024, 64], [512, PRW], [1, 512]])
                    ktlo = A(kT, kv0 * KEXT + g * 512, [[KEXT * NKV, 64], [KEXT, PRW], [1, 512]])
                    kthi = A(kT, 64 * KEXT * NKV + kv0 * KEXT + g * 512,
                             [[KEXT * NKV, 64], [KEXT, PRW], [1, 512]])
                    ctx.op("vector", lambda e: e.tensor_tensor(tA, x1, cc, ALU.mult),
                           waits=[("tm", m_K), (f"d_cs{cb}", mk["mcs"][cb])])
                    ctx.op("vector", lambda e: e.tensor_tensor(tB, x2, ss, ALU.mult))
                    ctx.op("vector", lambda e: e.tensor_tensor(ktlo, tA, tB, ALU.add))
                    ctx.op("vector", lambda e: e.tensor_tensor(tA, x2, cc, ALU.mult))
                    ctx.op("vector", lambda e: e.tensor_tensor(tB, x1, ss, ALU.mult))
                    ctx.op("vector", lambda e: e.tensor_tensor(kthi, tA, tB, ALU.subtract))
                    mk["ropeM"][p] = ctx.c("vm")
                    mk["cs_v"][cb] = ctx.c("vm")
                    m_kt = ctx.c("vm")
                    ctx.op("scalar", lambda e, kv0=kv0, g=g: e.activation(
                        A(rosq, 0, [[1024, 128], [512, PRW], [1, 512]]),
                        A(kT, kv0 * KEXT + g * 512, [[KEXT * NKV, 128], [KEXT, PRW], [1, 512]]),
                        AF.Square),
                        waits=[("vm", m_kt), ("tm", mk["rms_t"])])
                    return ctx.c("sm")

                def rms_cols(g, p, m_rosq):
                    kv0 = p * PRW
                    for pi in range(PRW):
                        kv = kv0 + pi
                        for ts in range(4):
                            col = kv * KCH + g * 4 + ts
                            ctx.op("tensor", lambda e, pi=pi, ts=ts, col=col: e.matmul(
                                ps_rc[:, col:col + 1],
                                rosq[:, pi * 512 + ts * 128: pi * 512 + ts * 128 + 128],
                                onescol, start=True, stop=True),
                                waits=([("sm", m_rosq)] if (pi == 0 and ts == 0) else []),
                                inc=("tm" if (pi == PRW - 1 and ts == 3) else None))
                    mk["rms_t"] = ctx.c("tm")
                    return ctx.c("tm")

                def compute(g):
                    mk["ropeM"] = {}
                    mK = {}
                    rq = {}
                    mK[0] = emit_kproj(g, 0)
                    m_ev = ctx.c("sm")
                    rq[0] = rope_pair(g, 0, mK[0])
                    vts = 0
                    for p in range(NPR):
                        for _ in range(4 // NPR):
                            m_ev = emit_vproj(g, vts, m_ev)
                            vts += 1
                        rms_cols(g, p, rq[p])
                        if p + 1 < NPR:
                            mK[p + 1] = emit_kproj(g, p + 1)
                            rq[p + 1] = rope_pair(g, p + 1, mK[p + 1])
                    while vts < 4:
                        m_ev = emit_vproj(g, vts, m_ev)
                        vts += 1
                    mk["rank1_t"] = ctx.c("tm")
                    mk["kvdone"] = ctx.c("tm")

                pre_jobs = ([("wq", scr_wq2, i) for i in range(DK)] +
                            [("wo", scr_wo2, i) for i in range(DK)])
                mk["pre_i"] = 0

                def emit_precast(n):
                    for _ in range(n):
                        i = mk["pre_i"]
                        if i >= len(pre_jobs):
                            return
                        mk["pre_i"] += 1
                        name, scr, dk = pre_jobs[i]
                        pb = i % 2
                        ps2 = "ab"[pb]
                        W = QD if name == "wq" else D
                        ctx.op("gp", lambda e, name=name, dk=dk, pb=pb, W=W: e.dma_start(
                            out=wpre[pb][:, 0:W],
                            in_=A(dram[name].tensor, dk * 128 * W, [[W, 128], [1, W]])),
                            waits=[(f"d_ws2{ps2}", ctx.c(f"d_ws2{ps2}"))],
                            inc=f"d_wp2{ps2}", n=16)
                        m_l = ctx.c(f"d_wp2{ps2}")
                        ctx.op("sync", lambda e, scr=scr, dk=dk, pb=pb, W=W: e.dma_start(
                            out=A(scr.tensor, dk * 128, [[D, 128], [128 * D, W // 128], [1, 128]]),
                            in_=A(wpre[pb], 0, [[2048, 128], [128, W // 128], [1, 128]])),
                            waits=[(f"d_wp2{ps2}", m_l)], inc=f"d_ws2{ps2}", n=16)

                emit_prep_dma_only(0)
                emit_prep_compute(0)
                for g in range(NGK):
                    if g + 1 < NGK:
                        emit_prep_dma_only(g + 1)
                    emit_precast(3)
                    compute(g)
                    if g + 1 < NGK:
                        emit_prep_compute(g + 1)
                emit_precast(len(pre_jobs))
                ctx.wait("sync", [("d_ws2a", ctx.c("d_ws2a")), ("d_ws2b", ctx.c("d_ws2b"))])

                m_cols = ctx.c("tm")
                ctx.op("scalar", lambda e: e.activation(
                    A(rsc, 0, [[NKV * KCH, 128], [1, NKV * KCH]]),
                    ps_rc[:, :], AF.Ln, bias=CC(1), scale=CC(2)),
                    waits=[("tm", m_cols)])
                ctx.op("scalar", lambda e: e.activation(
                    A(rsc, 0, [[NKV * KCH, 128], [1, NKV * KCH]]),
                    A(rsc, 0, [[NKV * KCH, 128], [1, NKV * KCH]]),
                    AF.Exp, scale=CC(3), bias=CC(5)))
                mk["lastLn"] = ctx.c("sm")

                if "kv" in dump:
                    mv = ctx.c("vm")
                    ctx.op("sync", lambda e: e.dma_start(out=dump_t["kT"][:, :], in_=kT[:, :]),
                           waits=[("vm", mv)], inc="d_dump", n=16)
                    ctx.op("sync", lambda e: e.dma_start(out=dump_t["V"][:, :], in_=V[:, :]),
                           inc="d_dump", n=16)
                    ctx.wait("sync", [("d_dump", ctx.c("d_dump"))])

            run_phase(nc, ctx, ph_kv)

        with ExitStack() as at_es:
            def asb(name, shape, dt=F32):
                return at_es.enter_context(nc.sbuf_tensor(name, shape, dt))

            def aps(name, shape):
                return at_es.enter_context(nc.psum_tensor(name, shape, F32))

            qT = asb("qT", [128, NH * 512], BF16)
            xr4v = qT.bitcast(F32)
            lyT = asb("lyT", [128, max(NH, DK) * 512], BF16)
            ln1q = lyT
            yT = lyT
            xbq0 = asb("xbq0", [128, DK * 512], BF16)
            xbq1 = asb("xbq1", [128, DK * 512], BF16)
            xbqs = [xbq0, xbq1]
            xsq2 = asb("xsq2", [128, 512], BF16)
            probs0 = asb("probs0", [128, 1024], BF16)
            probs1 = asb("probs1", [128, 1024], BF16)
            reca = row_t
            recb = row_r
            wqb0 = asb("wqb0", [128, DK * 128], BF16)
            wqb1 = asb("wqb1", [128, DK * 128], BF16)
            csq = asb("csq", [128, 512])
            snq = asb("snq", [64, 512])
            roq0 = asb("roq0", [128, 512])
            roq1 = asb("roq1", [128, 512])
            tmpq = asb("tmpq", [128, 512])
            rowr1 = asb("rowr1", [1, 512])
            lnt2 = asb("lnt2", [128, 512])
            negmq = asb("negmq", [1, 512], BF16)
            qcolsb = asb("qcolsb", [1, QD], BF16)
            o1f0 = asb("o1f0", [128, 512])
            o1f1 = asb("o1f1", [128, 512])
            roqs = [roq0, roq1]
            rowrs = [row_r, rowr1]
            pS0 = aps("pS0", [128, 1024])
            pS1 = aps("pS1", [128, 1024])
            pY = aps("pY", [128, 1024])
            pR = aps("pR", [128, 1024])

            probs = [probs0, probs1]
            pS = [pS0, pS1]
            wqb = [wqb0, wqb1]
            o1f = [o1f0, o1f1]

            def ph_slots(ctx):
                ctx.barrier()
                mk = {"q_v": 0, "wq_t": [0, 0], "yev": 0, "avp": 0,
                      "xres_v": [0, 0], "o1_t": 0, "pexp": [0, 0], "msc": {},
                      "qcol_s": 0, "qS_s": 0, "qpfree": 0, "pSfree_s": 0, "pSfree_v": 0,
                      "norm_fn": None, "normv": 0, "store_pend": None, "store_fn": None,
                      "xbq_v": [0, 0], "xbq_t": [0, 0],
                      "m_xq": [0, 0], "pRfree": 0}
                def xq_load(slot):
                    sb2 = slot % 2
                    ctx.op("gp", lambda e, slot=slot, sb2=sb2: e.dma_start(
                        out=A(xbqs[sb2], 0, [[DK * 512, 128], [512, DK], [1, 512]]),
                        in_=A(dram["xqT"].tensor, slot * 512,
                              [[OWN, 128], [128 * OWN, DK], [1, 512]])),
                        waits=[("vm", mk["xbq_v"][sb2]), ("tm", mk["xbq_t"][sb2])],
                        inc=f"d_xq{sb2}", n=16)
                    mk["m_xq"][sb2] = ctx.c(f"d_xq{sb2}")

                xq_load(0)
                for slot in range(NS):
                    uslot = sum(ITERS[:slot])
                    sb2 = slot % 2
                    xbq = xbqs[sb2]
                    m_xq0 = (f"d_xq{sb2}", mk["m_xq"][sb2])
                    m_xq = mk["m_xq"][sb2]
                    ctx.op("sync", lambda e, slot=slot: e.dma_start(
                        out=csq[:, :], in_=A(dram["cosq"].tensor, slot * 512, [[OWN, 128], [1, 512]])),
                        waits=[("vm", mk["q_v"])], inc="d_cs", n=16)
                    ctx.op("sync", lambda e, slot=slot: e.dma_start(
                        out=snq[:, :], in_=A(dram["sinq"].tensor, slot * 512, [[OWN, 64], [1, 512]])),
                        inc="d_cs", n=16)
                    m_cs = ctx.c("d_cs")
                    for dk in range(DK):
                        ctx.op("tensor", lambda e, dk=dk: e.matmul(
                            pS1[0:1, 0:512], onescol, xbq[:, dk * 512:(dk + 1) * 512],
                            start=(dk == 0), stop=(dk == DK - 1)),
                            waits=([m_xq0, ("sm", mk["pSfree_s"]),
                                    ("vm", mk["pSfree_v"])] if dk == 0 else []),
                            inc=("tm" if dk == DK - 1 else None))
                    m_stats = ctx.c("tm")
                    ctx.op("vector", lambda e: e.tensor_scalar_mul(negmq[:, :], pS1[0:1, 0:512], -1.0 / D),
                           waits=[("tm", m_stats)])
                    m_negmq = ctx.c("vm")
                    if mk.get("norm_fn"):
                        fn = mk["norm_fn"]
                        mk["norm_fn"] = None
                        fn(ctx)

                    mq = {"proj": {}, "ropeM": {}, "exp": {}, "hilo": {}, "ktm": {}}

                    def q_P(qc):
                        buf = qc % 2
                        ctx.op("sync" if buf == 0 else "gp", lambda e, qc=qc, buf=buf: e.dma_start(
                            out=A(wqb[buf], 0, [[DK * 128, 128], [1, DK * 128]]),
                            in_=A(scr_wq2.tensor, qc * 128 * D, [[D, 128], [1, D]])),
                            waits=[("tm", mk["wq_t"][buf])], inc=f"d_wq{buf}", n=16)
                        m_wq = ctx.c(f"d_wq{buf}")
                        if slot == 0:
                            for dk in range(DK):
                                ctx.op("tensor", lambda e, dk=dk, buf=buf: e.matmul(
                                    pR[0:1, 128:256], onescol,
                                    wqb[buf][:, dk * 128:(dk + 1) * 128],
                                    start=(dk == 0), stop=(dk == DK - 1)),
                                    waits=([(f"d_wq{buf}", m_wq),
                                            ("sm", mk["qcol_s"])] if dk == 0 else []),
                                    inc=("tm" if dk == DK - 1 else None))
                            m_qc = ctx.c("tm")
                            ctx.op("scalar", lambda e, qc=qc: e.activation(
                                qcolsb[0:1, qc * 128:(qc + 1) * 128], pR[0:1, 128:256], AF.Copy),
                                waits=[("tm", m_qc)])
                            mk["qcol_s"] = ctx.c("sm")
                        qp = pS0[:, buf * 512:buf * 512 + 512]
                        for dk in range(DK):
                            w = []
                            if dk == 0:
                                w = [(f"d_wq{buf}", m_wq), m_xq0,
                                     ("vm", mq["ropeM"].get(qc - 2, 0)),
                                     ("sm", max(mq["exp"].get(qc - 2, 0),
                                                mk["pSfree_s"] if qc < 2 else 0)),
                                     ("vm", mk["pSfree_v"] if qc < 2 else 0)]
                            ctx.op("tensor", lambda e, qc=qc, dk=dk, qp=qp, buf=buf: e.matmul(
                                qp, wqb[buf][:, dk * 128:(dk + 1) * 128],
                                xbq[:, dk * 512:(dk + 1) * 512],
                                start=(dk == 0), stop=False), waits=w)
                        ctx.op("tensor", lambda e, qc=qc, qp=qp: e.matmul(
                            qp, qcolsb[0:1, qc * 128:(qc + 1) * 128], negmq[0:1, :],
                            start=False, stop=True),
                            waits=[("vm", m_negmq), ("sm", mk["qcol_s"])], inc="tm")
                        mk["wq_t"][buf] = ctx.c("tm")
                        mq["proj"][qc] = ctx.c("tm")

                    def q_R(qc):
                        qp = pS0[:, (qc % 2) * 512:(qc % 2) * 512 + 512]
                        ro = roqs[qc % 2]
                        x1 = qp[0:64, :]
                        x2 = qp[64:128, :]
                        ctx.op("vector", lambda e, qp=qp, ro=ro: e.tensor_mul(ro[:, :], qp, csq[:, :]),
                               waits=[("tm", mq["proj"][qc]), ("d_cs", m_cs)])
                        ctx.op("vector", lambda e, x2=x2: e.tensor_mul(tmpq[0:64, :], x2, snq[:, :]))
                        ctx.op("vector", lambda e, ro=ro: e.tensor_add(ro[0:64, :], ro[0:64, :], tmpq[0:64, :]))
                        ctx.op("vector", lambda e, x1=x1: e.tensor_mul(tmpq[64:128, :], x1, snq[:, :]))
                        ctx.op("vector", lambda e, ro=ro: e.tensor_sub(ro[64:128, :], ro[64:128, :], tmpq[64:128, :]))
                        mq["ropeM"][qc] = ctx.c("vm")

                    def q_S(qc):
                        ro = roqs[qc % 2]
                        ctx.op("scalar", lambda e, ro=ro: e.activation(xsq2[:, :], ro[:, :], AF.Square),
                               waits=[("vm", mq["ropeM"][qc]), ("tm", ctx.c("tm"))])
                        m_rosq = ctx.c("sm")
                        ctx.op("tensor", lambda e: e.matmul(pS1[0:1, 512:1024], onescol, xsq2[:, :],
                                                            start=True, stop=True),
                               waits=[("sm", m_rosq), ("vm", m_negmq if qc == 0 else 0)], inc="tm")
                        m_rms = ctx.c("tm")
                        ctx.op("scalar", lambda e: e.activation(row_t[:, :], pS1[0:1, 512:1024], AF.Ln,
                                                                bias=CC(1, 1), scale=CC(2, 1)),
                               waits=[("tm", m_rms)])
                        ctx.op("scalar", lambda e, qc=qc: e.activation(rowrs[qc % 2][:, :], row_t[:, :],
                                                                       AF.Exp, scale=CC(3, 1)),
                               waits=[("tm", mq["ktm"].get(qc - 2, 0))])
                        mq["exp"][qc] = ctx.c("sm")

                    def q_F(qc):
                        ro = roqs[qc % 2]
                        m_rqb = bcastF(ctx, pY[:, 0:512], rowrs[qc % 2][0:1, :],
                                       [("sm", mq["exp"][qc]),
                                        ("vm", mk["yev"] if qc == 0 else mq["hilo"].get(qc - 1, 0))])
                        mq["ktm"][qc] = m_rqb
                        ctx.op("vector", lambda e, qc=qc, ro=ro: e.tensor_mul(
                            qT[:, qc * 512:(qc + 1) * 512], ro[:, :], pY[:, 0:512]),
                            waits=[("tm", m_rqb),
                                   ("d_xr0", ctx.c("d_xr0")), ("d_xr1", ctx.c("d_xr1"))])
                        mq["hilo"][qc] = ctx.c("vm")

                    q_P(0)
                    if NH > 1:
                        q_P(1)
                    q_R(0)
                    q_S(0)
                    for qc in range(NH):
                        if qc + 2 < NH:
                            q_P(qc + 2)
                        if qc + 1 < NH:
                            q_R(qc + 1)
                            q_S(qc + 1)
                        q_F(qc)
                    m_qT = ctx.c("vm")
                    mk["qS_s"] = ctx.c("sm")
                    if mk.get("store_pend"):
                        mk["store_fn"](ctx)
                    if slot + 1 < NS:
                        xq_load(slot + 1)

                    iters = ITERS[slot]
                    plains = iters - 4
                    for kv in range(NKV):
                        for pair in range(NPAIR):
                            h0 = kv * GRP + pair * 2
                            m_exp = {-2: mk["pexp"][0], -1: mk["pexp"][1]}
                            m_av_h = {-2: mk["avp"], -1: mk["avp"]}

                            def geo(it):
                                diag = it >= plains
                                r = it - plains if diag else 0
                                return (it % 2, diag, 128 * r,
                                        it if not diag else TCH + slot * 4 + r)

                            def emit_scores(it):
                                g2, diag, noff, pos = geo(it)
                                w_sc = [("sm", m_exp.get(it - 2, 0))]
                                if it == 0:
                                    w_sc.append(("vm", m_qT))
                                    w_sc.append(("sm", mk["qS_s"]))
                                for h in range(2):
                                    ctx.op("tensor", lambda e, pos=pos, h=h,
                                           noff=noff, g2=g2: e.matmul(
                                        pS[g2][:, h * 512 + noff: (h + 1) * 512],
                                        kT[:, kv * KEXT + pos * 128: kv * KEXT + (pos + 1) * 128],
                                        qT[:, (h0 + h) * 512 + noff: (h0 + h + 1) * 512],
                                        start=True, stop=True),
                                        waits=(w_sc if h == 0 else []),
                                        inc=("tm" if h == 1 else None))
                                m_sc = ctx.c("tm")
                                m_mask = 0
                                if diag:
                                    ctx.op("vector", lambda e, g2=g2, noff=noff: e.tensor_tensor(
                                        A(pS[g2], noff, [[1024, 128], [512, 2], [1, 128]]),
                                        A(pS[g2], noff, [[1024, 128], [512, 2], [1, 128]]),
                                        A(trib, 0, [[128, 128], [0, 2], [1, 128]]),
                                        ALU.add),
                                        waits=[("tm", m_sc)])
                                    m_mask = ctx.c("vm")
                                mk["msc"][it] = (m_sc, m_mask)

                            def emit_exp(it):
                                g2, diag, noff, pos = geo(it)
                                u = uslot + it
                                m_sc, m_mask = mk["msc"][it]
                                wexp = [("tm", m_sc), ("tm", m_av_h.get(it - 2, 0)),
                                        ("vm", m_mask)]
                                rcol = kv * KCH + pos
                                ctx.op("scalar", lambda e, g2=g2, noff=noff, u=u,
                                       rcol=rcol: e.activation(
                                    A(probs[g2], noff, [[1024, 128], [512, 2], [1, 512 - noff]]),
                                    A(pS[g2], noff, [[1024, 128], [512, 2], [1, 512 - noff]]),
                                    AF.Exp, bias=kbb[:, u:u + 1], scale=rsc[:, rcol:rcol + 1]),
                                    waits=wexp)
                                m_exp[it] = ctx.c("sm")

                            def emit_sumav(it):
                                g2, diag, noff, pos = geo(it)
                                for h in range(2):
                                    w = [("sm", m_exp[it])] if h == 0 else []
                                    if it == 0 and h == 0:
                                        w.append(("vm", max(mk["yev"], mk["pRfree"])))
                                    ctx.op("tensor", lambda e, h=h, g2=g2, noff=noff,
                                           it=it: e.matmul(
                                        pR[0:1, h * 512 + noff:h * 512 + 512], onescol,
                                        probs[g2][:, h * 512 + noff:(h + 1) * 512],
                                        start=(it == 0), stop=(it == iters - 1)),
                                        waits=w)
                                for h in range(2):
                                    ctx.op("tensor", lambda e, h=h, g2=g2, noff=noff,
                                           pos=pos, it=it: e.matmul(
                                        pY[:, h * 512 + noff:(h + 1) * 512],
                                        V[:, pos * KD + kv * 128: pos * KD + (kv + 1) * 128],
                                        probs[g2][:, h * 512 + noff:(h + 1) * 512],
                                        start=(it == 0), stop=(it == iters - 1)),
                                        inc=("tm" if h == 1 else None))
                                m_av_h[it] = ctx.c("tm")

                            mk["msc"] = {}
                            for it2 in range(0, iters, 2):
                                emit_scores(it2)
                                emit_exp(it2)
                                emit_scores(it2 + 1)
                                if it2 >= 1:
                                    emit_sumav(it2 - 1)
                                emit_exp(it2 + 1)
                                emit_sumav(it2)
                            emit_sumav(iters - 1)
                            m_av = ctx.c("tm")
                            mk["pexp"] = [m_exp[iters - 2], m_exp[iters - 1]]
                            mk["avp"] = m_av
                            ctx.op("scalar", lambda e: e.activation(reca[:, :], pR[0:1, 0:512],
                                                                    AF.Ln),
                                   waits=[("tm", m_av)])
                            ctx.op("scalar", lambda e: e.activation(reca[:, :], reca[:, :],
                                                                    AF.Exp, scale=CC(7, 1)))
                            ctx.op("scalar", lambda e: e.activation(recb[:, :], pR[0:1, 512:1024],
                                                                    AF.Ln))
                            ctx.op("scalar", lambda e: e.activation(recb[:, :], recb[:, :],
                                                                    AF.Exp, scale=CC(7, 1)))
                            m_rec = ctx.c("sm")
                            ctx.op("tensor", lambda e: e.matmul(pR[:, 0:512], onesrf[0:1, :], reca[0:1, :],
                                                                start=True, stop=True),
                                   waits=[("sm", m_rec)])
                            ctx.op("tensor", lambda e: e.matmul(pR[:, 512:1024], onesrf[0:1, :], recb[0:1, :],
                                                                start=True, stop=True), inc="tm")
                            m_rb = ctx.c("tm")
                            ctx.op("vector", lambda e: e.tensor_copy(roq0[:, :], pR[:, 0:512]),
                                   waits=[("tm", m_rb)])
                            ctx.op("vector", lambda e: e.tensor_copy(tmpq[:, :], pR[:, 512:1024]))
                            rbsh = [roq0, tmpq]
                            for h in range(2):
                                ctx.op("vector", lambda e, h=h, h0=h0: e.tensor_mul(
                                    yT[:, (h0 + h) * 512:(h0 + h + 1) * 512],
                                    pY[:, h * 512:(h + 1) * 512], rbsh[h][:, :]),
                                    waits=[("d_l20", ctx.c("d_l20"))] if (kv == 0 and pair == 0 and h == 0) else [])
                            mk["yev"] = ctx.c("vm")

                    m_yT = ctx.c("vm")
                    mk["pSfree_s"] = ctx.c("sm")
                    mk["pSfree_v"] = ctx.c("vm")
                    if "qy" in dump:
                        ctx.op("sync", lambda e, slot=slot: e.dma_start(
                            out=A(dump_t["qT"].tensor, slot * NH * 512,
                                  [[NS * NH * 512, 128], [1, NH * 512]]),
                            in_=qT[:, :]), waits=[("vm", m_yT)], inc="d_dump", n=16)
                        ctx.op("sync", lambda e, slot=slot: e.dma_start(
                            out=A(dump_t["yT"].tensor, slot * NH * 512,
                                  [[NS * NH * 512, 128], [1, NH * 512]]),
                            in_=yT[:, :]), inc="d_dump", n=16)

                    m_add = {}
                    m_o1b = {}
                    m_xr4 = {}
                    for dk in range(DK):
                        buf = dk % 2
                        NXR = min(4, (NH * 256) // 1024)
                        HWID = NXR * 512
                        g4 = dk // NXR
                        if dk % NXR == 0:
                            xb4 = g4 % 2
                            ctx.op("sync", lambda e, g4=g4, xb4=xb4, slot=slot: e.dma_start(
                                out=A(xr4v, xb4 * HWID, [[NH * 256, 128], [512, NXR], [1, 512]]),
                                in_=A(dram["xqT"].tensor, g4 * NXR * 128 * OWN + slot * 512,
                                      [[OWN, 128], [128 * OWN, NXR], [1, 512]])),
                                waits=[("vm", max(mk["xres_v"][xb4], mk["yev"])),
                                       ("tm", mk["avp"] if g4 < 2 else 0),
                                       (f"d_xr{xb4}", ctx.c(f"d_xr{xb4}"))],
                                inc=f"d_xr{xb4}", n=16)
                            m_xr4[g4] = ctx.c(f"d_xr{xb4}")
                        ctx.op("sync" if buf == 0 else "gp", lambda e, dk=dk, buf=buf: e.dma_start(
                            out=A(wqb[buf], 0, [[DK * 128, 128], [1, DK * 128]]),
                            in_=A(scr_wo2.tensor, dk * 128 * D, [[D, 128], [1, D]])),
                            waits=[("tm", mk["wq_t"][buf])], inc=f"d_wq{buf}", n=16)
                        m_wo = ctx.c(f"d_wq{buf}")
                        m_xr = m_xr4[g4]
                        pO = pY[:, buf * 512:buf * 512 + 512]
                        for yf in range(NH):
                            w = []
                            if yf == 0:
                                w = [(f"d_wq{buf}", m_wo),
                                     ("vm", m_yT if dk < 2 else m_add[dk - 2])]
                            ctx.op("tensor", lambda e, dk=dk, yf=yf, pO=pO, buf=buf: e.matmul(
                                pO, wqb[buf][:, yf * 128:(yf + 1) * 128],
                                yT[:, yf * 512:(yf + 1) * 512],
                                start=(yf == 0), stop=(yf == NH - 1)), waits=w,
                                inc=("tm" if yf == NH - 1 else None))
                        mk["wq_t"][buf] = ctx.c("tm")
                        m_o1 = ctx.c("tm")
                        ctx.op("vector", lambda e, buf=buf, pO=pO, dk=dk, g4=g4: e.tensor_add(
                            o1f[buf][:, :], pO,
                            xr4v[:, (g4 % 2) * HWID + (dk % NXR) * 512:(g4 % 2) * HWID + (dk % NXR) * 512 + 512]),
                            waits=[("tm", m_o1), (f"d_xr{g4 % 2}", m_xr),
                                   (f"d_sp{buf}", ctx.c(f"d_sp{buf}")),
                                   ("sm", m_o1b.get(dk - 2, 0))])
                        mk["xres_v"][g4 % 2] = ctx.c("vm")
                        m_add[dk] = ctx.c("vm")
                        ctx.op("sync", lambda e, buf=buf, dk=dk, slot=slot: e.dma_start(
                            out=A(scr_out1.tensor, dk * 128 * OWN + slot * 512,
                                  [[OWN, 128], [1, 512]]),
                            in_=o1f[buf][:, :]),
                            waits=[("vm", m_add[dk])], inc=f"d_sp{buf}", n=16)
                        ctx.op("scalar", lambda e, buf=buf, dk=dk: e.activation(
                            xbq[:, dk * 512:(dk + 1) * 512], o1f[buf][:, :], AF.Copy),
                            waits=[("vm", m_add[dk])])
                        m_o1b[dk] = ctx.c("sm")
                    xsqw = lnt2.bitcast(BF16)
                    nb = 2
                    SQW = nb * 512
                    m_prb2 = 0
                    for hb in range(DK // nb):
                        ctx.op("scalar", lambda e, hb=hb, SQW=SQW: e.activation(
                            xsqw[:, 0:SQW], xbq[:, hb * SQW:(hb + 1) * SQW], AF.Square),
                            waits=[("sm", m_o1b[min(DK - 1, (hb + 1) * nb - 1)]), ("tm", m_prb2)]
                                  + ([("vm", mk["normv"])] if hb == 0 else []))
                        m_sq2 = ctx.c("sm")
                        for j in range(nb):
                            dk = hb * nb + j
                            ctx.op("tensor", lambda e, dk=dk: e.matmul(
                                pR[0:1, 0:512], onescol, xbq[:, dk * 512:(dk + 1) * 512],
                                start=(dk == 0), stop=(dk == DK - 1)),
                                waits=[("sm", m_o1b[dk])])
                            ctx.op("tensor", lambda e, j=j, dk=dk: e.matmul(
                                pR[0:1, 512:1024], onescol, xsqw[:, j * 512:(j + 1) * 512],
                                start=(dk == 0), stop=(dk == DK - 1)),
                                waits=([("sm", m_sq2)] if j == 0 else []),
                                inc=("tm" if j == nb - 1 else None))
                        m_prb2 = ctx.c("tm")
                    m_stats2 = ctx.c("tm")
                    mk["o1_t"] = m_stats2
                    m_rows3 = ln_rows2(ctx, pR[0:1, 0:512], pR[0:1, 512:1024], m_stats2, 0)
                    m_bc3 = bcastF(ctx, pR[:, 0:512], row_r[0:1, :], [("vm", m_rows3)])
                    m_bc4 = bcastF(ctx, pR[:, 512:1024], row_mr[0:1, :], [("vm", m_rows3)])

                    def emit_norm(ctx, xbq=xbq, m_bc3=m_bc3, m_bc4=m_bc4, slot=slot, sb2=sb2):
                        NB4 = min(4, DK)
                        for hb in range(DK // NB4):
                            ctx.op("vector", lambda e, hb=hb, xbq=xbq, NB4=NB4: e.tensor_tensor(
                                A(xr4v, 0, [[NH * 256, 128], [512, NB4], [1, 512]]),
                                A(xbq, hb * NB4 * 512, [[DK * 512, 128], [512, NB4], [1, 512]]),
                                A(pR, 0, [[1024, 128], [0, NB4], [1, 512]]), ALU.mult),
                                waits=([("tm", m_bc3), ("d_l20", ctx.c("d_l20"))] if hb == 0 else []))
                            ctx.op("vector", lambda e, hb=hb, NB4=NB4: e.tensor_tensor(
                                A(ln1q, hb * NB4 * 512, [[DK * 512, 128], [512, NB4], [1, 512]]),
                                A(xr4v, 0, [[NH * 256, 128], [512, NB4], [1, 512]]),
                                A(pR, 512, [[1024, 128], [0, NB4], [1, 512]]), ALU.subtract),
                                waits=([("tm", m_bc4)] if hb == 0 else []))
                        m_l2 = ctx.c("vm")
                        mk["pRfree"] = m_l2
                        mk["normv"] = m_l2
                        mk["xbq_v"][sb2] = m_l2
                        mk["store_pend"] = (m_l2, slot)

                    def emit_store(ctx):
                        m_l2s, pslot = mk["store_pend"]
                        mk["store_pend"] = None
                        ctx.op("sync", lambda e, pslot=pslot: e.dma_start(
                            out=A(scr_ln2.tensor, pslot * 512,
                                  [[OWN, 128], [128 * OWN, DK], [1, 512]]),
                            in_=A(ln1q, 0, [[DK * 512, 128], [512, DK], [1, 512]])),
                            waits=[("vm", m_l2s)], inc="d_l20", n=16)

                    mk["store_fn"] = emit_store
                    if slot + 1 < NS:
                        mk["norm_fn"] = emit_norm
                    else:
                        emit_norm(ctx)
                        emit_store(ctx)
                    mk["q_v"] = ctx.c("vm")
                    mk["xbq_t"][sb2] = m_stats2

                ctx.wait("sync", [("d_sp0", ctx.c("d_sp0")), ("d_sp1", ctx.c("d_sp1")),
                                  ("d_l20", ctx.c("d_l20"))])
                if ctx.c("d_dump"):
                    ctx.wait("sync", [("d_dump", ctx.c("d_dump"))])

            run_phase(nc, ctx, ph_slots)

        akv_es.close()

        with ExitStack() as f1_es:
            def fsb(name, shape, dt=F32):
                return f1_es.enter_context(nc.sbuf_tensor(name, shape, dt))

            def fps(name, shape):
                return f1_es.enter_context(nc.psum_tensor(name, shape, F32))

            ln2T = fsb("ln2T", [128, DK * OWN], BF16)
            wgb0 = fsb("wgb0", [128, DK * 128], BF16)
            wgb1 = fsb("wgb1", [128, DK * 128], BF16)
            wub0 = fsb("wub0", [128, DK * 128], BF16)
            wub1 = fsb("wub1", [128, DK * 128], BF16)
            silo = fsb("silo", [128, NGRP * 512])
            gst0 = fsb("gst0", [128, NGRP * 512], BF16)
            gst1 = fsb("gst1", [128, NGRP * 512], BF16)
            wds0 = fsb("wds0", [128, D], BF16)
            wds1 = fsb("wds1", [128, D], BF16)
            pG = fps("pG", [128, NGRP * 512])
            pU = fps("pU", [128, NGRP * 512])
            wgb = [wgb0, wgb1]
            wub = [wub0, wub1]
            gst = [gst0, gst1]
            wds = [wds0, wds1]

            def ph_f1(ctx):
                ctx.barrier()
                ctx.op("sync", lambda e: e.dma_start(
                    out=A(ln2T, 0, [[DK * OWN, 128], [OWN, DK], [1, OWN]]),
                    in_=A(scr_ln2.tensor, 0, [[OWN, 128], [128 * OWN, DK], [1, OWN]])),
                    inc="d_ln2ld", n=16)
                m_ln2 = ctx.c("d_ln2ld")
                mk = {"w_t": [0, 0], "silu_s": 0, "gated_v": 0, "g_v": [0, 0],
                      "wd_v": [0, 0]}
                for hc in range(HC):
                    buf = hc % 2
                    ctx.op("gp", lambda e, hc=hc, buf=buf: e.dma_start(
                        out=A(wgb[buf], 0, [[DK * 128, 128], [128, DK], [1, 128]]),
                        in_=A(dram["wgT"].tensor, hc * 128,
                              [[DH, 128], [128 * DH, DK], [1, 128]])),
                        waits=[("tm", mk["w_t"][buf])], inc=f"d_fw{buf}", n=16)
                    ctx.op("gp", lambda e, hc=hc, buf=buf: e.dma_start(
                        out=A(wub[buf], 0, [[DK * 128, 128], [128, DK], [1, 128]]),
                        in_=A(dram["wuT"].tensor, hc * 128,
                              [[DH, 128], [128 * DH, DK], [1, 128]])),
                        inc=f"d_fw{buf}", n=16)
                    m_w = ctx.c(f"d_fw{buf}")
                    ctx.op("gp", lambda e, hc=hc, buf=buf: e.dma_start(
                        out=wds[buf][:, :],
                        in_=A(dram["wdT"].tensor, hc * 128 * D, [[D, 128], [1, D]])),
                        waits=[(f"d_ws{buf}", mk["wd_v"][buf])], inc=f"d_wp{buf}", n=16)
                    m_wd = ctx.c(f"d_wp{buf}")
                    ctx.op("sync", lambda e, hc=hc, buf=buf: e.dma_start(
                        out=A(scr_wd.tensor, hc * 128 * D, [[D, 128], [1, D]]),
                        in_=wds[buf][:, :]),
                        waits=[(f"d_wp{buf}", m_wd)], inc=f"d_ws{buf}", n=16)
                    mk["wd_v"][buf] = ctx.c(f"d_ws{buf}")
                    for dk in range(DK):
                        w = []
                        if dk == 0:
                            w = [(f"d_fw{buf}", m_w), ("d_ln2ld", m_ln2),
                                 ("sm", mk["silu_s"])]
                        for gi in range(NGRP):
                            ctx.op("tensor", lambda e, dk=dk, gi=gi, buf=buf: e.matmul(
                                pG[:, gi * 512:(gi + 1) * 512],
                                wgb[buf][:, dk * 128:(dk + 1) * 128],
                                ln2T[:, dk * OWN + gi * 512: dk * OWN + (gi + 1) * 512],
                                start=(dk == 0), stop=(dk == DK - 1)),
                                waits=(w if gi == 0 else []),
                                inc=("tm" if (dk == DK - 1 and gi == NGRP - 1) else None))
                    m_g = ctx.c("tm")
                    for dk in range(DK):
                        w = []
                        if dk == 0:
                            w = [(f"d_fw{buf}", m_w), ("vm", mk["gated_v"])]
                        for gi in range(NGRP):
                            ctx.op("tensor", lambda e, dk=dk, gi=gi, buf=buf: e.matmul(
                                pU[:, gi * 512:(gi + 1) * 512],
                                wub[buf][:, dk * 128:(dk + 1) * 128],
                                ln2T[:, dk * OWN + gi * 512: dk * OWN + (gi + 1) * 512],
                                start=(dk == 0), stop=(dk == DK - 1)),
                                waits=(w if gi == 0 else []),
                                inc=("tm" if (dk == DK - 1 and gi == NGRP - 1) else None))
                    mk["w_t"][buf] = ctx.c("tm")
                    m_u = ctx.c("tm")
                    ctx.op("scalar", lambda e: e.activation(silo[:, :], pG[:, :], AF.Silu),
                           waits=[("tm", m_g), ("vm", mk["gated_v"])])
                    mk["silu_s"] = ctx.c("sm")
                    ctx.op("vector", lambda e, buf=buf: e.tensor_mul(gst[buf][:, :], silo[:, :], pU[:, :]),
                           waits=[("tm", m_u), ("sm", mk["silu_s"]),
                                  (f"d_gs{buf}", mk["g_v"][buf])])
                    mk["gated_v"] = ctx.c("vm")
                    m_gv = ctx.c("vm")
                    ctx.op("sync", lambda e, hc=hc, buf=buf: e.dma_start(
                        out=A(scr_gated.tensor, hc * 128 * OWN, [[OWN, 128], [1, OWN]]),
                        in_=gst[buf][:, :]),
                        waits=[("vm", m_gv)], inc=f"d_gs{buf}", n=16)
                    mk["g_v"][buf] = ctx.c(f"d_gs{buf}")
                ctx.wait("sync", [("d_gs0", ctx.c("d_gs0")), ("d_gs1", ctx.c("d_gs1")),
                                  ("d_ws0", ctx.c("d_ws0")), ("d_ws1", ctx.c("d_ws1"))])

            run_phase(nc, ctx, ph_f1)

        with ExitStack() as f2_es:
            def gsb(name, shape, dt=F32):
                return f2_es.enter_context(nc.sbuf_tensor(name, shape, dt))

            gld0 = gsb("gld0", [128, HGS * 512], BF16)
            gld1 = gsb("gld1", [128, HGS * 512], BF16)
            gld2 = gsb("gld2", [128, HGS * 512], BF16)
            wdl0 = gsb("wdl0", [128, HGS * DSETW * 128], BF16)
            wdl1 = gsb("wdl1", [128, HGS * DSETW * 128], BF16)
            wdl2 = gsb("wdl2", [128, HGS * DSETW * 128], BF16)
            o1t0 = gsb("o1t0", [128, DSETW * 512])
            o1t1 = gsb("o1t1", [128, DSETW * 512])
            ost0 = gsb("ost0", [128, 512])
            ost1 = gsb("ost1", [128, 512])
            pDa = f2_es.enter_context(nc.psum_tensor("pDa", [128, DSETW * 512], F32))
            pDb = (f2_es.enter_context(nc.psum_tensor("pDb", [128, DSETW * 512], F32))
                   if F2ROT else pDa)
            gld = [gld0, gld1, gld2]
            wdl = [wdl0, wdl1, wdl2]
            o1t = [o1t0, o1t1]
            ost = [ost0, ost1]
            pDx = [pDa, pDb]
            DSW = DSETW * 128

            def ph_f2(ctx):
                ctx.barrier()
                mk = {"mm_t": [0, 0, 0], "ost_v": [0, 0], "ep_v": [0, 0], "o1_v": [0, 0]}
                units = [(gi, dset) for gi in range(NGRP) for dset in range(NDSET)]
                sched = [(u, hg) for u in range(len(units)) for hg in range(NHG)]

                def o1_load(u):
                    gi, dset = units[u]
                    ub = u % 2
                    ctx.op("gp", lambda e, gi=gi, dset=dset, ub=ub: e.dma_start(
                        out=A(o1t[ub], 0, [[DSETW * 512, 128], [512, DSETW], [1, 512]]),
                        in_=A(scr_out1.tensor, dset * DSW * OWN + gi * 512,
                              [[OWN, 128], [128 * OWN, DSETW], [1, 512]])),
                        waits=[("vm", mk["ep_v"][ub if F2ROT else 0])], inc=f"d_o1{ub}", n=16)
                    return ctx.c(f"d_o1{ub}")

                def f2_load(idx):
                    u, hg = sched[idx]
                    gi, dset = units[u]
                    buf = idx % 3
                    ctx.op("sync", lambda e, gi=gi, hg=hg, buf=buf: e.dma_start(
                        out=A(gld[buf], 0, [[HGS * 512, 128], [512, HGS], [1, 512]]),
                        in_=A(scr_gated.tensor, hg * HGS * 128 * OWN + gi * 512,
                              [[OWN, 128], [128 * OWN, HGS], [1, 512]])),
                        waits=[("tm", mk["mm_t"][buf])], inc=f"d_gl{buf}", n=16)
                    half = HGS // 2
                    ctx.op("sync", lambda e, dset=dset, hg=hg, buf=buf, half=half: e.dma_start(
                        out=A(wdl[buf], 0, [[HGS * DSW, 128], [DSW, half], [1, DSW]]),
                        in_=A(scr_wd.tensor, hg * HGS * 128 * D + dset * DSW,
                              [[D, 128], [128 * D, half], [1, DSW]])),
                        inc=f"d_wl{buf}", n=16)
                    ctx.op("gp", lambda e, dset=dset, hg=hg, buf=buf, half=half: e.dma_start(
                        out=A(wdl[buf], half * DSW, [[HGS * DSW, 128], [DSW, HGS - half], [1, DSW]]),
                        in_=A(scr_wd.tensor, (hg * HGS + half) * 128 * D + dset * DSW,
                              [[D, 128], [128 * D, HGS - half], [1, DSW]])),
                        waits=[("tm", mk["mm_t"][buf])], inc=f"d_wm{buf}", n=16)
                    return ctx.c(f"d_gl{buf}"), ctx.c(f"d_wl{buf}"), ctx.c(f"d_wm{buf}")

                fm = {0: f2_load(0), 1: f2_load(1)}
                mo1 = {0: o1_load(0)}
                for idx, (u, hg) in enumerate(sched):
                    gi, dset = units[u]
                    ub = (u % 2) if F2ROT else 0
                    pD = pDx[ub]
                    buf = idx % 3
                    if hg == 0 and u + 1 < len(units):
                        mo1[u + 1] = o1_load(u + 1)
                    if idx + 2 < len(sched):
                        fm[idx + 2] = f2_load(idx + 2)
                    m_g, m_wdl, m_wm = fm[idx]
                    for hci in range(HGS):
                        w = []
                        if hci == 0:
                            w = [(f"d_gl{buf}", m_g), (f"d_wl{buf}", m_wdl),
                                 (f"d_wm{buf}", m_wm)]
                            if hg == 0:
                                w.append(("vm", mk["ep_v"][ub]))
                        for dki in range(DSETW):
                            ctx.op("tensor", lambda e, hci=hci, dki=dki, buf=buf,
                                   pD=pD: e.matmul(
                                pD[:, dki * 512:(dki + 1) * 512],
                                wdl[buf][:, hci * DSW + dki * 128: hci * DSW + dki * 128 + 128],
                                gld[buf][:, hci * 512:(hci + 1) * 512],
                                start=(hg == 0 and hci == 0),
                                stop=(hg == NHG - 1 and hci == HGS - 1)),
                                waits=(w if dki == 0 else []),
                                inc=("tm" if (hci == HGS - 1 and dki == DSETW - 1) else None))
                    mk["mm_t"][buf] = ctx.c("tm")
                    if hg == NHG - 1:
                        m_mm = ctx.c("tm")
                        ob = u % 2
                        for dki in range(DSETW):
                            obuf = dki % 2
                            ctx.op("vector", lambda e, dki=dki, obuf=obuf, pD=pD, ob=ob: e.tensor_add(
                                ost[obuf][:, :], pD[:, dki * 512:(dki + 1) * 512],
                                o1t[ob][:, dki * 512:(dki + 1) * 512]),
                                waits=[("tm", m_mm), (f"d_o1{ob}", mo1[u]),
                                       (f"d_ou{obuf}", mk["ost_v"][obuf])])
                            m_o = ctx.c("vm")
                            ctx.op("sync", lambda e, dki=dki, dset=dset, gi=gi, obuf=obuf: e.dma_start(
                                out=A(out.tensor, (dset * DSETW + dki) * 128 * OWN + gi * 512,
                                      [[OWN, 128], [1, 512]]),
                                in_=ost[obuf][:, :]),
                                waits=[("vm", m_o)], inc=f"d_ou{obuf}", n=16)
                            mk["ost_v"][obuf] = ctx.c(f"d_ou{obuf}")
                        mk["ep_v"][ub] = ctx.c("vm")
                ctx.wait("sync", [("d_ou0", ctx.c("d_ou0")), ("d_ou1", ctx.c("d_ou1"))])
                ctx.barrier()

            run_phase(nc, ctx, ph_f2)

    return nc



def kernel(L0_attn_x, L0_attn_wq, L0_attn_wk, L0_attn_wv, L0_attn_wo,
           L0_ffn_w_gate, L0_ffn_w_up, L0_ffn_w_down):
    cfg = CFG
    maps = host_inputs(cfg, L0_attn_x, L0_attn_wq, L0_attn_wk, L0_attn_wv,
                       L0_attn_wo, L0_ffn_w_gate, L0_ffn_w_up, L0_ffn_w_down)
    nc = build_program(cfg)
    res = run_bass_kernel_spmd(nc, maps, core_ids=list(range(cfg["NCORE"])))
    outp = np.empty((cfg["B"], cfg["T"], cfg["D"]), np.float32)
    for c in range(cfg["NCORE"]):
        b, ct = c // 2, c % 2
        _, perm_q, _ = core_perms(cfg, ct)
        outp[b, perm_q, :] = np.asarray(res.results[c]["outT"]).T
    return outp
